# version 04
# baseline (speedup 1.0000x reference)
import numpy as np
import ml_dtypes
import concourse.bass as bass
import concourse.mybir as mybir
from concourse.bass import IndirectOffsetOnAxis
from concourse.bass_utils import run_bass_kernel_spmd

F32 = mybir.dt.float32
BF16 = mybir.dt.bfloat16
I32 = mybir.dt.int32
AF = mybir.ActivationFunctionType
ALU = mybir.AluOpType
AX = mybir.AxisListType

T = 4096
NT = 32
C = 1024
NCC = 8
NH = 16
HID = 4096
NJ = 32
E = 8
CAP = 1228
CAPP = 1280
NST = 10
NSLOT = E * CAPP
SCALE = 0.125
NEG = -30000.0
DROP_BIG = float(1 << 20)
LN_EPS = 1e-5
SLOT_BLOCKS = [(0, 512), (512, 512), (1024, 256)]


def _mk_masks():
    m = np.full((3, 128, 384), NEG, np.float32)
    for p in range(128):
        lo, hi = p + 96, p + 159
        m[1, p, lo:hi + 1] = 0.0
        m[0, p, max(lo, 128):hi + 1] = 0.0
        if p <= 31:
            m[0, p, 128] = np.log(33.0 - p)
        m[2, p, lo:min(hi, 255) + 1] = 0.0
        if p >= 97:
            m[2, p, 255] = np.log(p - 95.0)
    return m


def _mk_lseg():
    L = np.zeros((128, 128), np.float32)
    for pp in range(128):
        for p in range(128):
            if pp % 8 == p % 8 and pp // 8 < p // 8:
                L[pp, p] = 1.0
    return L


class Builder:
    def __init__(self, nc):
        self.nc = nc
        self.sems = {}

    def s(self, name):
        if name not in self.sems:
            self.sems[name] = [self.nc.alloc_semaphore(name), 0]
        return self.sems[name]

    def inc(self, inst, name, k=1):
        h = self.s(name)
        inst.then_inc(h[0], k)
        h[1] += k

    def val(self, name):
        return self.s(name)[1]

    def wait(self, eng, name, v=None):
        h = self.s(name)
        eng.wait_ge(h[0], h[1] if v is None else v)

    def dinc(self, eng, name, k=1):
        eng.drain()
        self.inc(eng.nop(), name, k)

    def snap(self, name):
        return (name, self.s(name)[1])

    def wait_snap(self, eng, snap):
        if snap is not None:
            name, v = snap
            eng.wait_ge(self.s(name)[0], v)


class Res:
    def __init__(self, b):
        self.b = b
        self.snaps = []

    def wait_free(self, eng):
        for sn in self.snaps:
            self.b.wait_snap(eng, sn)
        self.snaps = []

    def consumed(self, name):
        self.snaps.append(self.b.snap(name))


def build_nc(phases="ABCDE", dump=False):
    nc = bass.Bass("TRN2", target_bir_lowering=False)
    b = Builder(nc)
    from contextlib import ExitStack

    din = dict(kind="ExternalInput")
    x_d = nc.dram_tensor("x", [T, C], F32, **din).ap()
    wq_d = nc.dram_tensor("wq", [C, C], F32, **din).ap()
    wk_d = nc.dram_tensor("wk", [C, C], F32, **din).ap()
    wv_d = nc.dram_tensor("wv", [C, C], F32, **din).ap()
    wo_d = nc.dram_tensor("wo", [C, C], F32, **din).ap()
    rw_d = nc.dram_tensor("rw", [C, E], F32, **din).ap()
    bo_d = nc.dram_tensor("bo_b", [128, C], F32, **din).ap()
    rb_d = nc.dram_tensor("rb_b", [128, E], F32, **din).ap()
    w1_d = nc.dram_tensor("w1r", [E, NJ, 128, 1024], F32, **din).ap()
    wg_d = nc.dram_tensor("wgr", [E, NJ, 128, 1024], F32, **din).ap()
    w2_d = nc.dram_tensor("w2", [E, HID, C], F32, **din).ap()
    b1_d = nc.dram_tensor("b1c", [E, 128, NJ], F32, **din).ap()
    bg_d = nc.dram_tensor("bgc", [E, 128, NJ], F32, **din).ap()
    b2_d = nc.dram_tensor("b2c", [E, 128, E], F32, **din).ap()
    idb_d = nc.dram_tensor("ident_bf", [128, 128], BF16, **din).ap()
    idf_d = nc.dram_tensor("ident_f32", [128, 128], F32, **din).ap()
    msk_d = nc.dram_tensor("masks", [3, 128, 384], F32, **din).ap()
    io8_d = nc.dram_tensor("iota8_b", [128, E], F32, **din).ap()

    skind = dict(kind="ExternalOutput") if dump else dict(kind="Internal")
    einp_d = nc.dram_tensor("einp_hbm", [NSLOT + 1, C], BF16, **skind).ap()
    eo_d = nc.dram_tensor("eo_hbm", [NSLOT, C], F32, **skind).ap()
    out_d = nc.dram_tensor("out", [T, C], F32, kind="ExternalOutput").ap()
    if dump:
        xn_dump = nc.dram_tensor("xn_dump", [T, C], BF16, kind="ExternalOutput").ap()
        lg_dump = nc.dram_tensor("lg_dump", [128, NT * E], F32, kind="ExternalOutput").ap()
        rt_dump = nc.dram_tensor("rt_dump", [4, 128, 2 * NT], F32,
                                 kind="ExternalOutput").ap()
        afm_dump = nc.dram_tensor("afm_dump", [128, NJ * CAPP], BF16,
                                  kind="ExternalOutput").ap()
        eost_dump = nc.dram_tensor("eost_dump", [128, E * 512], F32,
                                   kind="ExternalOutput").ap()

    x_t = x_d.rearrange("(n p) c -> n p c", p=128)
    out_t = out_d.rearrange("(n p) c -> n p c", p=128)
    einp_t = einp_d[0:NSLOT, :].rearrange("(n p) c -> n p c", p=128)
    eo_t = eo_d.rearrange("(n p) c -> n p c", p=128)

    idb_sb = nc.alloc_sbuf_tensor("idb_sb", [128, 128], BF16)
    idf_sb = nc.alloc_sbuf_tensor("idf_sb", [128, 128], F32)
    i01_sb = nc.alloc_sbuf_tensor("i01_sb", [128, 2 * NT], F32)
    prob_sb = nc.alloc_sbuf_tensor("prob_sb", [128, 2 * NT], F32)
    sidx_sb = nc.alloc_sbuf_tensor("sidx_sb", [128, 2 * NT], I32)
    gidx_sb = nc.alloc_sbuf_tensor("gidx_sb", [128, 2 * NT], I32)

    stA = ExitStack()

    def SA(name, shape, dtype):
        return stA.enter_context(nc.sbuf_tensor(name, shape, dtype))

    xn_all = SA("xn_all", [128, NT * C], BF16)
    zeros_sb = SA("zeros_sb", [128, C], BF16)
    cst_sb = SA("cst_sb", [128, 2], F32)
    rb_sb = SA("rb_sb", [128, E], F32)
    io8_sb = SA("io8_sb", [128, E], F32)
    ltok_all = SA("ltok_all", [128, NT * E], F32)

    stA2 = ExitStack()
    def SA2(name, shape, dtype):
        return stA2.enter_context(nc.sbuf_tensor(name, shape, dtype))

    bo_sb = SA2("bo_sb", [128, C], F32)
    msk_sb = SA2("msk_sb", [128, 3 * 384], F32)
    wq_sb = SA2("wq_sb", [128, NCC * C], BF16)
    wk_sb = SA2("wk_sb", [128, NCC * C], BF16)
    wv_sb = SA2("wv_sb", [128, NCC * C], BF16)
    wo_sb = SA2("wo_sb", [128, NCC * C], BF16)
    rw32_sb = SA2("rw32_sb", [128, NCC * E], F32)
    xs_sb = SA2("xs_sb", [128, 2 * C], F32)
    sq_sb = SA2("sq_sb", [128, 2 * C], F32)
    st_sb = SA2("st_sb", [128, 16], F32)
    xf_sb = SA2("xf_sb", [128, 2 * C], BF16)
    xf32_sb = SA2("xf32_sb", [128, C], F32)
    q_sb = SA2("q_sb", [128, 2 * C], BF16)
    k_sb = SA2("k_sb", [128, 3 * C], BF16)
    v_sb = SA2("v_sb", [128, 3 * C], BF16)
    sc_sb = SA2("sc_sb", [128, 2 * 384], F32)
    exn_sb = SA2("exn_sb", [128, 2 * 384], F32)
    exT_sb = SA2("exT_sb", [128, 2 * 384], BF16)
    sm_sb = SA2("sm_sb", [128, 16], F32)
    ao_sb = SA2("ao_sb", [128, 2 * C], BF16)
    wout_sb = SA2("wout_sb", [128, 2 * C], F32)

    stPA = ExitStack()

    def PA(name, shape, dtype):
        return stPA.enter_context(nc.psum_tensor(name, shape, dtype))

    p_tp = [PA(f"p_tp{i}", [128, 128], F32) for i in range(3)]
    p_mm = [PA(f"p_mm{i}", [128, 512], F32) for i in range(2)]
    p_sc = [PA(f"p_sc{i}", [128, 384], F32) for i in range(2)]
    p_av = PA("p_av", [128, 128], F32)

    r_xs = [Res(b) for _ in range(2)]
    r_sq = [Res(b) for _ in range(2)]
    r_xf = [Res(b) for _ in range(2)]
    r_q = [Res(b) for _ in range(2)]
    r_ao = [Res(b) for _ in range(2)]
    r_wout = [Res(b) for _ in range(2)]
    r_sc = [Res(b) for _ in range(2)]
    r_exn = [Res(b) for _ in range(2)]
    r_exT = [Res(b) for _ in range(2)]
    r_sm = [Res(b) for _ in range(2)]
    g_tp = [Res(b) for _ in range(3)]
    g_mm = [Res(b) for _ in range(2)]
    g_sc = [Res(b) for _ in range(2)]
    g_av = Res(b)

    exn_done = {}
    exT_done = {}
    qk_done = {}
    tp_done = {}
    av_done = {}
    evac_q = []
    wo_evac_q = []

    with nc.Block() as blk:
        def _pre_v(v):
            v.memset(cst_sb[:, 0:1], LN_EPS)
            b.inc(v.memset(cst_sb[:, 1:2], -0.5), "cst_v", 1)
        blk.vector(_pre_v)

        def _pre_sy(s):
            for dst, src in ((bo_sb, bo_d), (rb_sb, rb_d), (io8_sb, io8_d),
                             (idb_sb, idb_d), (idf_sb, idf_d)):
                b.inc(s.dma_start(out=dst[:], in_=src), "c_ld", 16)
            b.inc(s.dma_start(out=msk_sb[:], in_=msk_d.transpose([1, 0, 2])),
                  "c_ld", 16)
        blk.sync(_pre_sy)

        def _pre_gp(g):
            g.memset(zeros_sb[:], 0.0)
            b.inc(g.tensor_copy(out=zeros_sb[:, 0:1], in_=zeros_sb[:, 0:1]), "z_done", 1)
            for dst, src in ((wq_sb, wq_d), (wk_sb, wk_d), (wv_sb, wv_d), (wo_sb, wo_d)):
                for cc in range(NCC):
                    b.inc(g.dma_start(out=dst[:, cc * C:(cc + 1) * C],
                                      in_=src[cc * 128:(cc + 1) * 128, :]), "aw_ld", 16)
        blk.gpsimd(_pre_gp)

        def _pre_rw(s):
            for cc in range(NCC):
                b.inc(s.dma_start(out=rw32_sb[:, cc * E:(cc + 1) * E],
                                  in_=rw_d[cc * 128:(cc + 1) * 128, :]), "c_ld", 16)
        blk.sync(_pre_rw)

        def _pre_zi(s):
            b.wait(s, "z_done")
            for i in range(NSLOT // 128):
                b.inc(s.dma_start(out=einp_t[i], in_=zeros_sb[:]), "zi", 16)
        blk.scalar(_pre_zi)

        def attn_pe_body(t, jq, h):
            def kvt(dk):
                tt = jq - 1 + dk
                return None if (tt < 0 or tt > NT - 1) else tt % 3
            if h - 2 >= 0:
                h2 = h - 2
                hp2 = (h2 % 2) * 64
                b.wait_snap(t, exT_done[(jq, h2)])
                if h2 % 2 == 0:
                    g_av.wait_free(t)
                for dk in range(3):
                    s3 = kvt(dk)
                    vt = (zeros_sb[:, h2 * 64:h2 * 64 + 64] if s3 is None
                          else v_sb[:, s3 * C + h2 * 64: s3 * C + h2 * 64 + 64])
                    b.inc(t.matmul(p_av[hp2:hp2 + 64, :], vt,
                                   exT_sb[:, (h2 % 2) * 384 + dk * 128:
                                          (h2 % 2) * 384 + dk * 128 + 128],
                                   start=(dk == 0), stop=(dk == 2),
                                   tile_position=(0, hp2)), "av_pe", 1)
                av_done[(jq, h2)] = b.snap("av_pe")
                r_exT[h2 % 2].consumed("av_pe")
            if 0 <= h - 1 <= NH - 1:
                h1 = h - 1
                b.wait_snap(t, exn_done[(jq, h1)])
                for dk in range(3):
                    g_tp[dk].wait_free(t)
                    b.inc(t.transpose(p_tp[dk][:],
                                      exn_sb[:, (h1 % 2) * 384 + dk * 128:
                                             (h1 % 2) * 384 + dk * 128 + 128],
                                      idf_sb[:]), "atp_pe", 1)
                tp_done[(jq, h1)] = b.snap("atp_pe")
                r_exn[h1 % 2].consumed("atp_pe")
            if h <= NH - 1:
                hp, hc = (h % 2) * 64, (h // 2) * 128
                qs = (jq % 2) * C
                g_sc[h % 2].wait_free(t)
                for dk in range(3):
                    s3 = kvt(dk)
                    kt = (zeros_sb[hp:hp + 64, hc:hc + 128] if s3 is None
                          else k_sb[hp:hp + 64, s3 * C + hc: s3 * C + hc + 128])
                    b.inc(t.matmul(p_sc[h % 2][:, dk * 128:(dk + 1) * 128],
                                   q_sb[hp:hp + 64, qs + hc: qs + hc + 128], kt,
                                   start=(dk == 0), stop=(dk == 2)), "qk_pe", 1)
                qk_done[(jq, h)] = b.snap("qk_pe")

        def attn_vec_body(v, jq, h, msk):
            if h - 2 >= 0 and (h - 2) % 2 == 1:
                h2 = h - 2
                cc = h2 // 2
                b.wait_snap(v, av_done[(jq, h2)])
                if cc == 0:
                    r_ao[jq % 2].wait_free(v)
                v.tensor_copy(
                    out=ao_sb[:, (jq % 2) * C + cc * 128:(jq % 2) * C + cc * 128 + 128],
                    in_=p_av[:])
                b.dinc(v, "ave_v", 1)
                g_av.consumed("ave_v")
            if 0 <= h - 1 <= NH - 1:
                h1 = h - 1
                b.wait_snap(v, tp_done[(jq, h1)])
                r_exT[h1 % 2].wait_free(v)
                for dk in range(3):
                    b.inc(v.tensor_copy(
                        out=exT_sb[:, (h1 % 2) * 384 + dk * 128:
                                   (h1 % 2) * 384 + dk * 128 + 128],
                        in_=p_tp[dk][:]), "exT_v", 1)
                    g_tp[dk].consumed("exT_v")
                b.dinc(v, "exT_d", 1)
                exT_done[(jq, h1)] = b.snap("exT_d")
            if h <= NH - 1:
                hs = (h % 2) * 384
                b.wait_snap(v, qk_done[(jq, h)])
                r_sc[h % 2].wait_free(v)
                b.inc(v.scalar_tensor_tensor(out=sc_sb[:, hs:hs + 384],
                                             in0=p_sc[h % 2][:], scalar=SCALE,
                                             in1=msk, op0=ALU.mult, op1=ALU.add),
                      "msk_v", 1)
                g_sc[h % 2].consumed("msk_v")
                v.drain()
                smx = sm_sb[:, (h % 2) * 8:(h % 2) * 8 + 8]
                r_sm[h % 2].wait_free(v)
                v.reduce_max(smx[:, 0:1], sc_sb[:, hs:hs + 384], axis=AX.X)
                v.drain()
                v.tensor_scalar_mul(smx[:, 1:2], smx[:, 0:1], -1.0)
                b.dinc(v, "mx_v", 1)

        def attn_act_body(s, jq, h):
            if h <= NH - 1:
                hs = (h % 2) * 384
                smx = sm_sb[:, (h % 2) * 8:(h % 2) * 8 + 8]
                b.wait(s, "mx_v", b.val("mx_v"))
                r_exn[h % 2].wait_free(s)
                s.activation(out=exn_sb[:, hs:hs + 384], in_=sc_sb[:, hs:hs + 384],
                             func=AF.Exp, bias=smx[:, 1:2],
                             accum_out=smx[:, 2:3])
                b.dinc(s, "exp_s", 1)
                r_sc[h % 2].consumed("exp_s")
                r_sm[h % 2].consumed("exp_s")

        def attn_vec2_body(v, jq, h):
            if h <= NH - 1:
                hs = (h % 2) * 384
                smx = sm_sb[:, (h % 2) * 8:(h % 2) * 8 + 8]
                b.wait(v, "exp_s", b.val("exp_s"))
                v.reciprocal(smx[:, 3:4], smx[:, 2:3])
                v.drain()
                v.tensor_scalar_mul(exn_sb[:, hs:hs + 384], exn_sb[:, hs:hs + 384],
                                    smx[:, 3:4])
                b.dinc(v, "exn_v", 1)
                exn_done[(jq, h)] = b.snap("exn_v")

        for j in range(NT + 1):
            jq = j - 1
            sl = j % 2
            if j < NT:
                def _ld(s, j=j, sl=sl):
                    r_xs[sl].wait_free(s)
                    b.inc(s.dma_start(out=xs_sb[:, sl * C:(sl + 1) * C], in_=x_t[j]),
                          "x_in", 16)
                blk.sync(_ld)

                def _ln_v(v, j=j, sl=sl):
                    xv = xs_sb[:, sl * C:(sl + 1) * C]
                    st = st_sb[:, sl * 8:sl * 8 + 8]
                    b.wait(v, "x_in", 16 * (j + 1))
                    if j == 0:
                        b.wait(v, "c_ld")
                    v.reduce_sum(st[:, 0:1], xv, axis=AX.X)
                    v.drain()
                    v.tensor_scalar_mul(st[:, 2:3], st[:, 0:1], 1.0 / C)
                    b.dinc(v, "oi_v", 1)
                blk.vector(_ln_v)

                def _ln_s(s, j=j, sl=sl):
                    xv = xs_sb[:, sl * C:(sl + 1) * C]
                    st = st_sb[:, sl * 8:sl * 8 + 8]
                    b.wait(s, "x_in", 16 * (j + 1))
                    r_sq[sl].wait_free(s)
                    s.activation(out=sq_sb[:, sl * C:(sl + 1) * C], in_=xv,
                                 func=AF.Square, accum_out=st[:, 1:2])
                    b.dinc(s, "ln_s1", 1)
                blk.scalar(_ln_s)

                def _ln_v2(v, j=j, sl=sl):
                    st = st_sb[:, sl * 8:sl * 8 + 8]
                    b.wait(v, "ln_s1", b.val("ln_s1"))
                    v.tensor_mul(st[:, 3:4], st[:, 2:3], st[:, 2:3])
                    v.drain()
                    v.scalar_tensor_tensor(out=st[:, 4:5], in0=st[:, 1:2],
                                           scalar=1.0 / C, in1=st[:, 3:4],
                                           op0=ALU.mult, op1=ALU.subtract)
                    b.dinc(v, "ln_v2", 1)
                blk.vector(_ln_v2)

                def _ln_s2(s, j=j, sl=sl):
                    st = st_sb[:, sl * 8:sl * 8 + 8]
                    b.wait(s, "ln_v2", b.val("ln_v2"))
                    if j == 0:
                        b.wait(s, "cst_v")
                    s.activation(out=st[:, 5:6], in_=st[:, 4:5], func=AF.Ln,
                                 bias=cst_sb[:, 0:1])
                    s.drain()
                    s.activation(out=st[:, 6:7], in_=st[:, 5:6], func=AF.Exp,
                                 scale=cst_sb[:, 1:2])
                    b.dinc(s, "ln_s2", 1)
                blk.scalar(_ln_s2)

                def _ln_v3(v, j=j, sl=sl):
                    st = st_sb[:, sl * 8:sl * 8 + 8]
                    b.wait(v, "ln_s2", b.val("ln_s2"))
                    v.scalar_tensor_tensor(out=st[:, 7:8], in0=st[:, 2:3],
                                           scalar=-1.0, in1=st[:, 6:7],
                                           op0=ALU.mult, op1=ALU.mult)
                    b.dinc(v, "ln_v3", 1)
                blk.vector(_ln_v3)

                def _xn(s, j=j, sl=sl):
                    xv = xs_sb[:, sl * C:(sl + 1) * C]
                    st = st_sb[:, sl * 8:sl * 8 + 8]
                    b.wait(s, "ln_v3", b.val("ln_v3"))
                    s.activation(out=sq_sb[:, sl * C:(sl + 1) * C], in_=xv,
                                 func=AF.Identity, scale=st[:, 6:7],
                                 bias=st[:, 7:8])
                    s.drain()
                    s.activation(out=xn_all[:, j * C:(j + 1) * C],
                                 in_=sq_sb[:, sl * C:(sl + 1) * C], func=AF.Identity)
                    b.dinc(s, "xn_s", 1)
                blk.scalar(_xn)
                r_xs[sl].consumed("xn_s")
                r_xs[sl].consumed("oi_v")

                def _oinit(s, j=j, sl=sl):
                    b.wait(s, "oi_v", b.val("oi_v"))
                    b.inc(s.dma_start(out=out_t[j], in_=xs_sb[:, sl * C:(sl + 1) * C]),
                          "o_init", 16)
                blk.sync(_oinit)
                r_xs[sl].consumed("o_init")

                def _tp(t, j=j, sl=sl):
                    b.wait(t, "xn_s", b.val("xn_s"))
                    r_xf[sl].wait_free(t)
                    base = b.val("tpe_v")
                    for cc in range(NCC):
                        if cc < 3:
                            g_tp[cc].wait_free(t)
                        else:
                            b.wait(t, "tpe_v", base + 2 * (cc - 2))
                        b.inc(t.transpose(p_tp[cc % 3][:],
                                          sq_sb[:, sl * C + cc * 128: sl * C + cc * 128 + 128],
                                          idf_sb[:]), "tp_pe", 1)
                blk.tensor(_tp)
                r_sq[sl].consumed("tp_pe")

                def _tpe(v, j=j, sl=sl):
                    base = b.val("tp_pe") - NCC
                    for cc in range(NCC):
                        b.wait(v, "tp_pe", base + cc + 1)
                        b.inc(v.tensor_copy(
                            out=xf_sb[:, sl * C + cc * 128: sl * C + cc * 128 + 128],
                            in_=p_tp[cc % 3][:]), "tpe_v", 1)
                        b.inc(v.tensor_copy(
                            out=xf32_sb[:, cc * 128: cc * 128 + 128],
                            in_=p_tp[cc % 3][:]), "tpe_v", 1)
                        g_tp[cc % 3].consumed("tpe_v")
                    b.dinc(v, "tpe_d", 1)
                blk.vector(_tpe)

                def _qkv(t, j=j, sl=sl):
                    if j == 0:
                        b.wait(t, "aw_ld")
                    b.wait(t, "tpe_d", b.val("tpe_d"))
                    b.wait(t, "woe_v", b.val("woe_v"))
                    xf = xf_sb[:, sl * C:(sl + 1) * C]
                    base = b.val("qkv_v")
                    uses = [("q", ec) for ec in range(NCC)] + \
                           [("k", ec) for ec in range(NCC)] + \
                           [("v", 0), ("v", 1), ("lg", 0)]
                    prev_use = {}
                    for u, (kind, idx) in enumerate(uses):
                        bank = idx % 2 if kind != "lg" else 0
                        if bank in prev_use:
                            b.wait(t, "qkv_v", base + prev_use[bank] + 1)
                        else:
                            g_mm[bank].wait_free(t)
                        prev_use[bank] = u
                        if kind == "q" or kind == "k":
                            wsb = wq_sb if kind == "q" else wk_sb
                            for cc in range(NCC):
                                b.inc(t.matmul(p_mm[bank][:, 0:128],
                                               wsb[:, cc * C + idx * 128: cc * C + idx * 128 + 128],
                                               xf[:, cc * 128:(cc + 1) * 128],
                                               start=(cc == 0), stop=(cc == NCC - 1)),
                                      "mm_pe", 1)
                        elif kind == "v":
                            for cc in range(NCC):
                                b.inc(t.matmul(p_mm[bank][:, 0:512],
                                               xf[:, cc * 128:(cc + 1) * 128],
                                               wv_sb[:, cc * C + idx * 512: cc * C + idx * 512 + 512],
                                               start=(cc == 0), stop=(cc == NCC - 1)),
                                      "mm_pe", 1)
                        else:
                            for cc in range(NCC):
                                b.inc(t.matmul(p_mm[bank][0:128, 0:E],
                                               xf32_sb[:, cc * 128:(cc + 1) * 128],
                                               rw32_sb[:, cc * E:(cc + 1) * E],
                                               start=(cc == 0), stop=(cc == NCC - 1)),
                                      "mm_pe", 1)
                        evac_q.append((j, kind, idx, bank, b.snap("mm_pe")))
                blk.tensor(_qkv)
                r_xf[sl].consumed("mm_pe")

                def _qkve(v, j=j, sl=sl):
                    while evac_q:
                        jj, kind, idx, bank, sn = evac_q.pop(0)
                        b.wait_snap(v, sn)
                        if kind == "q":
                            if idx == 0:
                                r_q[jj % 2].wait_free(v)
                            b.inc(v.tensor_copy(
                                out=q_sb[:, (jj % 2) * C + idx * 128:(jj % 2) * C + idx * 128 + 128],
                                in_=p_mm[bank][:, 0:128]), "qkv_v", 1)
                        elif kind == "k":
                            b.inc(v.tensor_copy(
                                out=k_sb[:, (jj % 3) * C + idx * 128:(jj % 3) * C + idx * 128 + 128],
                                in_=p_mm[bank][:, 0:128]), "qkv_v", 1)
                        elif kind == "v":
                            b.inc(v.tensor_copy(
                                out=v_sb[:, (jj % 3) * C + idx * 512:(jj % 3) * C + idx * 512 + 512],
                                in_=p_mm[bank][:, 0:512]), "qkv_v", 1)
                        else:
                            v.tensor_copy(out=ltok_all[:, jj * E:(jj + 1) * E],
                                          in_=p_mm[bank][0:128, 0:E])
                            b.dinc(v, "lgv", 1)
                        g_mm[bank].consumed("qkv_v" if kind != "lg" else "lgv")
                    b.dinc(v, "qkv_d", 1)
                blk.vector(_qkve)


            if 0 <= jq <= NT - 1:
                mtile = 0 if jq == 0 else (2 if jq == NT - 1 else 1)
                msk = msk_sb[:, mtile * 384:(mtile + 1) * 384]
                for h in range(NH + 2):
                    def _a_pe(t, jq=jq, h=h):
                        if h == 0:
                            b.wait(t, "qkv_d", b.val("qkv_d"))
                            if jq == 0:
                                b.wait(t, "z_done")
                        attn_pe_body(t, jq, h)
                    blk.tensor(_a_pe)

                    def _a_ve(v, jq=jq, h=h, msk=msk):
                        attn_vec_body(v, jq, h, msk)
                    blk.vector(_a_ve)

                    def _a_sc(s, jq=jq, h=h):
                        attn_act_body(s, jq, h)
                    blk.scalar(_a_sc)

                    def _a_ve2(v, jq=jq, h=h):
                        attn_vec2_body(v, jq, h)
                    blk.vector(_a_ve2)
                r_q[jq % 2].consumed("qk_pe")

                def _wo(t, jq=jq):
                    b.wait(t, "ave_v", b.val("ave_v"))
                    for half in range(2):
                        g_mm[half].wait_free(t)
                        for fc in range(NCC):
                            b.inc(t.matmul(p_mm[half][:, 0:512],
                                           ao_sb[:, (jq % 2) * C + fc * 128:(jq % 2) * C + fc * 128 + 128],
                                           wo_sb[:, fc * C + half * 512: fc * C + half * 512 + 512],
                                           start=(fc == 0), stop=(fc == NCC - 1)),
                                  "wo_pe", 1)
                        wo_evac_q.append((jq, half, b.snap("wo_pe")))
                blk.tensor(_wo)
                r_ao[jq % 2].consumed("wo_pe")

                def _woe(v, jq=jq):
                    while wo_evac_q:
                        jj, half, sn = wo_evac_q.pop(0)
                        b.wait_snap(v, sn)
                        if half == 0:
                            r_wout[jj % 2].wait_free(v)
                        b.inc(v.tensor_tensor(
                            out=wout_sb[:, (jj % 2) * C + half * 512:(jj % 2) * C + half * 512 + 512],
                            in0=p_mm[half][:, 0:512],
                            in1=bo_sb[:, half * 512: half * 512 + 512],
                            op=ALU.add), "woe_v", 1)
                        g_mm[half].consumed("woe_v")
                    b.dinc(v, "woe_d", 1)
                blk.vector(_woe)

                def _oacc(g, jq=jq):
                    b.wait(g, "woe_d", b.val("woe_d"))
                    b.wait(g, "o_init", 16 * (jq + 1))
                    b.inc(g.dma_start(out=out_t[jq],
                                      in_=wout_sb[:, (jq % 2) * C:(jq % 2) * C + C],
                                      accum_op=ALU.add), "o_acc", 16)
                blk.gpsimd(_oacc)
                r_wout[jq % 2].consumed("o_acc")

        if dump:
            def _dump_a(s):
                b.wait(s, "xn_s", b.val("xn_s"))
                for j in range(NT):
                    b.inc(s.dma_start(
                        out=xn_dump.rearrange("(n p) c -> n p c", p=128)[j],
                        in_=xn_all[:, j * C:(j + 1) * C]), "dmp", 16)
                b.wait(s, "lgv", b.val("lgv"))
                b.inc(s.dma_start(out=lg_dump, in_=ltok_all[:]), "dmp", 16)
            blk.sync(_dump_a)

    g_bt = [Res(b) for _ in range(2)]
    if "B" in phases:
        stA2.close()
        stB = ExitStack()

        def SB(name, shape, dtype):
            return stB.enter_context(nc.sbuf_tensor(name, shape, dtype))

        m_em = SB("m_em", [E, 2 * T], F32)
        cnt_em = SB("cnt_em", [E, 2 * T], F32)
        smB_sb = SB("smB_sb", [128, 16], F32)
        oh_sb = SB("oh_sb", [128, 2 * E], F32)
        ltok_sb = SB("ltok_sb", [128, E], F32)
        top8_sb = SB("top8_sb", [128, E], F32)
        mi_sb = SB("mi_sb", [128, E], mybir.dt.uint32)
        ctok_sb = SB("ctok_sb", [128, 2 * E], F32)
        tmp8_sb = SB("tmp8_sb", [128, 8], F32)
        zcol_sb = SB("zcol_sb", [E, 1], F32)

        with nc.Block() as blk:
            for j in range(NT):
                def _b_v(v, j=j):
                    b.inc(v.tensor_add(ltok_sb[:], ltok_all[:, j * E:(j + 1) * E],
                                       rb_sb[:]), "bv", 1)
                    v.drain()
                    v.reduce_max(smB_sb[:, 8:9], ltok_sb[:], axis=AX.X)
                    v.drain()
                    v.tensor_scalar(out=oh_sb[:, 0:E], in0=ltok_sb[:],
                                    scalar1=smB_sb[:, 8:9], scalar2=None,
                                    op0=ALU.is_ge)
                    v.drain()
                    v.tensor_mul(tmp8_sb[:], oh_sb[:, 0:E], io8_sb[:])
                    v.drain()
                    v.reduce_sum(i01_sb[:, 2 * j:2 * j + 1], tmp8_sb[:], axis=AX.X)
                    v.scalar_tensor_tensor(out=top8_sb[:], in0=oh_sb[:, 0:E],
                                           scalar=-DROP_BIG, in1=ltok_sb[:],
                                           op0=ALU.mult, op1=ALU.add)
                    v.drain()
                    v.reduce_max(smB_sb[:, 9:10], top8_sb[:], axis=AX.X)
                    v.drain()
                    v.tensor_scalar(out=oh_sb[:, E:2 * E], in0=top8_sb[:],
                                    scalar1=smB_sb[:, 9:10], scalar2=None,
                                    op0=ALU.is_ge)
                    v.drain()
                    v.tensor_mul(tmp8_sb[:], oh_sb[:, E:2 * E], io8_sb[:])
                    v.drain()
                    v.reduce_sum(i01_sb[:, 2 * j + 1:2 * j + 2], tmp8_sb[:], axis=AX.X)
                    v.tensor_sub(smB_sb[:, 0:1], smB_sb[:, 9:10], smB_sb[:, 8:9])
                    b.dinc(v, "bv", 1)
                blk.vector(_b_v)

                def _b_s(s, j=j):
                    b.wait(s, "bv", b.val("bv"))
                    s.activation(out=smB_sb[:, 2:3], in_=smB_sb[:, 0:1], func=AF.Exp)
                    b.dinc(s, "bs", 1)
                blk.scalar(_b_s)

                def _b_v2(v, j=j):
                    b.wait(v, "bs", b.val("bs"))
                    v.tensor_scalar_add(smB_sb[:, 3:4], smB_sb[:, 2:3], 1.0)
                    v.drain()
                    v.reciprocal(prob_sb[:, 2 * j:2 * j + 1], smB_sb[:, 3:4])
                    v.drain()
                    v.tensor_scalar(out=prob_sb[:, 2 * j + 1:2 * j + 2],
                                    in0=prob_sb[:, 2 * j:2 * j + 1],
                                    scalar1=-1.0, scalar2=1.0,
                                    op0=ALU.mult, op1=ALU.add)
                    for kk in range(2):
                        v.tensor_scalar(out=oh_sb[:, kk * E:(kk + 1) * E],
                                        in0=io8_sb[:],
                                        scalar1=i01_sb[:, 2 * j + kk:2 * j + kk + 1],
                                        scalar2=None, op0=ALU.is_equal)
                    b.dinc(v, "bv2", 1)
                blk.vector(_b_v2)

                def _b_tp2(t, j=j):
                    b.wait(t, "bv2", b.val("bv2"))
                    for kk in range(2):
                        g_bt[kk].wait_free(t)
                        t.transpose(p_mm[kk][0:8, 0:128],
                                    oh_sb[:, kk * E:(kk + 1) * E], idf_sb[:])
                    b.dinc(t, "btp2_pe", 2)
                blk.tensor(_b_tp2)

                def _b_v3(v, j=j):
                    b.wait(v, "btp2_pe", b.val("btp2_pe"))
                    for kk in range(2):
                        v.tensor_copy(
                            out=m_em[:, kk * T + j * 128: kk * T + j * 128 + 128],
                            in_=p_mm[kk][0:8, 0:128])
                    b.dinc(v, "bv3", 1)
                    for kk in range(2):
                        g_bt[kk].consumed("bv3")
                blk.vector(_b_v3)

            def _scan(v):
                b.wait(v, "bv3", b.val("bv3"))
                v.memset(zcol_sb[:], 0.0)
                v.drain()
                v.tensor_tensor_scan(out=cnt_em[:], data0=m_em[:],
                                     data1=zcol_sb[:].to_broadcast([E, 2 * T]),
                                     initial=0.0, op0=ALU.add, op1=ALU.add)
                b.dinc(v, "scan_v", 1)
            blk.vector(_scan)

            for j in range(NT):
                def _c_tp(t, j=j):
                    if j == 0:
                        b.wait(t, "scan_v", b.val("scan_v"))
                    for kk in range(2):
                        g_bt[kk].wait_free(t)
                        b.inc(t.transpose(p_mm[kk][0:128, 0:8],
                                          cnt_em[:, kk * T + j * 128: kk * T + j * 128 + 128],
                                          idf_sb[0:8, 0:8]), "ctp_pe", 1)
                blk.tensor(_c_tp)

                def _c_v(v, j=j):
                    b.wait(v, "ctp_pe", b.val("ctp_pe"))
                    for kk in range(2):
                        b.inc(v.tensor_copy(out=ctok_sb[:, kk * E:(kk + 1) * E],
                                            in_=p_mm[kk][0:128, 0:8]), "cv", 1)
                        g_bt[kk].consumed("cv")
                    v.drain()
                    for kk in range(2):
                        ctok = ctok_sb[:, kk * E:(kk + 1) * E]
                        v.tensor_scalar(out=oh_sb[:, kk * E:(kk + 1) * E], in0=io8_sb[:],
                                        scalar1=i01_sb[:, 2 * j + kk:2 * j + kk + 1],
                                        scalar2=None, op0=ALU.is_equal)
                        v.drain()
                        v.tensor_mul(tmp8_sb[:], oh_sb[:, kk * E:(kk + 1) * E], ctok)
                        v.drain()
                        v.reduce_sum(smB_sb[:, 0:1], tmp8_sb[:], axis=AX.X)
                        v.drain()
                        v.tensor_scalar_add(smB_sb[:, 1:2], smB_sb[:, 0:1], -1.0)
                        v.drain()
                        v.scalar_tensor_tensor(out=smB_sb[:, 2:3],
                                               in0=i01_sb[:, 2 * j + kk:2 * j + kk + 1],
                                               scalar=float(CAPP), in1=smB_sb[:, 1:2],
                                               op0=ALU.mult, op1=ALU.add)
                        v.tensor_scalar(out=smB_sb[:, 3:4], in0=smB_sb[:, 1:2],
                                        scalar1=float(CAP), scalar2=None, op0=ALU.is_lt)
                        v.drain()
                        v.tensor_mul(smB_sb[:, 4:5], smB_sb[:, 2:3], smB_sb[:, 3:4])
                        v.tensor_scalar(out=smB_sb[:, 7:8], in0=smB_sb[:, 3:4],
                                        scalar1=-float(NSLOT), scalar2=float(NSLOT),
                                        op0=ALU.mult, op1=ALU.add)
                        v.drain()
                        v.tensor_add(smB_sb[:, 4:5], smB_sb[:, 4:5], smB_sb[:, 7:8])
                        v.drain()
                        v.tensor_copy(out=sidx_sb[:, 2 * j + kk:2 * j + kk + 1],
                                      in_=smB_sb[:, 4:5])
                        v.tensor_scalar_min(smB_sb[:, 5:6], smB_sb[:, 1:2],
                                            float(CAP - 1))
                        v.drain()
                        v.scalar_tensor_tensor(out=smB_sb[:, 6:7],
                                               in0=i01_sb[:, 2 * j + kk:2 * j + kk + 1],
                                               scalar=float(CAPP), in1=smB_sb[:, 5:6],
                                               op0=ALU.mult, op1=ALU.add)
                        v.drain()
                        v.tensor_copy(out=gidx_sb[:, 2 * j + kk:2 * j + kk + 1],
                                      in_=smB_sb[:, 6:7])
                    b.dinc(v, "cv", 1)
                blk.vector(_c_v)

            if dump:
                def _dump_rt(s):
                    b.wait(s, "cv", b.val("cv"))
                    b.inc(s.dma_start(out=rt_dump[0], in_=i01_sb[:]), "dmp", 16)
                    b.inc(s.dma_start(out=rt_dump[1], in_=prob_sb[:]), "dmp", 16)
                    b.inc(s.dma_start(out=rt_dump[2], in_=sidx_sb[:].bitcast(F32)),
                          "dmp", 16)
                    b.inc(s.dma_start(out=rt_dump[3], in_=gidx_sb[:].bitcast(F32)),
                          "dmp", 16)
                blk.sync(_dump_rt)
        stB.close()
    else:
        stA2.close()

    if "C" in phases:
        with nc.Block() as blk:
            def _scat(g):
                b.wait(g, "zi", b.val("zi"))
                for j in range(NT):
                    for kk in range(2):
                        b.inc(g.indirect_dma_start(
                            out=einp_d,
                            out_offset=IndirectOffsetOnAxis(
                                sidx_sb[:, 2 * j + kk:2 * j + kk + 1], 0),
                            in_=xn_all[:, j * C:(j + 1) * C],
                            in_offset=None), "cs", 16)
                g.wait_ge(b.s("cs")[0], b.val("cs"))
            blk.gpsimd(_scat)

    stPA.close()
    stA.close()

    if "D" in phases:
        stD = ExitStack()

        def SD(name, shape, dtype):
            return stD.enter_context(nc.sbuf_tensor(name, shape, dtype))

        a_fm = SD("a_fm", [128, NJ * CAPP], BF16)
        einp_fm = SD("einp_fm", [128, NCC * CAPP], BF16)
        eit_sb = SD("eit_sb", [128, 2 * C], BF16)
        w2_sb = SD("w2_sb", [128, NJ * C], BF16)
        w1_sl = SD("w1_sl", [128, 2 * 1024], BF16)
        wg_sl = SD("wg_sl", [128, 2 * 1024], BF16)
        tg_sb = SD("tg_sb", [128, 512], F32)
        m_sb2 = SD("m_sb2", [128, 512], F32)
        eo_st = SD("eo_st", [128, E * 512], F32)
        eo_sl = SD("eo_sl", [128, C], F32)
        b1_sb = SD("b1_sb", [128, NJ], F32)
        bg_sb = SD("bg_sb", [128, NJ], F32)
        b2_sb = SD("b2_sb", [128, E], F32)

        stPD = ExitStack()

        def PDp(name, shape, dtype):
            return stPD.enter_context(nc.psum_tensor(name, shape, dtype))

        p_h = PDp("p_h", [128, 512], F32)
        p_g = PDp("p_g", [128, 512], F32)
        p_o = [PDp(f"p_o{i}", [128, 512], F32) for i in range(2)]
        p_dt = [PDp(f"p_dt{i}", [128, 128], BF16) for i in range(2)]
        p_et = PDp("p_et", [128, 128], F32)

        g_h = Res(b)
        g_g = Res(b)
        g_o = [Res(b) for _ in range(2)]
        r_eit = [Res(b) for _ in range(2)]
        r_w1 = [Res(b) for _ in range(2)]
        r_wg = [Res(b) for _ in range(2)]
        r_tg = [Res(b) for _ in range(2)]
        r_m = [Res(b) for _ in range(2)]
        r_eosl = Res(b)
        g_w2 = Res(b)
        g_afm = Res(b)
        g_eost = Res(b)
        g_bias = Res(b)

        w1_arr = {}
        w2_arr = {}
        p1_q = []
        tg_q = []
        m_q = []
        p2_q = []

        with nc.Block() as blk:
            for e in range(E):
                def _bias(s, e=e):
                    if e == 0 and "C" in phases:
                        b.wait(s, "cs", b.val("cs"))
                    g_bias.wait_free(s)
                    b.inc(s.dma_start(out=b1_sb[:], in_=b1_d[e]), "bia", 16)
                    b.inc(s.dma_start(out=bg_sb[:], in_=bg_d[e]), "bia", 16)
                    b.inc(s.dma_start(out=b2_sb[:], in_=b2_d[e]), "bia", 16)
                blk.sync(_bias)
                bia_now = b.snap("bia")

                for st in range(NST):
                    def _slab_ld(s, e=e, st=st):
                        r_eit[st % 2].wait_free(s)
                        b.inc(s.dma_start(out=eit_sb[:, (st % 2) * C:(st % 2) * C + C],
                                          in_=einp_t[e * NST + st]), "slab", 16)
                    blk.sync(_slab_ld)

                    def _slab_tp(t, e=e, st=st):
                        b.wait(t, "slab", b.val("slab"))
                        base = b.val("dte_v")
                        for cc in range(NCC):
                            if base + cc - 1 > 0:
                                b.wait(t, "dte_v", base + cc - 1)
                            b.inc(t.transpose(p_dt[cc % 2][:],
                                              eit_sb[:, (st % 2) * C + cc * 128:
                                                     (st % 2) * C + cc * 128 + 128],
                                              idb_sb[:]), "dtp_pe", 1)
                    blk.tensor(_slab_tp)

                    def _slab_ev(v, e=e, st=st):
                        base = b.val("dtp_pe") - NCC
                        for cc in range(NCC):
                            b.wait(v, "dtp_pe", base + cc + 1)
                            b.inc(v.tensor_copy(
                                out=einp_fm[:, cc * CAPP + st * 128: cc * CAPP + st * 128 + 128],
                                in_=p_dt[cc % 2][:]), "dte_v", 1)
                        b.dinc(v, "dte_d", 1)
                    blk.vector(_slab_ev)
                    r_eit[st % 2].consumed("dtp_pe")
                slab_done = b.snap("dte_d")

                def _w_pre(g, e=e):
                    for jj in range(2):
                        r_w1[jj].wait_free(g)
                        b.inc(g.dma_start(out=w1_sl[:, jj * 1024:(jj + 1) * 1024],
                                          in_=w1_d[e, jj]), "w1d", 16)
                        w1_arr[(e, jj, 0)] = b.snap("w1d")
                        r_wg[jj].wait_free(g)
                        b.inc(g.dma_start(out=wg_sl[:, jj * 1024:(jj + 1) * 1024],
                                          in_=wg_d[e, jj]), "w1d", 16)
                        w1_arr[(e, jj, 1)] = b.snap("w1d")
                blk.gpsimd(_w_pre)

                for j in range(NJ):
                    def _p1_pe(t, e=e, j=j):
                        if j == 0:
                            b.wait_snap(t, slab_done)
                            g_afm.wait_free(t)
                        b.wait_snap(t, w1_arr[(e, j, 0)])
                        b.wait_snap(t, w1_arr[(e, j, 1)])
                        base_m = b.val("m_v")
                        base_tg = b.val("tg_s")
                        for bi, (s0, w) in enumerate(SLOT_BLOCKS):
                            if bi == 0:
                                g_h.wait_free(t)
                            else:
                                b.wait(t, "m_v", base_m + bi)
                            for cc in range(NCC):
                                b.inc(t.matmul(
                                    p_h[:, 0:w],
                                    w1_sl[:, (j % 2) * 1024 + cc * 128:(j % 2) * 1024 + cc * 128 + 128],
                                    einp_fm[:, cc * CAPP + s0: cc * CAPP + s0 + w],
                                    start=(cc == 0), stop=(cc == NCC - 1)), "h_pe", 1)
                            h_now = b.snap("h_pe")
                            if bi == 0:
                                g_g.wait_free(t)
                            else:
                                b.wait(t, "tg_s", base_tg + bi)
                            for cc in range(NCC):
                                b.inc(t.matmul(
                                    p_g[:, 0:w],
                                    wg_sl[:, (j % 2) * 1024 + cc * 128:(j % 2) * 1024 + cc * 128 + 128],
                                    einp_fm[:, cc * CAPP + s0: cc * CAPP + s0 + w],
                                    start=(cc == 0), stop=(cc == NCC - 1)), "g_pe", 1)
                            p1_q.append((e, j, bi, s0, w, h_now, b.snap("g_pe")))
                    blk.tensor(_p1_pe)

                    for bi in range(len(SLOT_BLOCKS)):
                        def _p1_sc(v, e=e, j=j, bi=bi):
                            for item in list(p1_q):
                                ee, jj, bb_, s0, w, hsn, gsn = item
                                if (ee, jj, bb_) != (e, j, bi):
                                    continue
                                b.wait_snap(v, gsn)
                                if jj == 0 and bb_ == 0:
                                    b.wait_snap(v, bia_now)
                                r_tg[0].wait_free(v)
                                v.tensor_scalar_add(tg_sb[:, 0:w], p_g[:, 0:w],
                                                    bg_sb[:, jj:jj + 1])
                                b.dinc(v, "tg_s", 1)
                                tg_q.append((ee, jj, bb_, s0, w, hsn, b.snap("tg_s")))
                                p1_q.remove(item)
                        blk.vector(_p1_sc)

                        def _p1_v(v, e=e, j=j, bi=bi):
                            for item in list(tg_q):
                                ee, jj, bb_, s0, w, hsn, tsn = item
                                if (ee, jj, bb_) != (e, j, bi):
                                    continue
                                b.wait_snap(v, hsn)
                                b.wait_snap(v, tsn)
                                r_m[0].wait_free(v)
                                v.scalar_tensor_tensor(
                                    out=m_sb2[:, 0:w],
                                    in0=p_h[:, 0:w], scalar=b1_sb[:, jj:jj + 1],
                                    in1=tg_sb[:, 0:w],
                                    op0=ALU.add, op1=ALU.mult)
                                b.dinc(v, "m_v", 1)
                                g_h.consumed("m_v")
                                g_g.consumed("m_v")
                                r_tg[0].consumed("m_v")
                                m_q.append((ee, jj, bb_, s0, w, b.snap("m_v")))
                                tg_q.remove(item)
                        blk.vector(_p1_v)

                        def _p1_si(s, e=e, j=j, bi=bi):
                            for item in list(m_q):
                                ee, jj, bb_, s0, w, msn = item
                                if (ee, jj, bb_) != (e, j, bi):
                                    continue
                                b.wait_snap(s, msn)
                                s.activation(
                                    out=a_fm[:, jj * CAPP + s0: jj * CAPP + s0 + w],
                                    in_=m_sb2[:, 0:w],
                                    func=AF.Silu)
                                b.dinc(s, "a_s", 1)
                                r_m[0].consumed("a_s")
                                m_q.remove(item)
                        blk.scalar(_p1_si)

                    r_w1[j % 2].consumed("h_pe")
                    r_wg[j % 2].consumed("g_pe")

                    def _w_next(g, e=e, j=j):
                        jn = j + 2
                        if jn < NJ:
                            r_w1[jn % 2].wait_free(g)
                            b.inc(g.dma_start(
                                out=w1_sl[:, (jn % 2) * 1024:(jn % 2) * 1024 + 1024],
                                in_=w1_d[e, jn]), "w1d", 16)
                            w1_arr[(e, jn, 0)] = b.snap("w1d")
                            r_wg[jn % 2].wait_free(g)
                            b.inc(g.dma_start(
                                out=wg_sl[:, (jn % 2) * 1024:(jn % 2) * 1024 + 1024],
                                in_=wg_d[e, jn]), "w1d", 16)
                            w1_arr[(e, jn, 1)] = b.snap("w1d")
                        if j == 0:
                            g_w2.wait_free(g)
                        b.inc(g.dma_start(out=w2_sb[:, j * C:(j + 1) * C],
                                          in_=w2_d[e, j * 128:(j + 1) * 128, :]),
                              "w2d", 16)
                        w2_arr[(e, j)] = b.snap("w2d")
                    blk.gpsimd(_w_next)

                if dump and e == 0:
                    def _dump_afm(s, e=e):
                        b.wait(s, "a_s", b.val("a_s"))
                        b.inc(s.dma_start(out=afm_dump, in_=a_fm[:]), "dmp", 16)
                    blk.sync(_dump_afm)
                for bi, (s0, w) in enumerate(SLOT_BLOCKS):
                    def _p2_pe(t, e=e, bi=bi, s0=s0, w=w):
                        if bi == 0:
                            b.wait(t, "a_s", b.val("a_s"))
                            b.wait_snap(t, w2_arr[(e, NJ - 1)])
                        base_oev = b.val("oev_v")
                        for oc in range(NCC):
                            if oc < 2:
                                g_o[oc % 2].wait_free(t)
                            else:
                                b.wait(t, "oev_v", base_oev + oc - 1)
                            for jj in range(NJ):
                                b.inc(t.matmul(
                                    p_o[oc % 2][:, 0:w],
                                    w2_sb[:, jj * C + oc * 128: jj * C + oc * 128 + 128],
                                    a_fm[:, jj * CAPP + s0: jj * CAPP + s0 + w],
                                    start=(jj == 0), stop=(jj == NJ - 1)), "o_pe", 1)
                            p2_q.append((e, bi, oc, w, b.snap("o_pe")))
                    blk.tensor(_p2_pe)

                    def _p2_ev(v, e=e, bi=bi):
                        while p2_q:
                            ee, bb_, oc, w_, sn = p2_q.pop(0)
                            b.wait_snap(v, sn)
                            if oc == 0:
                                g_eost.wait_free(v)
                                b.wait(v, "eow", b.val("eow"))
                            v.tensor_scalar_add(eo_st[:, oc * 512:oc * 512 + w_],
                                                p_o[oc % 2][:, 0:w_],
                                                b2_sb[:, oc:oc + 1])
                            b.dinc(v, "oev_v", 1)
                            g_o[oc % 2].consumed("oev_v")
                    blk.vector(_p2_ev)

                    if dump and e == 0 and bi == 0:
                        def _dump_eost(s, e=e):
                            b.wait(s, "oev_v", b.val("oev_v"))
                            b.inc(s.dma_start(out=eost_dump, in_=eo_st[:]), "dmp", 16)
                        blk.sync(_dump_eost)
                    nst_b = w // 128
                    for st in range(nst_b):
                        def _eo_tp(t, e=e, bi=bi, s0=s0, st=st):
                            b.wait(t, "oev_v", b.val("oev_v"))
                            base = b.val("ete_v")
                            for oc in range(NCC):
                                if base + oc > 0:
                                    b.wait(t, "ete_v", base + oc)
                                b.inc(t.transpose(
                                    p_et[:],
                                    eo_st[:, oc * 512 + st * 128: oc * 512 + st * 128 + 128],
                                    idf_sb[:]), "etp_pe", 1)
                        blk.tensor(_eo_tp)

                        def _eo_ev(v, e=e, bi=bi, s0=s0, st=st):
                            r_eosl.wait_free(v)
                            base = b.val("etp_pe") - NCC
                            for oc in range(NCC):
                                b.wait(v, "etp_pe", base + oc + 1)
                                b.inc(v.tensor_copy(out=eo_sl[:, oc * 128:(oc + 1) * 128],
                                                    in_=p_et[:]), "ete_v", 1)
                            b.dinc(v, "ete_d", 1)
                        blk.vector(_eo_ev)

                        def _eo_dma(s, e=e, bi=bi, s0=s0, st=st):
                            b.wait(s, "ete_d", b.val("ete_d"))
                            b.inc(s.dma_start(out=eo_t[e * NST + s0 // 128 + st],
                                              in_=eo_sl[:]), "eow", 16)
                        blk.sync(_eo_dma)
                        r_eosl.consumed("eow")
                    g_eost.consumed("etp_pe")

                g_afm.consumed("o_pe")
                g_w2.consumed("o_pe")
                g_bias.consumed("m_v")
                g_bias.consumed("tg_s")
                g_bias.consumed("oev_v")

        stPD.close()
        stD.close()

    if "E" in phases:
        stE = ExitStack()

        def SE(name, shape, dtype):
            return stE.enter_context(nc.sbuf_tensor(name, shape, dtype))

        g0_sb = SE("g0_sb", [128, 2 * C], F32)
        g1_sb = SE("g1_sb", [128, 2 * C], F32)
        rr_sb = SE("rr_sb", [128, 2 * C], F32)
        fo_sb = SE("fo_sb", [128, 2 * C], F32)
        r_g0 = [Res(b) for _ in range(2)]
        r_g1 = [Res(b) for _ in range(2)]
        r_rr = [Res(b) for _ in range(2)]
        r_fo = [Res(b) for _ in range(2)]

        with nc.Block() as blk:
            for j in range(NT):
                sl = j % 2

                def _e_rd(s, j=j, sl=sl):
                    if j == 0:
                        b.wait(s, "o_acc", b.val("o_acc"))
                    r_rr[sl].wait_free(s)
                    b.inc(s.dma_start(out=rr_sb[:, sl * C:(sl + 1) * C], in_=out_t[j]),
                          "rrd", 16)
                blk.sync(_e_rd)

                def _e_g(g, j=j, sl=sl):
                    if j == 0:
                        b.wait(g, "eow", b.val("eow"))
                    r_g0[sl].wait_free(g)
                    b.inc(g.indirect_dma_start(
                        out=g0_sb[:, sl * C:(sl + 1) * C], out_offset=None,
                        in_=eo_d,
                        in_offset=IndirectOffsetOnAxis(gidx_sb[:, 2 * j:2 * j + 1], 0)),
                          "gat", 16)
                    r_g1[sl].wait_free(g)
                    b.inc(g.indirect_dma_start(
                        out=g1_sb[:, sl * C:(sl + 1) * C], out_offset=None,
                        in_=eo_d,
                        in_offset=IndirectOffsetOnAxis(gidx_sb[:, 2 * j + 1:2 * j + 2], 0)),
                          "gat", 16)
                blk.gpsimd(_e_g)

                def _e_v(v, j=j, sl=sl):
                    b.wait(v, "gat", b.val("gat"))
                    b.wait(v, "rrd", b.val("rrd"))
                    r_fo[sl].wait_free(v)
                    v.tensor_scalar_mul(fo_sb[:, sl * C:(sl + 1) * C],
                                        g0_sb[:, sl * C:(sl + 1) * C],
                                        prob_sb[:, 2 * j:2 * j + 1])
                    v.drain()
                    v.scalar_tensor_tensor(out=fo_sb[:, sl * C:(sl + 1) * C],
                                           in0=g1_sb[:, sl * C:(sl + 1) * C],
                                           scalar=prob_sb[:, 2 * j + 1:2 * j + 2],
                                           in1=fo_sb[:, sl * C:(sl + 1) * C],
                                           op0=ALU.mult, op1=ALU.add)
                    v.drain()
                    v.tensor_add(fo_sb[:, sl * C:(sl + 1) * C],
                                 fo_sb[:, sl * C:(sl + 1) * C],
                                 rr_sb[:, sl * C:(sl + 1) * C])
                    b.dinc(v, "cmb_v", 1)
                    r_g0[sl].consumed("cmb_v")
                    r_g1[sl].consumed("cmb_v")
                    r_rr[sl].consumed("cmb_v")
                blk.vector(_e_v)

                def _e_wr(s, j=j, sl=sl):
                    b.wait(s, "cmb_v", b.val("cmb_v"))
                    b.inc(s.dma_start(out=out_t[j], in_=fo_sb[:, sl * C:(sl + 1) * C]),
                          "fo", 16)
                blk.sync(_e_wr)
                r_fo[sl].consumed("fo")

        stE.close()

    with nc.Block() as blk:
        def _wait_all(s):
            for name in ("x_in", "o_init", "o_acc", "zi", "cs", "slab", "w1d",
                         "w2d", "bia", "eow", "rrd", "gat", "fo", "aw_ld",
                         "c_ld", "dmp"):
                if name in b.sems:
                    b.wait(s, name, b.val(name))
        blk.sync(_wait_all)

    return nc


def _prep_inputs(inputs):
    f32 = np.float32
    x = np.ascontiguousarray(np.asarray(inputs["L0_attn_x"], dtype=f32))
    wq = np.asarray(inputs["L0_attn_Wq"], dtype=f32)
    wk = np.asarray(inputs["L0_attn_Wk"], dtype=f32)
    wv = np.asarray(inputs["L0_attn_Wv"], dtype=f32)
    wo = np.asarray(inputs["L0_attn_Wo"], dtype=f32)
    bo = np.asarray(inputs["L0_attn_bo"], dtype=f32)
    rw = np.ascontiguousarray(np.asarray(inputs["L0_ffn_router_w"], dtype=f32))
    rb = np.asarray(inputs["L0_ffn_router_b"], dtype=f32)
    w1 = np.asarray(inputs["L0_ffn_w_c_fc"], dtype=f32)
    b1 = np.asarray(inputs["L0_ffn_b_c_fc"], dtype=f32)
    wg = np.asarray(inputs["L0_ffn_w_gate"], dtype=f32)
    bg = np.asarray(inputs["L0_ffn_b_gate"], dtype=f32)
    w2 = np.ascontiguousarray(np.asarray(inputs["L0_ffn_w_c_proj"], dtype=f32))
    b2 = np.asarray(inputs["L0_ffn_b_c_proj"], dtype=f32)

    com = {
        "wq": np.ascontiguousarray(wq.T), "wk": np.ascontiguousarray(wk.T),
        "wv": np.ascontiguousarray(wv.T), "wo": np.ascontiguousarray(wo.T),
        "rw": rw,
        "bo_b": np.ascontiguousarray(np.broadcast_to(bo, (128, C))),
        "rb_b": np.ascontiguousarray(np.broadcast_to(rb, (128, E))),
        "w1r": np.ascontiguousarray(
            w1.reshape(E, NCC, 128, NJ, 128).transpose(0, 3, 2, 1, 4)
            .reshape(E, NJ, 128, 1024)),
        "wgr": np.ascontiguousarray(
            wg.reshape(E, NCC, 128, NJ, 128).transpose(0, 3, 2, 1, 4)
            .reshape(E, NJ, 128, 1024)),
        "w2": w2,
        "b1c": np.ascontiguousarray(b1.reshape(E, NJ, 128).transpose(0, 2, 1)),
        "bgc": np.ascontiguousarray(bg.reshape(E, NJ, 128).transpose(0, 2, 1)),
        "b2c": np.ascontiguousarray(b2.reshape(E, NCC, 128).transpose(0, 2, 1)),
        "ident_bf": np.eye(128, dtype=ml_dtypes.bfloat16),
        "ident_f32": np.eye(128, dtype=f32),
        "masks": _mk_masks(),
        "iota8_b": np.ascontiguousarray(
            np.broadcast_to(np.arange(E, dtype=f32), (128, E))),
    }
    return x, com


def kernel(**inputs):
    x, com = _prep_inputs(inputs)
    nc = build_nc()
    in_maps = []
    for i in range(8):
        m = dict(com)
        m["x"] = np.ascontiguousarray(x[i])
        in_maps.append(m)
    res = run_bass_kernel_spmd(nc, in_maps, core_ids=list(range(8)))
    out = np.stack([res.results[i]["out"] for i in range(8)], axis=0)
    return out.astype(np.float32)


# version 05
# speedup vs baseline: 1.2759x; 1.2759x over previous
import numpy as np
import ml_dtypes
import concourse.bass as bass
import concourse.mybir as mybir
from concourse.bass import IndirectOffsetOnAxis
from concourse.bass_utils import run_bass_kernel_spmd

F32 = mybir.dt.float32
BF16 = mybir.dt.bfloat16
I32 = mybir.dt.int32
AF = mybir.ActivationFunctionType
ALU = mybir.AluOpType
AX = mybir.AxisListType

T = 4096
NT = 32
C = 1024
NCC = 8
NH = 16
HID = 4096
NJ = 32
E = 8
CAP = 1228
CAPP = 1280
NST = 10
NSLOT = E * CAPP
SCALE = 0.125
NEG = -30000.0
DROP_BIG = float(1 << 20)
LN_EPS = 1e-5
SLOT_BLOCKS = [(0, 512), (512, 512), (1024, 256)]


def _mk_masks():
    m = np.full((3, 128, 384), NEG, np.float32)
    for p in range(128):
        lo, hi = p + 96, p + 159
        m[1, p, lo:hi + 1] = 0.0
        m[0, p, max(lo, 128):hi + 1] = 0.0
        if p <= 31:
            m[0, p, 128] = np.log(33.0 - p)
        m[2, p, lo:min(hi, 255) + 1] = 0.0
        if p >= 97:
            m[2, p, 255] = np.log(p - 95.0)
    return m


def _mk_lseg():
    L = np.zeros((128, 128), np.float32)
    for pp in range(128):
        for p in range(128):
            if pp % 8 == p % 8 and pp // 8 < p // 8:
                L[pp, p] = 1.0
    return L


class Builder:
    def __init__(self, nc):
        self.nc = nc
        self.sems = {}

    def s(self, name):
        if name not in self.sems:
            self.sems[name] = [self.nc.alloc_semaphore(name), 0]
        return self.sems[name]

    def inc(self, inst, name, k=1):
        h = self.s(name)
        inst.then_inc(h[0], k)
        h[1] += k

    def val(self, name):
        return self.s(name)[1]

    def wait(self, eng, name, v=None):
        h = self.s(name)
        eng.wait_ge(h[0], h[1] if v is None else v)

    def dinc(self, eng, name, k=1):
        eng.drain()
        self.inc(eng.nop(), name, k)

    def snap(self, name):
        return (name, self.s(name)[1])

    def wait_snap(self, eng, snap):
        if snap is not None:
            name, v = snap
            eng.wait_ge(self.s(name)[0], v)


class Res:
    def __init__(self, b):
        self.b = b
        self.snaps = []

    def wait_free(self, eng):
        for sn in self.snaps:
            self.b.wait_snap(eng, sn)
        self.snaps = []

    def consumed(self, name):
        self.snaps.append(self.b.snap(name))


def build_nc(phases="ABCDE", dump=False):
    nc = bass.Bass("TRN2", target_bir_lowering=False)
    b = Builder(nc)
    from contextlib import ExitStack

    din = dict(kind="ExternalInput")
    x_d = nc.dram_tensor("x", [T, C], F32, **din).ap()
    wq_d = nc.dram_tensor("wq", [C, C], F32, **din).ap()
    wk_d = nc.dram_tensor("wk", [C, C], F32, **din).ap()
    wv_d = nc.dram_tensor("wv", [C, C], F32, **din).ap()
    wo_d = nc.dram_tensor("wo", [C, C], F32, **din).ap()
    rw_d = nc.dram_tensor("rw", [C, E], F32, **din).ap()
    bo_d = nc.dram_tensor("bo_b", [128, C], F32, **din).ap()
    rb_d = nc.dram_tensor("rb_b", [128, E], F32, **din).ap()
    w1_d = nc.dram_tensor("w1r", [E, NJ, 128, 1024], F32, **din).ap()
    wg_d = nc.dram_tensor("wgr", [E, NJ, 128, 1024], F32, **din).ap()
    w2_d = nc.dram_tensor("w2", [E, HID, C], F32, **din).ap()
    b1_d = nc.dram_tensor("b1c", [E, 128, NJ], F32, **din).ap()
    bg_d = nc.dram_tensor("bgc", [E, 128, NJ], F32, **din).ap()
    b2_d = nc.dram_tensor("b2c", [E, 128, E], F32, **din).ap()
    idb_d = nc.dram_tensor("ident_bf", [128, 128], BF16, **din).ap()
    idf_d = nc.dram_tensor("ident_f32", [128, 128], F32, **din).ap()
    msk_d = nc.dram_tensor("masks", [3, 128, 384], F32, **din).ap()
    io8_d = nc.dram_tensor("iota8_b", [128, E], F32, **din).ap()

    skind = dict(kind="ExternalOutput") if dump else dict(kind="Internal")
    einp_d = nc.dram_tensor("einp_hbm", [NSLOT + 1, C], BF16, **skind).ap()
    eo_d = nc.dram_tensor("eo_hbm", [NSLOT, C], F32, **skind).ap()
    out_d = nc.dram_tensor("out", [T, C], F32, kind="ExternalOutput").ap()
    if dump:
        xn_dump = nc.dram_tensor("xn_dump", [T, C], BF16, kind="ExternalOutput").ap()
        lg_dump = nc.dram_tensor("lg_dump", [128, NT * E], F32, kind="ExternalOutput").ap()
        rt_dump = nc.dram_tensor("rt_dump", [4, 128, 2 * NT], F32,
                                 kind="ExternalOutput").ap()
        afm_dump = nc.dram_tensor("afm_dump", [128, NJ * CAPP], BF16,
                                  kind="ExternalOutput").ap()
        eost_dump = nc.dram_tensor("eost_dump", [128, E * 512], F32,
                                   kind="ExternalOutput").ap()

    x_t = x_d.rearrange("(n p) c -> n p c", p=128)
    out_t = out_d.rearrange("(n p) c -> n p c", p=128)
    einp_t = einp_d[0:NSLOT, :].rearrange("(n p) c -> n p c", p=128)
    eo_t = eo_d.rearrange("(n p) c -> n p c", p=128)

    idb_sb = nc.alloc_sbuf_tensor("idb_sb", [128, 128], BF16)
    idf_sb = nc.alloc_sbuf_tensor("idf_sb", [128, 128], F32)
    i01_sb = nc.alloc_sbuf_tensor("i01_sb", [128, 2 * NT], F32)
    prob_sb = nc.alloc_sbuf_tensor("prob_sb", [128, 2 * NT], F32)
    sidx_sb = nc.alloc_sbuf_tensor("sidx_sb", [128, 2 * NT], I32)
    gidx_sb = nc.alloc_sbuf_tensor("gidx_sb", [128, 2 * NT], I32)

    stA = ExitStack()

    def SA(name, shape, dtype):
        return stA.enter_context(nc.sbuf_tensor(name, shape, dtype))

    xn_all = SA("xn_all", [128, NT * C], BF16)
    zeros_sb = SA("zeros_sb", [128, C], BF16)
    cst_sb = SA("cst_sb", [128, 2], F32)
    rb_sb = SA("rb_sb", [128, E], F32)
    io8_sb = SA("io8_sb", [128, E], F32)
    ltok_all = SA("ltok_all", [128, NT * E], F32)

    stA2 = ExitStack()
    def SA2(name, shape, dtype):
        return stA2.enter_context(nc.sbuf_tensor(name, shape, dtype))

    bo_sb = SA2("bo_sb", [128, C], F32)
    msk_sb = SA2("msk_sb", [128, 3 * 384], F32)
    wq_sb = SA2("wq_sb", [128, NCC * C], BF16)
    wk_sb = SA2("wk_sb", [128, NCC * C], BF16)
    wv_sb = SA2("wv_sb", [128, NCC * C], BF16)
    wo_sb = SA2("wo_sb", [128, NCC * C], BF16)
    rw32_sb = SA2("rw32_sb", [128, NCC * E], F32)
    xs_sb = SA2("xs_sb", [128, 2 * C], F32)
    sq_sb = SA2("sq_sb", [128, 2 * C], F32)
    st_sb = SA2("st_sb", [128, 16], F32)
    xf_sb = SA2("xf_sb", [128, 2 * C], BF16)
    xf32_sb = SA2("xf32_sb", [128, C], F32)
    q_sb = SA2("q_sb", [128, 2 * C], BF16)
    k_sb = SA2("k_sb", [128, 3 * C], BF16)
    v_sb = SA2("v_sb", [128, 3 * C], BF16)
    sc_sb = SA2("sc_sb", [128, 2 * 384], F32)
    exn_sb = SA2("exn_sb", [128, 2 * 384], F32)
    exT_sb = SA2("exT_sb", [128, 2 * 384], BF16)
    sm_sb = SA2("sm_sb", [128, 16], F32)
    ao_sb = SA2("ao_sb", [128, 2 * C], BF16)
    wout_sb = SA2("wout_sb", [128, 2 * C], F32)

    stPA = ExitStack()

    def PA(name, shape, dtype):
        return stPA.enter_context(nc.psum_tensor(name, shape, dtype))

    p_tp = [PA(f"p_tp{i}", [128, 128], F32) for i in range(3)]
    p_mm = [PA(f"p_mm{i}", [128, 512], F32) for i in range(2)]
    p_sc = [PA(f"p_sc{i}", [128, 384], F32) for i in range(2)]
    p_av = PA("p_av", [128, 128], F32)

    r_xs = [Res(b) for _ in range(2)]
    r_sq = [Res(b) for _ in range(2)]
    r_xf = [Res(b) for _ in range(2)]
    r_q = [Res(b) for _ in range(2)]
    r_ao = [Res(b) for _ in range(2)]
    r_wout = [Res(b) for _ in range(2)]
    r_sc = [Res(b) for _ in range(2)]
    r_exn = [Res(b) for _ in range(2)]
    r_exT = [Res(b) for _ in range(2)]
    r_sm = [Res(b) for _ in range(2)]
    g_tp = [Res(b) for _ in range(3)]
    g_mm = [Res(b) for _ in range(2)]
    g_sc = [Res(b) for _ in range(2)]
    g_av = Res(b)

    exn_done = {}
    exT_done = {}
    qk_done = {}
    tp_done = {}
    av_done = {}
    evac_q = []
    wo_evac_q = []

    with nc.Block() as blk:
        def _pre_v(v):
            v.memset(cst_sb[:, 0:1], LN_EPS)
            b.inc(v.memset(cst_sb[:, 1:2], -0.5), "cst_v", 1)
        blk.vector(_pre_v)

        def _pre_sy(s):
            for dst, src in ((bo_sb, bo_d), (rb_sb, rb_d), (io8_sb, io8_d),
                             (idb_sb, idb_d), (idf_sb, idf_d)):
                b.inc(s.dma_start(out=dst[:], in_=src), "c_ld", 16)
            b.inc(s.dma_start(out=msk_sb[:], in_=msk_d.transpose([1, 0, 2])),
                  "c_ld", 16)
        blk.sync(_pre_sy)

        def _pre_gp(g):
            g.memset(zeros_sb[:], 0.0)
            b.inc(g.tensor_copy(out=zeros_sb[:, 0:1], in_=zeros_sb[:, 0:1]), "z_done", 1)
            for dst, src in ((wq_sb, wq_d), (wk_sb, wk_d), (wv_sb, wv_d), (wo_sb, wo_d)):
                for cc in range(NCC):
                    b.inc(g.dma_start(out=dst[:, cc * C:(cc + 1) * C],
                                      in_=src[cc * 128:(cc + 1) * 128, :]), "aw_ld", 16)
        blk.gpsimd(_pre_gp)

        def _pre_rw(s):
            for cc in range(NCC):
                b.inc(s.dma_start(out=rw32_sb[:, cc * E:(cc + 1) * E],
                                  in_=rw_d[cc * 128:(cc + 1) * 128, :]), "c_ld", 16)
        blk.sync(_pre_rw)

        def _pre_zi(s):
            b.wait(s, "z_done")
            for i in range(NSLOT // 128):
                b.inc(s.dma_start(out=einp_t[i], in_=zeros_sb[:]), "zi", 16)
        blk.scalar(_pre_zi)

        def attn_pe_body(t, jq, h):
            def kvt(dk):
                tt = jq - 1 + dk
                return None if (tt < 0 or tt > NT - 1) else tt % 3
            if h - 2 >= 0:
                h2 = h - 2
                hp2 = (h2 % 2) * 64
                b.wait_snap(t, exT_done[(jq, h2)])
                if h2 % 2 == 0:
                    g_av.wait_free(t)
                for dk in range(3):
                    s3 = kvt(dk)
                    vt = (zeros_sb[:, h2 * 64:h2 * 64 + 64] if s3 is None
                          else v_sb[:, s3 * C + h2 * 64: s3 * C + h2 * 64 + 64])
                    b.inc(t.matmul(p_av[hp2:hp2 + 64, :], vt,
                                   exT_sb[:, (h2 % 2) * 384 + dk * 128:
                                          (h2 % 2) * 384 + dk * 128 + 128],
                                   start=(dk == 0), stop=(dk == 2),
                                   tile_position=(0, hp2)), "av_pe", 1)
                av_done[(jq, h2)] = b.snap("av_pe")
                r_exT[h2 % 2].consumed("av_pe")
            if 0 <= h - 1 <= NH - 1:
                h1 = h - 1
                b.wait_snap(t, exn_done[(jq, h1)])
                for dk in range(3):
                    g_tp[dk].wait_free(t)
                    b.inc(t.transpose(p_tp[dk][:],
                                      exn_sb[:, (h1 % 2) * 384 + dk * 128:
                                             (h1 % 2) * 384 + dk * 128 + 128],
                                      idf_sb[:]), "atp_pe", 1)
                tp_done[(jq, h1)] = b.snap("atp_pe")
                r_exn[h1 % 2].consumed("atp_pe")
            if h <= NH - 1:
                hp, hc = (h % 2) * 64, (h // 2) * 128
                qs = (jq % 2) * C
                g_sc[h % 2].wait_free(t)
                for dk in range(3):
                    s3 = kvt(dk)
                    kt = (zeros_sb[hp:hp + 64, hc:hc + 128] if s3 is None
                          else k_sb[hp:hp + 64, s3 * C + hc: s3 * C + hc + 128])
                    b.inc(t.matmul(p_sc[h % 2][:, dk * 128:(dk + 1) * 128],
                                   q_sb[hp:hp + 64, qs + hc: qs + hc + 128], kt,
                                   start=(dk == 0), stop=(dk == 2)), "qk_pe", 1)
                qk_done[(jq, h)] = b.snap("qk_pe")

        def attn_vec_body(v, jq, h, msk):
            if h - 2 >= 0 and (h - 2) % 2 == 1:
                h2 = h - 2
                cc = h2 // 2
                b.wait_snap(v, av_done[(jq, h2)])
                if cc == 0:
                    r_ao[jq % 2].wait_free(v)
                v.tensor_copy(
                    out=ao_sb[:, (jq % 2) * C + cc * 128:(jq % 2) * C + cc * 128 + 128],
                    in_=p_av[:])
                b.dinc(v, "ave_v", 1)
                g_av.consumed("ave_v")
            if 0 <= h - 1 <= NH - 1:
                h1 = h - 1
                b.wait_snap(v, tp_done[(jq, h1)])
                r_exT[h1 % 2].wait_free(v)
                for dk in range(3):
                    b.inc(v.tensor_copy(
                        out=exT_sb[:, (h1 % 2) * 384 + dk * 128:
                                   (h1 % 2) * 384 + dk * 128 + 128],
                        in_=p_tp[dk][:]), "exT_v", 1)
                    g_tp[dk].consumed("exT_v")
                b.dinc(v, "exT_d", 1)
                exT_done[(jq, h1)] = b.snap("exT_d")
            if h <= NH - 1:
                hs = (h % 2) * 384
                b.wait_snap(v, qk_done[(jq, h)])
                r_sc[h % 2].wait_free(v)
                b.inc(v.scalar_tensor_tensor(out=sc_sb[:, hs:hs + 384],
                                             in0=p_sc[h % 2][:], scalar=SCALE,
                                             in1=msk, op0=ALU.mult, op1=ALU.add),
                      "msk_v", 1)
                g_sc[h % 2].consumed("msk_v")
                smx = sm_sb[:, (h % 2) * 8:(h % 2) * 8 + 8]
                r_sm[h % 2].wait_free(v)
                v.reduce_max(smx[:, 0:1], sc_sb[:, hs:hs + 384], axis=AX.X)
                v.drain()
                b.inc(v.tensor_scalar_mul(smx[:, 1:2], smx[:, 0:1], -1.0), "mx_v", 1)

        def attn_act_body(s, jq, h):
            if h <= NH - 1:
                hs = (h % 2) * 384
                smx = sm_sb[:, (h % 2) * 8:(h % 2) * 8 + 8]
                b.wait(s, "mx_v", b.val("mx_v"))
                r_exn[h % 2].wait_free(s)
                b.inc(s.activation(out=exn_sb[:, hs:hs + 384],
                                   in_=sc_sb[:, hs:hs + 384],
                                   func=AF.Exp, bias=smx[:, 1:2],
                                   accum_out=smx[:, 2:3]), "exp_s", 1)
                r_sc[h % 2].consumed("exp_s")
                r_sm[h % 2].consumed("exp_s")

        def attn_vec2_body(v, jq, h):
            if h <= NH - 1:
                hs = (h % 2) * 384
                smx = sm_sb[:, (h % 2) * 8:(h % 2) * 8 + 8]
                b.wait(v, "exp_s", b.val("exp_s"))
                v.reciprocal(smx[:, 3:4], smx[:, 2:3])
                v.drain()
                v.tensor_scalar_mul(exn_sb[:, hs:hs + 384], exn_sb[:, hs:hs + 384],
                                    smx[:, 3:4])
                b.dinc(v, "exn_v", 1)
                exn_done[(jq, h)] = b.snap("exn_v")

        for j in range(NT + 1):
            jq = j - 1
            sl = j % 2
            if j < NT:
                def _ld(s, j=j, sl=sl):
                    r_xs[sl].wait_free(s)
                    b.inc(s.dma_start(out=xs_sb[:, sl * C:(sl + 1) * C], in_=x_t[j]),
                          "x_in", 16)
                blk.sync(_ld)

                def _ln_v(v, j=j, sl=sl):
                    xv = xs_sb[:, sl * C:(sl + 1) * C]
                    st = st_sb[:, sl * 8:sl * 8 + 8]
                    b.wait(v, "x_in", 16 * (j + 1))
                    if j == 0:
                        b.wait(v, "c_ld")
                    v.reduce_sum(st[:, 0:1], xv, axis=AX.X)
                    v.drain()
                    v.tensor_scalar_mul(st[:, 2:3], st[:, 0:1], 1.0 / C)
                    b.dinc(v, "oi_v", 1)
                blk.vector(_ln_v)

                def _ln_s(s, j=j, sl=sl):
                    xv = xs_sb[:, sl * C:(sl + 1) * C]
                    st = st_sb[:, sl * 8:sl * 8 + 8]
                    b.wait(s, "x_in", 16 * (j + 1))
                    r_sq[sl].wait_free(s)
                    b.inc(s.activation(out=sq_sb[:, sl * C:(sl + 1) * C], in_=xv,
                                       func=AF.Square, accum_out=st[:, 1:2]),
                          "ln_s1", 1)
                blk.scalar(_ln_s)

                def _ln_v2(v, j=j, sl=sl):
                    st = st_sb[:, sl * 8:sl * 8 + 8]
                    b.wait(v, "ln_s1", b.val("ln_s1"))
                    v.tensor_mul(st[:, 3:4], st[:, 2:3], st[:, 2:3])
                    v.drain()
                    v.scalar_tensor_tensor(out=st[:, 4:5], in0=st[:, 1:2],
                                           scalar=1.0 / C, in1=st[:, 3:4],
                                           op0=ALU.mult, op1=ALU.subtract)
                    b.dinc(v, "ln_v2", 1)
                blk.vector(_ln_v2)

                def _ln_s2(s, j=j, sl=sl):
                    st = st_sb[:, sl * 8:sl * 8 + 8]
                    b.wait(s, "ln_v2", b.val("ln_v2"))
                    if j == 0:
                        b.wait(s, "cst_v")
                    s.activation(out=st[:, 5:6], in_=st[:, 4:5], func=AF.Ln,
                                 bias=cst_sb[:, 0:1])
                    s.drain()
                    b.inc(s.activation(out=st[:, 6:7], in_=st[:, 5:6], func=AF.Exp,
                                       scale=cst_sb[:, 1:2]), "ln_s2", 1)
                blk.scalar(_ln_s2)

                def _ln_v3(v, j=j, sl=sl):
                    st = st_sb[:, sl * 8:sl * 8 + 8]
                    b.wait(v, "ln_s2", b.val("ln_s2"))
                    v.scalar_tensor_tensor(out=st[:, 7:8], in0=st[:, 2:3],
                                           scalar=-1.0, in1=st[:, 6:7],
                                           op0=ALU.mult, op1=ALU.mult)
                    b.dinc(v, "ln_v3", 1)
                blk.vector(_ln_v3)

                def _xn(v, j=j, sl=sl):
                    xv = xs_sb[:, sl * C:(sl + 1) * C]
                    st = st_sb[:, sl * 8:sl * 8 + 8]
                    b.wait(v, "ln_s2", b.val("ln_s2"))
                    v.scalar_tensor_tensor(out=sq_sb[:, sl * C:(sl + 1) * C],
                                           in0=xv, scalar=st[:, 6:7],
                                           in1=st[:, 7:8].to_broadcast([128, C]),
                                           op0=ALU.mult, op1=ALU.add)
                    v.tensor_copy(out=xn_all[:, j * C:(j + 1) * C],
                                  in_=sq_sb[:, sl * C:(sl + 1) * C])
                    b.dinc(v, "xn_s", 1)
                blk.vector(_xn)
                r_xs[sl].consumed("xn_s")
                r_xs[sl].consumed("oi_v")

                def _oinit(s, j=j, sl=sl):
                    b.wait(s, "oi_v", b.val("oi_v"))
                    b.inc(s.dma_start(out=out_t[j], in_=xs_sb[:, sl * C:(sl + 1) * C]),
                          "o_init", 16)
                blk.sync(_oinit)
                r_xs[sl].consumed("o_init")

                def _tp(t, j=j, sl=sl):
                    b.wait(t, "xn_s", b.val("xn_s"))
                    r_xf[sl].wait_free(t)
                    base = b.val("tpe_v")
                    for cc in range(NCC):
                        if cc < 3:
                            g_tp[cc].wait_free(t)
                        else:
                            b.wait(t, "tpe_v", base + 2 * (cc - 2))
                        b.inc(t.transpose(p_tp[cc % 3][:],
                                          sq_sb[:, sl * C + cc * 128: sl * C + cc * 128 + 128],
                                          idf_sb[:]), "tp_pe", 1)
                blk.tensor(_tp)
                r_sq[sl].consumed("tp_pe")

                def _tpe(v, j=j, sl=sl):
                    base = b.val("tp_pe") - NCC
                    for cc in range(NCC):
                        b.wait(v, "tp_pe", base + cc + 1)
                        b.inc(v.tensor_copy(
                            out=xf_sb[:, sl * C + cc * 128: sl * C + cc * 128 + 128],
                            in_=p_tp[cc % 3][:]), "tpe_v", 1)
                        b.inc(v.tensor_copy(
                            out=xf32_sb[:, cc * 128: cc * 128 + 128],
                            in_=p_tp[cc % 3][:]), "tpe_v", 1)
                        g_tp[cc % 3].consumed("tpe_v")
                    b.dinc(v, "tpe_d", 1)
                blk.vector(_tpe)

                def _qkv(t, j=j, sl=sl):
                    if j == 0:
                        b.wait(t, "aw_ld")
                    b.wait(t, "tpe_d", b.val("tpe_d"))
                    b.wait(t, "woe_v", b.val("woe_v"))
                    xf = xf_sb[:, sl * C:(sl + 1) * C]
                    base = b.val("qkv_v")
                    uses = [("q", ec) for ec in range(NCC)] + \
                           [("k", ec) for ec in range(NCC)] + \
                           [("v", 0), ("v", 1), ("lg", 0)]
                    prev_use = {}
                    for u, (kind, idx) in enumerate(uses):
                        bank = idx % 2 if kind != "lg" else 0
                        if bank in prev_use:
                            b.wait(t, "qkv_v", base + prev_use[bank] + 1)
                        else:
                            g_mm[bank].wait_free(t)
                        prev_use[bank] = u
                        if kind == "q" or kind == "k":
                            wsb = wq_sb if kind == "q" else wk_sb
                            for cc in range(NCC):
                                b.inc(t.matmul(p_mm[bank][:, 0:128],
                                               wsb[:, cc * C + idx * 128: cc * C + idx * 128 + 128],
                                               xf[:, cc * 128:(cc + 1) * 128],
                                               start=(cc == 0), stop=(cc == NCC - 1)),
                                      "mm_pe", 1)
                        elif kind == "v":
                            for cc in range(NCC):
                                b.inc(t.matmul(p_mm[bank][:, 0:512],
                                               xf[:, cc * 128:(cc + 1) * 128],
                                               wv_sb[:, cc * C + idx * 512: cc * C + idx * 512 + 512],
                                               start=(cc == 0), stop=(cc == NCC - 1)),
                                      "mm_pe", 1)
                        else:
                            for cc in range(NCC):
                                b.inc(t.matmul(p_mm[bank][0:128, 0:E],
                                               xf32_sb[:, cc * 128:(cc + 1) * 128],
                                               rw32_sb[:, cc * E:(cc + 1) * E],
                                               start=(cc == 0), stop=(cc == NCC - 1)),
                                      "mm_pe", 1)
                        evac_q.append((j, kind, idx, bank, b.snap("mm_pe")))
                blk.tensor(_qkv)
                r_xf[sl].consumed("mm_pe")

                def _qkve(v, j=j, sl=sl):
                    while evac_q:
                        jj, kind, idx, bank, sn = evac_q.pop(0)
                        b.wait_snap(v, sn)
                        if kind == "q":
                            if idx == 0:
                                r_q[jj % 2].wait_free(v)
                            b.inc(v.tensor_copy(
                                out=q_sb[:, (jj % 2) * C + idx * 128:(jj % 2) * C + idx * 128 + 128],
                                in_=p_mm[bank][:, 0:128]), "qkv_v", 1)
                        elif kind == "k":
                            b.inc(v.tensor_copy(
                                out=k_sb[:, (jj % 3) * C + idx * 128:(jj % 3) * C + idx * 128 + 128],
                                in_=p_mm[bank][:, 0:128]), "qkv_v", 1)
                        elif kind == "v":
                            b.inc(v.tensor_copy(
                                out=v_sb[:, (jj % 3) * C + idx * 512:(jj % 3) * C + idx * 512 + 512],
                                in_=p_mm[bank][:, 0:512]), "qkv_v", 1)
                        else:
                            v.tensor_copy(out=ltok_all[:, jj * E:(jj + 1) * E],
                                          in_=p_mm[bank][0:128, 0:E])
                            b.dinc(v, "lgv", 1)
                        g_mm[bank].consumed("qkv_v" if kind != "lg" else "lgv")
                    b.dinc(v, "qkv_d", 1)
                blk.vector(_qkve)


            if 0 <= jq <= NT - 1:
                mtile = 0 if jq == 0 else (2 if jq == NT - 1 else 1)
                msk = msk_sb[:, mtile * 384:(mtile + 1) * 384]
                for h in range(NH + 2):
                    def _a_pe(t, jq=jq, h=h):
                        if h == 0:
                            b.wait(t, "qkv_d", b.val("qkv_d"))
                            if jq == 0:
                                b.wait(t, "z_done")
                        attn_pe_body(t, jq, h)
                    blk.tensor(_a_pe)

                    def _a_ve(v, jq=jq, h=h, msk=msk):
                        attn_vec_body(v, jq, h, msk)
                    blk.vector(_a_ve)

                    def _a_sc(s, jq=jq, h=h):
                        attn_act_body(s, jq, h)
                    blk.scalar(_a_sc)

                    def _a_ve2(v, jq=jq, h=h):
                        attn_vec2_body(v, jq, h)
                    blk.vector(_a_ve2)
                r_q[jq % 2].consumed("qk_pe")

                def _wo(t, jq=jq):
                    b.wait(t, "ave_v", b.val("ave_v"))
                    for half in range(2):
                        g_mm[half].wait_free(t)
                        for fc in range(NCC):
                            b.inc(t.matmul(p_mm[half][:, 0:512],
                                           ao_sb[:, (jq % 2) * C + fc * 128:(jq % 2) * C + fc * 128 + 128],
                                           wo_sb[:, fc * C + half * 512: fc * C + half * 512 + 512],
                                           start=(fc == 0), stop=(fc == NCC - 1)),
                                  "wo_pe", 1)
                        wo_evac_q.append((jq, half, b.snap("wo_pe")))
                blk.tensor(_wo)
                r_ao[jq % 2].consumed("wo_pe")

                def _woe(v, jq=jq):
                    while wo_evac_q:
                        jj, half, sn = wo_evac_q.pop(0)
                        b.wait_snap(v, sn)
                        if half == 0:
                            r_wout[jj % 2].wait_free(v)
                        b.inc(v.tensor_tensor(
                            out=wout_sb[:, (jj % 2) * C + half * 512:(jj % 2) * C + half * 512 + 512],
                            in0=p_mm[half][:, 0:512],
                            in1=bo_sb[:, half * 512: half * 512 + 512],
                            op=ALU.add), "woe_v", 1)
                        g_mm[half].consumed("woe_v")
                    b.dinc(v, "woe_d", 1)
                blk.vector(_woe)

                def _oacc(g, jq=jq):
                    b.wait(g, "woe_d", b.val("woe_d"))
                    b.wait(g, "o_init", 16 * (jq + 1))
                    b.inc(g.dma_start(out=out_t[jq],
                                      in_=wout_sb[:, (jq % 2) * C:(jq % 2) * C + C],
                                      accum_op=ALU.add), "o_acc", 16)
                blk.gpsimd(_oacc)
                r_wout[jq % 2].consumed("o_acc")

        if dump:
            def _dump_a(s):
                b.wait(s, "xn_s", b.val("xn_s"))
                for j in range(NT):
                    b.inc(s.dma_start(
                        out=xn_dump.rearrange("(n p) c -> n p c", p=128)[j],
                        in_=xn_all[:, j * C:(j + 1) * C]), "dmp", 16)
                b.wait(s, "lgv", b.val("lgv"))
                b.inc(s.dma_start(out=lg_dump, in_=ltok_all[:]), "dmp", 16)
            blk.sync(_dump_a)

    g_bt = [Res(b) for _ in range(2)]
    if "B" in phases:
        stA2.close()
        stB = ExitStack()

        def SB(name, shape, dtype):
            return stB.enter_context(nc.sbuf_tensor(name, shape, dtype))

        m_em = SB("m_em", [E, 2 * T], F32)
        cnt_em = SB("cnt_em", [E, 2 * T], F32)
        smB_sb = SB("smB_sb", [128, 16], F32)
        oh_sb = SB("oh_sb", [128, 2 * E], F32)
        ltok_sb = SB("ltok_sb", [128, E], F32)
        top8_sb = SB("top8_sb", [128, E], F32)
        mi_sb = SB("mi_sb", [128, E], mybir.dt.uint32)
        ctok_sb = SB("ctok_sb", [128, 2 * E], F32)
        tmp8_sb = SB("tmp8_sb", [128, 8], F32)
        zcol_sb = SB("zcol_sb", [E, 1], F32)

        with nc.Block() as blk:
            for j in range(NT):
                def _b_v(v, j=j):
                    b.inc(v.tensor_add(ltok_sb[:], ltok_all[:, j * E:(j + 1) * E],
                                       rb_sb[:]), "bv", 1)
                    v.drain()
                    v.reduce_max(smB_sb[:, 8:9], ltok_sb[:], axis=AX.X)
                    v.drain()
                    v.tensor_scalar(out=oh_sb[:, 0:E], in0=ltok_sb[:],
                                    scalar1=smB_sb[:, 8:9], scalar2=None,
                                    op0=ALU.is_ge)
                    v.drain()
                    v.tensor_mul(tmp8_sb[:], oh_sb[:, 0:E], io8_sb[:])
                    v.drain()
                    v.reduce_sum(i01_sb[:, 2 * j:2 * j + 1], tmp8_sb[:], axis=AX.X)
                    v.scalar_tensor_tensor(out=top8_sb[:], in0=oh_sb[:, 0:E],
                                           scalar=-DROP_BIG, in1=ltok_sb[:],
                                           op0=ALU.mult, op1=ALU.add)
                    v.drain()
                    v.reduce_max(smB_sb[:, 9:10], top8_sb[:], axis=AX.X)
                    v.drain()
                    v.tensor_scalar(out=oh_sb[:, E:2 * E], in0=top8_sb[:],
                                    scalar1=smB_sb[:, 9:10], scalar2=None,
                                    op0=ALU.is_ge)
                    v.drain()
                    v.tensor_mul(tmp8_sb[:], oh_sb[:, E:2 * E], io8_sb[:])
                    v.drain()
                    v.reduce_sum(i01_sb[:, 2 * j + 1:2 * j + 2], tmp8_sb[:], axis=AX.X)
                    v.tensor_sub(smB_sb[:, 0:1], smB_sb[:, 9:10], smB_sb[:, 8:9])
                    b.dinc(v, "bv", 1)
                blk.vector(_b_v)

                def _b_s(s, j=j):
                    b.wait(s, "bv", b.val("bv"))
                    b.inc(s.activation(out=smB_sb[:, 2:3], in_=smB_sb[:, 0:1],
                                       func=AF.Exp), "bs", 1)
                blk.scalar(_b_s)

                def _b_v2(v, j=j):
                    b.wait(v, "bs", b.val("bs"))
                    v.tensor_scalar_add(smB_sb[:, 3:4], smB_sb[:, 2:3], 1.0)
                    v.drain()
                    v.reciprocal(prob_sb[:, 2 * j:2 * j + 1], smB_sb[:, 3:4])
                    v.drain()
                    v.tensor_scalar(out=prob_sb[:, 2 * j + 1:2 * j + 2],
                                    in0=prob_sb[:, 2 * j:2 * j + 1],
                                    scalar1=-1.0, scalar2=1.0,
                                    op0=ALU.mult, op1=ALU.add)
                    for kk in range(2):
                        v.tensor_scalar(out=oh_sb[:, kk * E:(kk + 1) * E],
                                        in0=io8_sb[:],
                                        scalar1=i01_sb[:, 2 * j + kk:2 * j + kk + 1],
                                        scalar2=None, op0=ALU.is_equal)
                    b.dinc(v, "bv2", 1)
                blk.vector(_b_v2)

                def _b_tp2(t, j=j):
                    b.wait(t, "bv2", b.val("bv2"))
                    for kk in range(2):
                        g_bt[kk].wait_free(t)
                        t.transpose(p_mm[kk][0:8, 0:128],
                                    oh_sb[:, kk * E:(kk + 1) * E], idf_sb[:])
                    b.dinc(t, "btp2_pe", 2)
                blk.tensor(_b_tp2)

                def _b_v3(v, j=j):
                    b.wait(v, "btp2_pe", b.val("btp2_pe"))
                    for kk in range(2):
                        v.tensor_copy(
                            out=m_em[:, kk * T + j * 128: kk * T + j * 128 + 128],
                            in_=p_mm[kk][0:8, 0:128])
                    b.dinc(v, "bv3", 1)
                    for kk in range(2):
                        g_bt[kk].consumed("bv3")
                blk.vector(_b_v3)

            def _scan(v):
                b.wait(v, "bv3", b.val("bv3"))
                v.memset(zcol_sb[:], 0.0)
                v.drain()
                v.tensor_tensor_scan(out=cnt_em[:], data0=m_em[:],
                                     data1=zcol_sb[:].to_broadcast([E, 2 * T]),
                                     initial=0.0, op0=ALU.add, op1=ALU.add)
                b.dinc(v, "scan_v", 1)
            blk.vector(_scan)

            for j in range(NT):
                def _c_tp(t, j=j):
                    if j == 0:
                        b.wait(t, "scan_v", b.val("scan_v"))
                    for kk in range(2):
                        g_bt[kk].wait_free(t)
                        b.inc(t.transpose(p_mm[kk][0:128, 0:8],
                                          cnt_em[:, kk * T + j * 128: kk * T + j * 128 + 128],
                                          idf_sb[0:8, 0:8]), "ctp_pe", 1)
                blk.tensor(_c_tp)

                def _c_v(v, j=j):
                    b.wait(v, "ctp_pe", b.val("ctp_pe"))
                    for kk in range(2):
                        b.inc(v.tensor_copy(out=ctok_sb[:, kk * E:(kk + 1) * E],
                                            in_=p_mm[kk][0:128, 0:8]), "cv", 1)
                        g_bt[kk].consumed("cv")
                    v.drain()
                    for kk in range(2):
                        ctok = ctok_sb[:, kk * E:(kk + 1) * E]
                        v.tensor_scalar(out=oh_sb[:, kk * E:(kk + 1) * E], in0=io8_sb[:],
                                        scalar1=i01_sb[:, 2 * j + kk:2 * j + kk + 1],
                                        scalar2=None, op0=ALU.is_equal)
                        v.drain()
                        v.tensor_mul(tmp8_sb[:], oh_sb[:, kk * E:(kk + 1) * E], ctok)
                        v.drain()
                        v.reduce_sum(smB_sb[:, 0:1], tmp8_sb[:], axis=AX.X)
                        v.drain()
                        v.tensor_scalar_add(smB_sb[:, 1:2], smB_sb[:, 0:1], -1.0)
                        v.drain()
                        v.scalar_tensor_tensor(out=smB_sb[:, 2:3],
                                               in0=i01_sb[:, 2 * j + kk:2 * j + kk + 1],
                                               scalar=float(CAPP), in1=smB_sb[:, 1:2],
                                               op0=ALU.mult, op1=ALU.add)
                        v.tensor_scalar(out=smB_sb[:, 3:4], in0=smB_sb[:, 1:2],
                                        scalar1=float(CAP), scalar2=None, op0=ALU.is_lt)
                        v.drain()
                        v.tensor_mul(smB_sb[:, 4:5], smB_sb[:, 2:3], smB_sb[:, 3:4])
                        v.tensor_scalar(out=smB_sb[:, 7:8], in0=smB_sb[:, 3:4],
                                        scalar1=-float(NSLOT), scalar2=float(NSLOT),
                                        op0=ALU.mult, op1=ALU.add)
                        v.drain()
                        v.tensor_add(smB_sb[:, 4:5], smB_sb[:, 4:5], smB_sb[:, 7:8])
                        v.drain()
                        v.tensor_copy(out=sidx_sb[:, 2 * j + kk:2 * j + kk + 1],
                                      in_=smB_sb[:, 4:5])
                        v.tensor_scalar_min(smB_sb[:, 5:6], smB_sb[:, 1:2],
                                            float(CAP - 1))
                        v.drain()
                        v.scalar_tensor_tensor(out=smB_sb[:, 6:7],
                                               in0=i01_sb[:, 2 * j + kk:2 * j + kk + 1],
                                               scalar=float(CAPP), in1=smB_sb[:, 5:6],
                                               op0=ALU.mult, op1=ALU.add)
                        v.drain()
                        v.tensor_copy(out=gidx_sb[:, 2 * j + kk:2 * j + kk + 1],
                                      in_=smB_sb[:, 6:7])
                    b.dinc(v, "cv", 1)
                blk.vector(_c_v)

            if dump:
                def _dump_rt(s):
                    b.wait(s, "cv", b.val("cv"))
                    b.inc(s.dma_start(out=rt_dump[0], in_=i01_sb[:]), "dmp", 16)
                    b.inc(s.dma_start(out=rt_dump[1], in_=prob_sb[:]), "dmp", 16)
                    b.inc(s.dma_start(out=rt_dump[2], in_=sidx_sb[:].bitcast(F32)),
                          "dmp", 16)
                    b.inc(s.dma_start(out=rt_dump[3], in_=gidx_sb[:].bitcast(F32)),
                          "dmp", 16)
                blk.sync(_dump_rt)
        stB.close()
    else:
        stA2.close()

    if "C" in phases:
        with nc.Block() as blk:
            def _scat(g):
                b.wait(g, "zi", b.val("zi"))
                for j in range(NT):
                    for kk in range(2):
                        b.inc(g.indirect_dma_start(
                            out=einp_d,
                            out_offset=IndirectOffsetOnAxis(
                                sidx_sb[:, 2 * j + kk:2 * j + kk + 1], 0),
                            in_=xn_all[:, j * C:(j + 1) * C],
                            in_offset=None), "cs", 16)
                g.wait_ge(b.s("cs")[0], b.val("cs"))
            blk.gpsimd(_scat)

    stPA.close()
    stA.close()

    if "D" in phases:
        stD = ExitStack()

        def SD(name, shape, dtype):
            return stD.enter_context(nc.sbuf_tensor(name, shape, dtype))

        a_fm = SD("a_fm", [128, NJ * CAPP], BF16)
        einp_fm = SD("einp_fm", [128, NCC * CAPP], BF16)
        eit_sb = SD("eit_sb", [128, 2 * C], BF16)
        w2_sb = SD("w2_sb", [128, NJ * C], BF16)
        w1_sl = SD("w1_sl", [128, 2 * 1024], BF16)
        wg_sl = SD("wg_sl", [128, 2 * 1024], BF16)
        tg_sb = SD("tg_sb", [128, 512], F32)
        m_sb2 = SD("m_sb2", [128, 512], F32)
        eo_st = SD("eo_st", [128, E * 512], F32)
        eo_sl = SD("eo_sl", [128, C], F32)
        b1_sb = SD("b1_sb", [128, NJ], F32)
        bg_sb = SD("bg_sb", [128, NJ], F32)
        b2_sb = SD("b2_sb", [128, E], F32)

        stPD = ExitStack()

        def PDp(name, shape, dtype):
            return stPD.enter_context(nc.psum_tensor(name, shape, dtype))

        p_h = PDp("p_h", [128, 512], F32)
        p_g = PDp("p_g", [128, 512], F32)
        p_o = [PDp(f"p_o{i}", [128, 512], F32) for i in range(2)]
        p_dt = [PDp(f"p_dt{i}", [128, 128], BF16) for i in range(2)]
        p_et = PDp("p_et", [128, 128], F32)

        g_h = Res(b)
        g_g = Res(b)
        g_o = [Res(b) for _ in range(2)]
        r_eit = [Res(b) for _ in range(2)]
        r_w1 = [Res(b) for _ in range(2)]
        r_wg = [Res(b) for _ in range(2)]
        r_tg = [Res(b) for _ in range(2)]
        r_m = [Res(b) for _ in range(2)]
        r_eosl = Res(b)
        g_w2 = Res(b)
        g_afm = Res(b)
        g_eost = Res(b)
        g_bias = Res(b)

        w1_arr = {}
        w2_arr = {}
        p1_q = []
        tg_q = []
        m_q = []
        p2_q = []

        with nc.Block() as blk:
            for e in range(E):
                def _bias(s, e=e):
                    if e == 0 and "C" in phases:
                        b.wait(s, "cs", b.val("cs"))
                    g_bias.wait_free(s)
                    b.inc(s.dma_start(out=b1_sb[:], in_=b1_d[e]), "bia", 16)
                    b.inc(s.dma_start(out=bg_sb[:], in_=bg_d[e]), "bia", 16)
                    b.inc(s.dma_start(out=b2_sb[:], in_=b2_d[e]), "bia", 16)
                blk.sync(_bias)
                bia_now = b.snap("bia")

                for st in range(NST):
                    def _slab_ld(s, e=e, st=st):
                        r_eit[st % 2].wait_free(s)
                        b.inc(s.dma_start(out=eit_sb[:, (st % 2) * C:(st % 2) * C + C],
                                          in_=einp_t[e * NST + st]), "slab", 16)
                    blk.sync(_slab_ld)

                    def _slab_tp(t, e=e, st=st):
                        b.wait(t, "slab", b.val("slab"))
                        base = b.val("dte_v")
                        for cc in range(NCC):
                            if base + cc - 1 > 0:
                                b.wait(t, "dte_v", base + cc - 1)
                            b.inc(t.transpose(p_dt[cc % 2][:],
                                              eit_sb[:, (st % 2) * C + cc * 128:
                                                     (st % 2) * C + cc * 128 + 128],
                                              idb_sb[:]), "dtp_pe", 1)
                    blk.tensor(_slab_tp)

                    def _slab_ev(v, e=e, st=st):
                        base = b.val("dtp_pe") - NCC
                        for cc in range(NCC):
                            b.wait(v, "dtp_pe", base + cc + 1)
                            b.inc(v.tensor_copy(
                                out=einp_fm[:, cc * CAPP + st * 128: cc * CAPP + st * 128 + 128],
                                in_=p_dt[cc % 2][:]), "dte_v", 1)
                        b.dinc(v, "dte_d", 1)
                    blk.vector(_slab_ev)
                    r_eit[st % 2].consumed("dtp_pe")
                slab_done = b.snap("dte_d")

                def _w_pre(g, e=e):
                    for jj in range(2):
                        r_w1[jj].wait_free(g)
                        b.inc(g.dma_start(out=w1_sl[:, jj * 1024:(jj + 1) * 1024],
                                          in_=w1_d[e, jj]), "w1d", 16)
                        w1_arr[(e, jj, 0)] = b.snap("w1d")
                        r_wg[jj].wait_free(g)
                        b.inc(g.dma_start(out=wg_sl[:, jj * 1024:(jj + 1) * 1024],
                                          in_=wg_d[e, jj]), "w1d", 16)
                        w1_arr[(e, jj, 1)] = b.snap("w1d")
                blk.gpsimd(_w_pre)

                for j in range(NJ):
                    def _p1_pe(t, e=e, j=j):
                        if j == 0:
                            b.wait_snap(t, slab_done)
                            g_afm.wait_free(t)
                        b.wait_snap(t, w1_arr[(e, j, 0)])
                        b.wait_snap(t, w1_arr[(e, j, 1)])
                        base_m = b.val("m_v")
                        base_tg = b.val("tg_s")
                        nbi = len(SLOT_BLOCKS)
                        for bi, (s0, w) in enumerate(SLOT_BLOCKS):
                            gi = (j * nbi + bi) % 2
                            ph = p_h if gi == 0 else p_o[0]
                            pg = p_g if gi == 0 else p_o[1]
                            if bi >= 2:
                                b.wait(t, "m_v", base_m + bi - 1)
                            else:
                                (g_h if gi == 0 else g_o[0]).wait_free(t)
                                (g_g if gi == 0 else g_o[1]).wait_free(t)
                            for cc in range(NCC):
                                b.inc(t.matmul(
                                    pg[:, 0:w],
                                    wg_sl[:, (j % 2) * 1024 + cc * 128:(j % 2) * 1024 + cc * 128 + 128],
                                    einp_fm[:, cc * CAPP + s0: cc * CAPP + s0 + w],
                                    start=(cc == 0), stop=(cc == NCC - 1)), "g_pe", 1)
                            g_now = b.snap("g_pe")
                            for cc in range(NCC):
                                b.inc(t.matmul(
                                    ph[:, 0:w],
                                    w1_sl[:, (j % 2) * 1024 + cc * 128:(j % 2) * 1024 + cc * 128 + 128],
                                    einp_fm[:, cc * CAPP + s0: cc * CAPP + s0 + w],
                                    start=(cc == 0), stop=(cc == NCC - 1)), "h_pe", 1)
                            p1_q.append((e, j, bi, s0, w, b.snap("h_pe"), g_now, gi))
                    blk.tensor(_p1_pe)

                    for bi in range(len(SLOT_BLOCKS)):
                        def _p1_sc(v, e=e, j=j, bi=bi):
                            for item in list(p1_q):
                                ee, jj, bb_, s0, w, hsn, gsn, gi = item
                                if (ee, jj, bb_) != (e, j, bi):
                                    continue
                                b.wait_snap(v, gsn)
                                if jj == 0 and bb_ == 0:
                                    b.wait_snap(v, bia_now)
                                r_tg[0].wait_free(v)
                                pg = p_g if gi == 0 else p_o[1]
                                v.tensor_scalar_add(tg_sb[:, 0:w], pg[:, 0:w],
                                                    bg_sb[:, jj:jj + 1])
                                b.dinc(v, "tg_s", 1)
                                (g_g if gi == 0 else g_o[1]).consumed("tg_s")
                                tg_q.append((ee, jj, bb_, s0, w, hsn, b.snap("tg_s"), gi))
                                p1_q.remove(item)
                        blk.vector(_p1_sc)

                        def _p1_v(v, e=e, j=j, bi=bi):
                            for item in list(tg_q):
                                ee, jj, bb_, s0, w, hsn, tsn, gi = item
                                if (ee, jj, bb_) != (e, j, bi):
                                    continue
                                b.wait_snap(v, hsn)
                                b.wait_snap(v, tsn)
                                r_m[0].wait_free(v)
                                ph = p_h if gi == 0 else p_o[0]
                                v.scalar_tensor_tensor(
                                    out=m_sb2[:, 0:w],
                                    in0=ph[:, 0:w], scalar=b1_sb[:, jj:jj + 1],
                                    in1=tg_sb[:, 0:w],
                                    op0=ALU.add, op1=ALU.mult)
                                b.dinc(v, "m_v", 1)
                                (g_h if gi == 0 else g_o[0]).consumed("m_v")
                                r_tg[0].consumed("m_v")
                                m_q.append((ee, jj, bb_, s0, w, b.snap("m_v")))
                                tg_q.remove(item)
                        blk.vector(_p1_v)

                        def _p1_si(s, e=e, j=j, bi=bi):
                            for item in list(m_q):
                                ee, jj, bb_, s0, w, msn = item
                                if (ee, jj, bb_) != (e, j, bi):
                                    continue
                                b.wait_snap(s, msn)
                                b.inc(s.activation(
                                    out=a_fm[:, jj * CAPP + s0: jj * CAPP + s0 + w],
                                    in_=m_sb2[:, 0:w],
                                    func=AF.Silu), "a_s", 1)
                                r_m[0].consumed("a_s")
                                m_q.remove(item)
                        blk.scalar(_p1_si)

                    r_w1[j % 2].consumed("h_pe")
                    r_wg[j % 2].consumed("g_pe")

                    def _w_next(g, e=e, j=j):
                        jn = j + 2
                        if jn < NJ:
                            r_w1[jn % 2].wait_free(g)
                            b.inc(g.dma_start(
                                out=w1_sl[:, (jn % 2) * 1024:(jn % 2) * 1024 + 1024],
                                in_=w1_d[e, jn]), "w1d", 16)
                            w1_arr[(e, jn, 0)] = b.snap("w1d")
                            r_wg[jn % 2].wait_free(g)
                            b.inc(g.dma_start(
                                out=wg_sl[:, (jn % 2) * 1024:(jn % 2) * 1024 + 1024],
                                in_=wg_d[e, jn]), "w1d", 16)
                            w1_arr[(e, jn, 1)] = b.snap("w1d")
                        if j == 0:
                            g_w2.wait_free(g)
                        b.inc(g.dma_start(out=w2_sb[:, j * C:(j + 1) * C],
                                          in_=w2_d[e, j * 128:(j + 1) * 128, :]),
                              "w2d", 16)
                        w2_arr[(e, j)] = b.snap("w2d")
                    blk.gpsimd(_w_next)

                if dump and e == 0:
                    def _dump_afm(s, e=e):
                        b.wait(s, "a_s", b.val("a_s"))
                        b.inc(s.dma_start(out=afm_dump, in_=a_fm[:]), "dmp", 16)
                    blk.sync(_dump_afm)
                for bi, (s0, w) in enumerate(SLOT_BLOCKS):
                    def _p2_pe(t, e=e, bi=bi, s0=s0, w=w):
                        if bi == 0:
                            b.wait(t, "a_s", b.val("a_s"))
                            b.wait_snap(t, w2_arr[(e, NJ - 1)])
                        base_oev = b.val("oev_v")
                        for oc in range(NCC):
                            if oc < 2:
                                g_o[oc % 2].wait_free(t)
                            else:
                                b.wait(t, "oev_v", base_oev + oc - 1)
                            for jj in range(NJ):
                                b.inc(t.matmul(
                                    p_o[oc % 2][:, 0:w],
                                    w2_sb[:, jj * C + oc * 128: jj * C + oc * 128 + 128],
                                    a_fm[:, jj * CAPP + s0: jj * CAPP + s0 + w],
                                    start=(jj == 0), stop=(jj == NJ - 1)), "o_pe", 1)
                            p2_q.append((e, bi, oc, w, b.snap("o_pe")))
                    blk.tensor(_p2_pe)

                    def _p2_ev(v, e=e, bi=bi):
                        while p2_q:
                            ee, bb_, oc, w_, sn = p2_q.pop(0)
                            b.wait_snap(v, sn)
                            if oc == 0:
                                g_eost.wait_free(v)
                                b.wait(v, "eow", b.val("eow"))
                            v.tensor_scalar_add(eo_st[:, oc * 512:oc * 512 + w_],
                                                p_o[oc % 2][:, 0:w_],
                                                b2_sb[:, oc:oc + 1])
                            b.dinc(v, "oev_v", 1)
                            g_o[oc % 2].consumed("oev_v")
                    blk.vector(_p2_ev)

                    if dump and e == 0 and bi == 0:
                        def _dump_eost(s, e=e):
                            b.wait(s, "oev_v", b.val("oev_v"))
                            b.inc(s.dma_start(out=eost_dump, in_=eo_st[:]), "dmp", 16)
                        blk.sync(_dump_eost)
                    nst_b = w // 128
                    for st in range(nst_b):
                        def _eo_tp(t, e=e, bi=bi, s0=s0, st=st):
                            b.wait(t, "oev_v", b.val("oev_v"))
                            base = b.val("ete_v")
                            for oc in range(NCC):
                                if base + oc > 0:
                                    b.wait(t, "ete_v", base + oc)
                                b.inc(t.transpose(
                                    p_et[:],
                                    eo_st[:, oc * 512 + st * 128: oc * 512 + st * 128 + 128],
                                    idf_sb[:]), "etp_pe", 1)
                        blk.tensor(_eo_tp)

                        def _eo_ev(v, e=e, bi=bi, s0=s0, st=st):
                            r_eosl.wait_free(v)
                            base = b.val("etp_pe") - NCC
                            for oc in range(NCC):
                                b.wait(v, "etp_pe", base + oc + 1)
                                b.inc(v.tensor_copy(out=eo_sl[:, oc * 128:(oc + 1) * 128],
                                                    in_=p_et[:]), "ete_v", 1)
                            b.dinc(v, "ete_d", 1)
                        blk.vector(_eo_ev)

                        def _eo_dma(s, e=e, bi=bi, s0=s0, st=st):
                            b.wait(s, "ete_d", b.val("ete_d"))
                            b.inc(s.dma_start(out=eo_t[e * NST + s0 // 128 + st],
                                              in_=eo_sl[:]), "eow", 16)
                        blk.sync(_eo_dma)
                        r_eosl.consumed("eow")
                    g_eost.consumed("etp_pe")

                g_afm.consumed("o_pe")
                g_w2.consumed("o_pe")
                g_bias.consumed("m_v")
                g_bias.consumed("tg_s")
                g_bias.consumed("oev_v")

        stPD.close()
        stD.close()

    if "E" in phases:
        stE = ExitStack()

        def SE(name, shape, dtype):
            return stE.enter_context(nc.sbuf_tensor(name, shape, dtype))

        g0_sb = SE("g0_sb", [128, 2 * C], F32)
        g1_sb = SE("g1_sb", [128, 2 * C], F32)
        rr_sb = SE("rr_sb", [128, 2 * C], F32)
        fo_sb = SE("fo_sb", [128, 2 * C], F32)
        r_g0 = [Res(b) for _ in range(2)]
        r_g1 = [Res(b) for _ in range(2)]
        r_rr = [Res(b) for _ in range(2)]
        r_fo = [Res(b) for _ in range(2)]

        with nc.Block() as blk:
            for j in range(NT):
                sl = j % 2

                def _e_rd(s, j=j, sl=sl):
                    if j == 0:
                        b.wait(s, "o_acc", b.val("o_acc"))
                    r_rr[sl].wait_free(s)
                    b.inc(s.dma_start(out=rr_sb[:, sl * C:(sl + 1) * C], in_=out_t[j]),
                          "rrd", 16)
                blk.sync(_e_rd)

                def _e_g(g, j=j, sl=sl):
                    if j == 0:
                        b.wait(g, "eow", b.val("eow"))
                    r_g0[sl].wait_free(g)
                    b.inc(g.indirect_dma_start(
                        out=g0_sb[:, sl * C:(sl + 1) * C], out_offset=None,
                        in_=eo_d,
                        in_offset=IndirectOffsetOnAxis(gidx_sb[:, 2 * j:2 * j + 1], 0)),
                          "gat", 16)
                    r_g1[sl].wait_free(g)
                    b.inc(g.indirect_dma_start(
                        out=g1_sb[:, sl * C:(sl + 1) * C], out_offset=None,
                        in_=eo_d,
                        in_offset=IndirectOffsetOnAxis(gidx_sb[:, 2 * j + 1:2 * j + 2], 0)),
                          "gat", 16)
                blk.gpsimd(_e_g)

                def _e_v(v, j=j, sl=sl):
                    b.wait(v, "gat", b.val("gat"))
                    b.wait(v, "rrd", b.val("rrd"))
                    r_fo[sl].wait_free(v)
                    v.tensor_scalar_mul(fo_sb[:, sl * C:(sl + 1) * C],
                                        g0_sb[:, sl * C:(sl + 1) * C],
                                        prob_sb[:, 2 * j:2 * j + 1])
                    v.drain()
                    v.scalar_tensor_tensor(out=fo_sb[:, sl * C:(sl + 1) * C],
                                           in0=g1_sb[:, sl * C:(sl + 1) * C],
                                           scalar=prob_sb[:, 2 * j + 1:2 * j + 2],
                                           in1=fo_sb[:, sl * C:(sl + 1) * C],
                                           op0=ALU.mult, op1=ALU.add)
                    v.drain()
                    v.tensor_add(fo_sb[:, sl * C:(sl + 1) * C],
                                 fo_sb[:, sl * C:(sl + 1) * C],
                                 rr_sb[:, sl * C:(sl + 1) * C])
                    b.dinc(v, "cmb_v", 1)
                    r_g0[sl].consumed("cmb_v")
                    r_g1[sl].consumed("cmb_v")
                    r_rr[sl].consumed("cmb_v")
                blk.vector(_e_v)

                def _e_wr(s, j=j, sl=sl):
                    b.wait(s, "cmb_v", b.val("cmb_v"))
                    b.inc(s.dma_start(out=out_t[j], in_=fo_sb[:, sl * C:(sl + 1) * C]),
                          "fo", 16)
                blk.sync(_e_wr)
                r_fo[sl].consumed("fo")

        stE.close()

    with nc.Block() as blk:
        def _wait_all(s):
            for name in ("x_in", "o_init", "o_acc", "zi", "cs", "slab", "w1d",
                         "w2d", "bia", "eow", "rrd", "gat", "fo", "aw_ld",
                         "c_ld", "dmp"):
                if name in b.sems:
                    b.wait(s, name, b.val(name))
        blk.sync(_wait_all)

    return nc


def _prep_inputs(inputs):
    f32 = np.float32
    x = np.ascontiguousarray(np.asarray(inputs["L0_attn_x"], dtype=f32))
    wq = np.asarray(inputs["L0_attn_Wq"], dtype=f32)
    wk = np.asarray(inputs["L0_attn_Wk"], dtype=f32)
    wv = np.asarray(inputs["L0_attn_Wv"], dtype=f32)
    wo = np.asarray(inputs["L0_attn_Wo"], dtype=f32)
    bo = np.asarray(inputs["L0_attn_bo"], dtype=f32)
    rw = np.ascontiguousarray(np.asarray(inputs["L0_ffn_router_w"], dtype=f32))
    rb = np.asarray(inputs["L0_ffn_router_b"], dtype=f32)
    w1 = np.asarray(inputs["L0_ffn_w_c_fc"], dtype=f32)
    b1 = np.asarray(inputs["L0_ffn_b_c_fc"], dtype=f32)
    wg = np.asarray(inputs["L0_ffn_w_gate"], dtype=f32)
    bg = np.asarray(inputs["L0_ffn_b_gate"], dtype=f32)
    w2 = np.ascontiguousarray(np.asarray(inputs["L0_ffn_w_c_proj"], dtype=f32))
    b2 = np.asarray(inputs["L0_ffn_b_c_proj"], dtype=f32)

    com = {
        "wq": np.ascontiguousarray(wq.T), "wk": np.ascontiguousarray(wk.T),
        "wv": np.ascontiguousarray(wv.T), "wo": np.ascontiguousarray(wo.T),
        "rw": rw,
        "bo_b": np.ascontiguousarray(np.broadcast_to(bo, (128, C))),
        "rb_b": np.ascontiguousarray(np.broadcast_to(rb, (128, E))),
        "w1r": np.ascontiguousarray(
            w1.reshape(E, NCC, 128, NJ, 128).transpose(0, 3, 2, 1, 4)
            .reshape(E, NJ, 128, 1024)),
        "wgr": np.ascontiguousarray(
            wg.reshape(E, NCC, 128, NJ, 128).transpose(0, 3, 2, 1, 4)
            .reshape(E, NJ, 128, 1024)),
        "w2": w2,
        "b1c": np.ascontiguousarray(b1.reshape(E, NJ, 128).transpose(0, 2, 1)),
        "bgc": np.ascontiguousarray(bg.reshape(E, NJ, 128).transpose(0, 2, 1)),
        "b2c": np.ascontiguousarray(b2.reshape(E, NCC, 128).transpose(0, 2, 1)),
        "ident_bf": np.eye(128, dtype=ml_dtypes.bfloat16),
        "ident_f32": np.eye(128, dtype=f32),
        "masks": _mk_masks(),
        "iota8_b": np.ascontiguousarray(
            np.broadcast_to(np.arange(E, dtype=f32), (128, E))),
    }
    return x, com


def kernel(**inputs):
    x, com = _prep_inputs(inputs)
    nc = build_nc()
    in_maps = []
    for i in range(8):
        m = dict(com)
        m["x"] = np.ascontiguousarray(x[i])
        in_maps.append(m)
    res = run_bass_kernel_spmd(nc, in_maps, core_ids=list(range(8)))
    out = np.stack([res.results[i]["out"] for i in range(8)], axis=0)
    return out.astype(np.float32)


# version 06
# speedup vs baseline: 1.4082x; 1.1036x over previous
import numpy as np
import ml_dtypes
import concourse.bass as bass
import concourse.mybir as mybir
from concourse.bass import IndirectOffsetOnAxis
from concourse.bass_utils import run_bass_kernel_spmd

F32 = mybir.dt.float32
BF16 = mybir.dt.bfloat16
I32 = mybir.dt.int32
AF = mybir.ActivationFunctionType
ALU = mybir.AluOpType
AX = mybir.AxisListType

T = 4096
NT = 32
C = 1024
NCC = 8
NH = 16
HID = 4096
NJ = 32
E = 8
CAP = 1228
CAPP = 1280
NST = 10
NSLOT = E * CAPP
SCALE = 0.125
NEG = -30000.0
DROP_BIG = float(1 << 20)
LN_EPS = 1e-5
SLOT_BLOCKS = [(0, 512), (512, 512), (1024, 256)]


def _mk_masks():
    m = np.full((3, 128, 384), NEG, np.float32)
    for p in range(128):
        lo, hi = p + 96, p + 159
        m[1, p, lo:hi + 1] = 0.0
        m[0, p, max(lo, 128):hi + 1] = 0.0
        if p <= 31:
            m[0, p, 128] = np.log(33.0 - p)
        m[2, p, lo:min(hi, 255) + 1] = 0.0
        if p >= 97:
            m[2, p, 255] = np.log(p - 95.0)
    return m


def _mk_lseg():
    L = np.zeros((128, 128), np.float32)
    for pp in range(128):
        for p in range(128):
            if pp % 8 == p % 8 and pp // 8 < p // 8:
                L[pp, p] = 1.0
    return L


class Builder:
    def __init__(self, nc):
        self.nc = nc
        self.sems = {}

    def s(self, name):
        if name not in self.sems:
            self.sems[name] = [self.nc.alloc_semaphore(name), 0]
        return self.sems[name]

    def inc(self, inst, name, k=1):
        h = self.s(name)
        inst.then_inc(h[0], k)
        h[1] += k

    def val(self, name):
        return self.s(name)[1]

    def wait(self, eng, name, v=None):
        h = self.s(name)
        eng.wait_ge(h[0], h[1] if v is None else v)

    def dinc(self, eng, name, k=1):
        eng.drain()
        self.inc(eng.nop(), name, k)

    def snap(self, name):
        return (name, self.s(name)[1])

    def wait_snap(self, eng, snap):
        if snap is not None:
            name, v = snap
            eng.wait_ge(self.s(name)[0], v)


class Res:
    def __init__(self, b):
        self.b = b
        self.snaps = []

    def wait_free(self, eng):
        for sn in self.snaps:
            self.b.wait_snap(eng, sn)
        self.snaps = []

    def consumed(self, name):
        self.snaps.append(self.b.snap(name))


def build_nc(phases="ABCDE", dump=False):
    nc = bass.Bass("TRN2", target_bir_lowering=False)
    b = Builder(nc)
    from contextlib import ExitStack

    din = dict(kind="ExternalInput")
    x_d = nc.dram_tensor("x", [T, C], F32, **din).ap()
    wq_d = nc.dram_tensor("wq", [C, C], F32, **din).ap()
    wk_d = nc.dram_tensor("wk", [C, C], F32, **din).ap()
    wv_d = nc.dram_tensor("wv", [C, C], F32, **din).ap()
    wo_d = nc.dram_tensor("wo", [C, C], F32, **din).ap()
    rw_d = nc.dram_tensor("rw", [C, E], F32, **din).ap()
    bo_d = nc.dram_tensor("bo_b", [128, C], F32, **din).ap()
    rb_d = nc.dram_tensor("rb_b", [128, E], F32, **din).ap()
    w1_d = nc.dram_tensor("w1r", [E, NJ, 128, 1024], F32, **din).ap()
    wg_d = nc.dram_tensor("wgr", [E, NJ, 128, 1024], F32, **din).ap()
    w2_d = nc.dram_tensor("w2", [E, HID, C], F32, **din).ap()
    b1_d = nc.dram_tensor("b1c", [E, 128, NJ], F32, **din).ap()
    bg_d = nc.dram_tensor("bgc", [E, 128, NJ], F32, **din).ap()
    b2_d = nc.dram_tensor("b2c", [E, 128, E], F32, **din).ap()
    idb_d = nc.dram_tensor("ident_bf", [128, 128], BF16, **din).ap()
    idf_d = nc.dram_tensor("ident_f32", [128, 128], F32, **din).ap()
    msk_d = nc.dram_tensor("masks", [3, 128, 384], F32, **din).ap()
    io8_d = nc.dram_tensor("iota8_b", [128, E], F32, **din).ap()

    skind = dict(kind="ExternalOutput") if dump else dict(kind="Internal")
    einp_d = nc.dram_tensor("einp_hbm", [NSLOT + 1, C], BF16, **skind).ap()
    eo_d = nc.dram_tensor("eo_hbm", [NSLOT, C], F32, **skind).ap()
    out_d = nc.dram_tensor("out", [T, C], F32, kind="ExternalOutput").ap()
    if dump:
        xn_dump = nc.dram_tensor("xn_dump", [T, C], BF16, kind="ExternalOutput").ap()
        lg_dump = nc.dram_tensor("lg_dump", [128, NT * E], F32, kind="ExternalOutput").ap()
        rt_dump = nc.dram_tensor("rt_dump", [4, 128, 2 * NT], F32,
                                 kind="ExternalOutput").ap()
        afm_dump = nc.dram_tensor("afm_dump", [128, NJ * CAPP], BF16,
                                  kind="ExternalOutput").ap()
        eost_dump = nc.dram_tensor("eost_dump", [128, E * 512], F32,
                                   kind="ExternalOutput").ap()

    x_t = x_d.rearrange("(n p) c -> n p c", p=128)
    out_t = out_d.rearrange("(n p) c -> n p c", p=128)
    einp_t = einp_d[0:NSLOT, :].rearrange("(n p) c -> n p c", p=128)
    eo_t = eo_d.rearrange("(n p) c -> n p c", p=128)

    idb_sb = nc.alloc_sbuf_tensor("idb_sb", [128, 128], BF16)
    idf_sb = nc.alloc_sbuf_tensor("idf_sb", [128, 128], F32)
    i01_sb = nc.alloc_sbuf_tensor("i01_sb", [128, 2 * NT], F32)
    prob_sb = nc.alloc_sbuf_tensor("prob_sb", [128, 2 * NT], F32)
    sidx_sb = nc.alloc_sbuf_tensor("sidx_sb", [128, 2 * NT], I32)
    gidx_sb = nc.alloc_sbuf_tensor("gidx_sb", [128, 2 * NT], I32)

    stA = ExitStack()

    def SA(name, shape, dtype):
        return stA.enter_context(nc.sbuf_tensor(name, shape, dtype))

    xn_all = SA("xn_all", [128, NT * C], BF16)
    zeros_sb = SA("zeros_sb", [128, C], BF16)
    cst_sb = SA("cst_sb", [128, 2], F32)
    rb_sb = SA("rb_sb", [128, E], F32)
    io8_sb = SA("io8_sb", [128, E], F32)
    ltok_all = SA("ltok_all", [128, NT * E], F32)

    stA2 = ExitStack()
    def SA2(name, shape, dtype):
        return stA2.enter_context(nc.sbuf_tensor(name, shape, dtype))

    bo_sb = SA2("bo_sb", [128, C], F32)
    msk_sb = SA2("msk_sb", [128, 3 * 384], F32)
    wq_sb = SA2("wq_sb", [128, NCC * C], BF16)
    wk_sb = SA2("wk_sb", [128, NCC * C], BF16)
    wv_sb = SA2("wv_sb", [128, NCC * C], BF16)
    wo_sb = SA2("wo_sb", [128, NCC * C], BF16)
    rw32_sb = SA2("rw32_sb", [128, NCC * E], F32)
    xs_sb = SA2("xs_sb", [128, 2 * C], F32)
    sq_sb = SA2("sq_sb", [128, 2 * C], F32)
    st_sb = SA2("st_sb", [128, 16], F32)
    xf_sb = SA2("xf_sb", [128, 2 * C], BF16)
    xf32_sb = SA2("xf32_sb", [128, C], F32)
    q_sb = SA2("q_sb", [128, 2 * C], BF16)
    k_sb = SA2("k_sb", [128, 3 * C], BF16)
    v_sb = SA2("v_sb", [128, 3 * C], BF16)
    sc_sb = SA2("sc_sb", [128, 2 * 384], F32)
    exn_sb = SA2("exn_sb", [128, 2 * 384], F32)
    exT_sb = SA2("exT_sb", [128, 2 * 384], BF16)
    sm_sb = SA2("sm_sb", [128, 16], F32)
    ao_sb = SA2("ao_sb", [128, 2 * C], BF16)
    wout_sb = SA2("wout_sb", [128, 2 * C], F32)

    stPA = ExitStack()

    def PA(name, shape, dtype):
        return stPA.enter_context(nc.psum_tensor(name, shape, dtype))

    p_tp = [PA(f"p_tp{i}", [128, 128], F32) for i in range(3)]
    p_mm = [PA(f"p_mm{i}", [128, 512], F32) for i in range(2)]
    p_sc = [PA(f"p_sc{i}", [128, 384], F32) for i in range(2)]
    p_av = PA("p_av", [128, 128], F32)

    r_xs = [Res(b) for _ in range(2)]
    r_sq = [Res(b) for _ in range(2)]
    r_xf = [Res(b) for _ in range(2)]
    r_q = [Res(b) for _ in range(2)]
    r_ao = [Res(b) for _ in range(2)]
    r_wout = [Res(b) for _ in range(2)]
    r_sc = [Res(b) for _ in range(2)]
    r_exn = [Res(b) for _ in range(2)]
    r_exT = [Res(b) for _ in range(2)]
    r_sm = [Res(b) for _ in range(2)]
    g_tp = [Res(b) for _ in range(3)]
    g_mm = [Res(b) for _ in range(2)]
    g_sc = [Res(b) for _ in range(2)]
    g_av = Res(b)

    exn_done = {}
    exp_done = {}
    exT_done = {}
    qk_done = {}
    tp_done = {}
    av_done = {}
    evac_q = []
    wo_evac_q = []

    with nc.Block() as blk:
        def _pre_v(v):
            v.memset(cst_sb[:, 0:1], LN_EPS)
            b.inc(v.memset(cst_sb[:, 1:2], -0.5), "cst_v", 1)
        blk.vector(_pre_v)

        def _pre_sy(s):
            for dst, src in ((bo_sb, bo_d), (rb_sb, rb_d), (io8_sb, io8_d),
                             (idb_sb, idb_d), (idf_sb, idf_d)):
                b.inc(s.dma_start(out=dst[:], in_=src), "c_ld", 16)
            b.inc(s.dma_start(out=msk_sb[:], in_=msk_d.transpose([1, 0, 2])),
                  "c_ld", 16)
        blk.sync(_pre_sy)

        def _pre_gp(g):
            g.memset(zeros_sb[:], 0.0)
            b.inc(g.tensor_copy(out=zeros_sb[:, 0:1], in_=zeros_sb[:, 0:1]), "z_done", 1)
            for dst, src in ((wq_sb, wq_d), (wk_sb, wk_d), (wv_sb, wv_d), (wo_sb, wo_d)):
                for cc in range(NCC):
                    b.inc(g.dma_start(out=dst[:, cc * C:(cc + 1) * C],
                                      in_=src[cc * 128:(cc + 1) * 128, :]), "aw_ld", 16)
        blk.gpsimd(_pre_gp)

        def _pre_rw(s):
            for cc in range(NCC):
                b.inc(s.dma_start(out=rw32_sb[:, cc * E:(cc + 1) * E],
                                  in_=rw_d[cc * 128:(cc + 1) * 128, :]), "c_ld", 16)
        blk.sync(_pre_rw)

        def _pre_zi(s):
            b.wait(s, "z_done")
            for i in range(NSLOT // 128):
                b.inc(s.dma_start(out=einp_t[i], in_=zeros_sb[:]), "zi", 16)
        blk.scalar(_pre_zi)

        def kvt(jq, dk):
            tt = jq - 1 + dk
            return None if (tt < 0 or tt > NT - 1) else tt % 3

        def attn_pe_body(t, jq, h):
            if 0 <= h - 3 <= NH - 1:
                h2 = h - 3
                hp2 = (h2 % 2) * 64
                b.wait_snap(t, exT_done[(jq, h2)])
                if h2 % 2 == 0:
                    g_av.wait_free(t)
                for dk in range(3):
                    s3 = kvt(jq, dk)
                    vt = (zeros_sb[:, h2 * 64:h2 * 64 + 64] if s3 is None
                          else v_sb[:, s3 * C + h2 * 64: s3 * C + h2 * 64 + 64])
                    b.inc(t.matmul(p_av[hp2:hp2 + 64, :], vt,
                                   exT_sb[:, (h2 % 2) * 384 + dk * 128:
                                          (h2 % 2) * 384 + dk * 128 + 128],
                                   start=(dk == 0), stop=(dk == 2),
                                   tile_position=(0, hp2)), "av_pe", 1)
                av_done[(jq, h2)] = b.snap("av_pe")
                r_exT[h2 % 2].consumed("av_pe")
            if 0 <= h - 2 <= NH - 1:
                h1 = h - 2
                b.wait_snap(t, exn_done[(jq, h1)])
                for dk in range(3):
                    g_tp[dk].wait_free(t)
                    b.inc(t.transpose(p_tp[dk][:],
                                      exn_sb[:, (h1 % 2) * 384 + dk * 128:
                                             (h1 % 2) * 384 + dk * 128 + 128],
                                      idf_sb[:]), "atp_pe", 1)
                tp_done[(jq, h1)] = b.snap("atp_pe")
                r_exn[h1 % 2].consumed("atp_pe")
            if h <= NH - 1:
                hp, hc = (h % 2) * 64, (h // 2) * 128
                qs = (jq % 2) * C
                g_sc[h % 2].wait_free(t)
                for dk in range(3):
                    s3 = kvt(jq, dk)
                    kt = (zeros_sb[hp:hp + 64, hc:hc + 128] if s3 is None
                          else k_sb[hp:hp + 64, s3 * C + hc: s3 * C + hc + 128])
                    b.inc(t.matmul(p_sc[h % 2][:, dk * 128:(dk + 1) * 128],
                                   q_sb[hp:hp + 64, qs + hc: qs + hc + 128], kt,
                                   start=(dk == 0), stop=(dk == 2)), "qk_pe", 1)
                qk_done[(jq, h)] = b.snap("qk_pe")

        def attn_vec_body(v, jq, h, msk):
            if 0 <= h - 3 <= NH - 1 and (h - 3) % 2 == 1:
                h2 = h - 3
                cc = h2 // 2
                b.wait_snap(v, av_done[(jq, h2)])
                if cc == 0:
                    r_ao[jq % 2].wait_free(v)
                v.tensor_copy(
                    out=ao_sb[:, (jq % 2) * C + cc * 128:(jq % 2) * C + cc * 128 + 128],
                    in_=p_av[:])
                b.dinc(v, "ave_v", 1)
                g_av.consumed("ave_v")
            if 0 <= h - 2 <= NH - 1:
                h1 = h - 2
                b.wait_snap(v, tp_done[(jq, h1)])
                r_exT[h1 % 2].wait_free(v)
                for dk in range(3):
                    b.inc(v.tensor_copy(
                        out=exT_sb[:, (h1 % 2) * 384 + dk * 128:
                                   (h1 % 2) * 384 + dk * 128 + 128],
                        in_=p_tp[dk][:]), "exT_v", 1)
                    g_tp[dk].consumed("exT_v")
                b.dinc(v, "exT_d", 1)
                exT_done[(jq, h1)] = b.snap("exT_d")
            if h <= NH - 1:
                hs = (h % 2) * 384
                b.wait_snap(v, qk_done[(jq, h)])
                r_sc[h % 2].wait_free(v)
                b.inc(v.scalar_tensor_tensor(out=sc_sb[:, hs:hs + 384],
                                             in0=p_sc[h % 2][:], scalar=SCALE,
                                             in1=msk, op0=ALU.mult, op1=ALU.add),
                      "msk_v", 1)
                g_sc[h % 2].consumed("msk_v")
                smx = sm_sb[:, (h % 2) * 8:(h % 2) * 8 + 8]
                r_sm[h % 2].wait_free(v)
                b.inc(v.reduce_max(smx[:, 1:2], sc_sb[:, hs:hs + 384], axis=AX.X,
                                   negate=True), "mx_v", 1)

        def attn_act_body(s, jq, h):
            if h <= NH - 1:
                hs = (h % 2) * 384
                smx = sm_sb[:, (h % 2) * 8:(h % 2) * 8 + 8]
                b.wait(s, "mx_v", b.val("mx_v"))
                r_exn[h % 2].wait_free(s)
                b.inc(s.activation(out=exn_sb[:, hs:hs + 384],
                                   in_=sc_sb[:, hs:hs + 384],
                                   func=AF.Exp, bias=smx[:, 1:2]), "exp_s", 1)
                exp_done[(jq, h)] = b.snap("exp_s")
                r_sc[h % 2].consumed("exp_s")
                r_sm[h % 2].consumed("exp_s")

        def attn_vec2_body(v, jq, h):
            if 0 <= h <= NH - 1:
                hs = (h % 2) * 384
                smx = sm_sb[:, (h % 2) * 8:(h % 2) * 8 + 8]
                b.wait(v, "exp_s", exp_done[(jq, h)][1])
                v.reduce_sum(smx[:, 2:3], exn_sb[:, hs:hs + 384], axis=AX.X)
                v.drain()
                v.reciprocal(smx[:, 3:4], smx[:, 2:3])
                v.drain()
                v.tensor_scalar_mul(exn_sb[:, hs:hs + 384], exn_sb[:, hs:hs + 384],
                                    smx[:, 3:4])
                b.dinc(v, "exn_v", 1)
                exn_done[(jq, h)] = b.snap("exn_v")

        for j in range(NT + 1):
            jq = j - 1
            sl = j % 2
            if j < NT:
                def _ld(s, j=j, sl=sl):
                    r_xs[sl].wait_free(s)
                    b.inc(s.dma_start(out=xs_sb[:, sl * C:(sl + 1) * C], in_=x_t[j]),
                          "x_in", 16)
                blk.sync(_ld)

                def _ln_v(v, j=j, sl=sl):
                    xv = xs_sb[:, sl * C:(sl + 1) * C]
                    st = st_sb[:, sl * 8:sl * 8 + 8]
                    b.wait(v, "x_in", 16 * (j + 1))
                    if j == 0:
                        b.wait(v, "c_ld")
                    v.reduce_sum(st[:, 0:1], xv, axis=AX.X)
                    v.drain()
                    v.tensor_scalar_mul(st[:, 2:3], st[:, 0:1], 1.0 / C)
                    b.dinc(v, "oi_v", 1)
                blk.vector(_ln_v)

                def _ln_s(v, j=j, sl=sl):
                    xv = xs_sb[:, sl * C:(sl + 1) * C]
                    st = st_sb[:, sl * 8:sl * 8 + 8]
                    b.wait(v, "x_in", 16 * (j + 1))
                    r_sq[sl].wait_free(v)
                    b.inc(v.scalar_tensor_tensor(out=sq_sb[:, sl * C:(sl + 1) * C],
                                                 in0=xv, scalar=1.0, in1=xv,
                                                 op0=ALU.mult, op1=ALU.mult,
                                                 accum_out=st[:, 1:2]), "ln_s1", 1)
                blk.vector(_ln_s)

                def _ln_v2(v, j=j, sl=sl):
                    st = st_sb[:, sl * 8:sl * 8 + 8]
                    b.wait(v, "ln_s1", b.val("ln_s1"))
                    v.tensor_mul(st[:, 3:4], st[:, 2:3], st[:, 2:3])
                    v.drain()
                    v.scalar_tensor_tensor(out=st[:, 4:5], in0=st[:, 1:2],
                                           scalar=1.0 / C, in1=st[:, 3:4],
                                           op0=ALU.mult, op1=ALU.subtract)
                    b.dinc(v, "ln_v2", 1)
                blk.vector(_ln_v2)

                def _ln_s2(s, j=j, sl=sl):
                    st = st_sb[:, sl * 8:sl * 8 + 8]
                    b.wait(s, "ln_v2", b.val("ln_v2"))
                    if j == 0:
                        b.wait(s, "cst_v")
                    s.activation(out=st[:, 5:6], in_=st[:, 4:5], func=AF.Ln,
                                 bias=cst_sb[:, 0:1])
                    s.drain()
                    b.inc(s.activation(out=st[:, 6:7], in_=st[:, 5:6], func=AF.Exp,
                                       scale=cst_sb[:, 1:2]), "ln_s2", 1)
                blk.scalar(_ln_s2)

                def _ln_v3(v, j=j, sl=sl):
                    st = st_sb[:, sl * 8:sl * 8 + 8]
                    b.wait(v, "ln_s2", b.val("ln_s2"))
                    v.scalar_tensor_tensor(out=st[:, 7:8], in0=st[:, 2:3],
                                           scalar=-1.0, in1=st[:, 6:7],
                                           op0=ALU.mult, op1=ALU.mult)
                    b.dinc(v, "ln_v3", 1)
                blk.vector(_ln_v3)

                def _xn(v, j=j, sl=sl):
                    xv = xs_sb[:, sl * C:(sl + 1) * C]
                    st = st_sb[:, sl * 8:sl * 8 + 8]
                    b.wait(v, "ln_s2", b.val("ln_s2"))
                    v.scalar_tensor_tensor(out=sq_sb[:, sl * C:(sl + 1) * C],
                                           in0=xv, scalar=st[:, 6:7],
                                           in1=st[:, 7:8].to_broadcast([128, C]),
                                           op0=ALU.mult, op1=ALU.add)
                    v.tensor_copy(out=xn_all[:, j * C:(j + 1) * C],
                                  in_=sq_sb[:, sl * C:(sl + 1) * C])
                    b.dinc(v, "xn_s", 1)
                blk.vector(_xn)
                r_xs[sl].consumed("xn_s")
                r_xs[sl].consumed("oi_v")

                def _oinit(s, j=j, sl=sl):
                    b.wait(s, "oi_v", b.val("oi_v"))
                    b.inc(s.dma_start(out=out_t[j], in_=xs_sb[:, sl * C:(sl + 1) * C]),
                          "o_init", 16)
                blk.sync(_oinit)
                r_xs[sl].consumed("o_init")

                def _tp(t, j=j, sl=sl):
                    b.wait(t, "xn_s", b.val("xn_s"))
                    r_xf[sl].wait_free(t)
                    base = b.val("tpe_v")
                    for cc in range(NCC):
                        if cc < 3:
                            g_tp[cc].wait_free(t)
                        else:
                            b.wait(t, "tpe_v", base + 2 * (cc - 2))
                        b.inc(t.transpose(p_tp[cc % 3][:],
                                          sq_sb[:, sl * C + cc * 128: sl * C + cc * 128 + 128],
                                          idf_sb[:]), "tp_pe", 1)
                blk.tensor(_tp)
                r_sq[sl].consumed("tp_pe")

                def _tpe(v, j=j, sl=sl):
                    base = b.val("tp_pe") - NCC
                    for cc in range(NCC):
                        b.wait(v, "tp_pe", base + cc + 1)
                        b.inc(v.tensor_copy(
                            out=xf_sb[:, sl * C + cc * 128: sl * C + cc * 128 + 128],
                            in_=p_tp[cc % 3][:]), "tpe_v", 1)
                        b.inc(v.tensor_copy(
                            out=xf32_sb[:, cc * 128: cc * 128 + 128],
                            in_=p_tp[cc % 3][:]), "tpe_v", 1)
                        g_tp[cc % 3].consumed("tpe_v")
                    b.dinc(v, "tpe_d", 1)
                blk.vector(_tpe)

                def _qkv(t, j=j, sl=sl):
                    if j == 0:
                        b.wait(t, "aw_ld")
                    b.wait(t, "tpe_d", b.val("tpe_d"))
                    b.wait(t, "woe_v", b.val("woe_v"))
                    xf = xf_sb[:, sl * C:(sl + 1) * C]
                    base = b.val("qkv_v")
                    uses = [("q", ec) for ec in range(NCC)] + \
                           [("k", ec) for ec in range(NCC)] + \
                           [("v", 0), ("v", 1), ("lg", 0)]
                    prev_use = {}
                    for u, (kind, idx) in enumerate(uses):
                        bank = idx % 2 if kind != "lg" else 0
                        if bank in prev_use:
                            b.wait(t, "qkv_v", base + prev_use[bank] + 1)
                        else:
                            g_mm[bank].wait_free(t)
                        prev_use[bank] = u
                        if kind == "q" or kind == "k":
                            wsb = wq_sb if kind == "q" else wk_sb
                            for cc in range(NCC):
                                b.inc(t.matmul(p_mm[bank][:, 0:128],
                                               wsb[:, cc * C + idx * 128: cc * C + idx * 128 + 128],
                                               xf[:, cc * 128:(cc + 1) * 128],
                                               start=(cc == 0), stop=(cc == NCC - 1)),
                                      "mm_pe", 1)
                        elif kind == "v":
                            for cc in range(NCC):
                                b.inc(t.matmul(p_mm[bank][:, 0:512],
                                               xf[:, cc * 128:(cc + 1) * 128],
                                               wv_sb[:, cc * C + idx * 512: cc * C + idx * 512 + 512],
                                               start=(cc == 0), stop=(cc == NCC - 1)),
                                      "mm_pe", 1)
                        else:
                            for cc in range(NCC):
                                b.inc(t.matmul(p_mm[bank][0:128, 0:E],
                                               xf32_sb[:, cc * 128:(cc + 1) * 128],
                                               rw32_sb[:, cc * E:(cc + 1) * E],
                                               start=(cc == 0), stop=(cc == NCC - 1)),
                                      "mm_pe", 1)
                        evac_q.append((j, kind, idx, bank, b.snap("mm_pe")))
                blk.tensor(_qkv)
                r_xf[sl].consumed("mm_pe")

                def _qkve(v, j=j, sl=sl):
                    while evac_q:
                        jj, kind, idx, bank, sn = evac_q.pop(0)
                        b.wait_snap(v, sn)
                        if kind == "q":
                            if idx == 0:
                                r_q[jj % 2].wait_free(v)
                            b.inc(v.tensor_copy(
                                out=q_sb[:, (jj % 2) * C + idx * 128:(jj % 2) * C + idx * 128 + 128],
                                in_=p_mm[bank][:, 0:128]), "qkv_v", 1)
                        elif kind == "k":
                            b.inc(v.tensor_copy(
                                out=k_sb[:, (jj % 3) * C + idx * 128:(jj % 3) * C + idx * 128 + 128],
                                in_=p_mm[bank][:, 0:128]), "qkv_v", 1)
                        elif kind == "v":
                            b.inc(v.tensor_copy(
                                out=v_sb[:, (jj % 3) * C + idx * 512:(jj % 3) * C + idx * 512 + 512],
                                in_=p_mm[bank][:, 0:512]), "qkv_v", 1)
                        else:
                            v.tensor_copy(out=ltok_all[:, jj * E:(jj + 1) * E],
                                          in_=p_mm[bank][0:128, 0:E])
                            b.dinc(v, "lgv", 1)
                        g_mm[bank].consumed("qkv_v" if kind != "lg" else "lgv")
                    b.dinc(v, "qkv_d", 1)
                blk.vector(_qkve)


            if 0 <= jq <= NT - 1:
                mtile = 0 if jq == 0 else (2 if jq == NT - 1 else 1)
                msk = msk_sb[:, mtile * 384:(mtile + 1) * 384]
                for h in range(NH + 4):
                    def _a_pe(t, jq=jq, h=h):
                        if h == 0:
                            b.wait(t, "qkv_d", b.val("qkv_d"))
                            if jq == 0:
                                b.wait(t, "z_done")
                        attn_pe_body(t, jq, h)
                    blk.tensor(_a_pe)

                    def _a_ve(v, jq=jq, h=h, msk=msk):
                        attn_vec_body(v, jq, h, msk)
                    blk.vector(_a_ve)

                    def _a_sc(s, jq=jq, h=h):
                        attn_act_body(s, jq, h)
                    blk.scalar(_a_sc)

                    def _a_ve2(v, jq=jq, h=h):
                        attn_vec2_body(v, jq, h - 1)
                    blk.vector(_a_ve2)
                r_q[jq % 2].consumed("qk_pe")

                def _wo(t, jq=jq):
                    b.wait(t, "ave_v", b.val("ave_v"))
                    for half in range(2):
                        g_mm[half].wait_free(t)
                        for fc in range(NCC):
                            b.inc(t.matmul(p_mm[half][:, 0:512],
                                           ao_sb[:, (jq % 2) * C + fc * 128:(jq % 2) * C + fc * 128 + 128],
                                           wo_sb[:, fc * C + half * 512: fc * C + half * 512 + 512],
                                           start=(fc == 0), stop=(fc == NCC - 1)),
                                  "wo_pe", 1)
                        wo_evac_q.append((jq, half, b.snap("wo_pe")))
                blk.tensor(_wo)
                r_ao[jq % 2].consumed("wo_pe")

                def _woe(v, jq=jq):
                    while wo_evac_q:
                        jj, half, sn = wo_evac_q.pop(0)
                        b.wait_snap(v, sn)
                        if half == 0:
                            r_wout[jj % 2].wait_free(v)
                        b.inc(v.tensor_tensor(
                            out=wout_sb[:, (jj % 2) * C + half * 512:(jj % 2) * C + half * 512 + 512],
                            in0=p_mm[half][:, 0:512],
                            in1=bo_sb[:, half * 512: half * 512 + 512],
                            op=ALU.add), "woe_v", 1)
                        g_mm[half].consumed("woe_v")
                    b.dinc(v, "woe_d", 1)
                blk.vector(_woe)

                def _oacc(g, jq=jq):
                    b.wait(g, "woe_d", b.val("woe_d"))
                    b.wait(g, "o_init", 16 * (jq + 1))
                    b.inc(g.dma_start(out=out_t[jq],
                                      in_=wout_sb[:, (jq % 2) * C:(jq % 2) * C + C],
                                      accum_op=ALU.add), "o_acc", 16)
                blk.gpsimd(_oacc)
                r_wout[jq % 2].consumed("o_acc")

        if dump:
            def _dump_a(s):
                b.wait(s, "xn_s", b.val("xn_s"))
                for j in range(NT):
                    b.inc(s.dma_start(
                        out=xn_dump.rearrange("(n p) c -> n p c", p=128)[j],
                        in_=xn_all[:, j * C:(j + 1) * C]), "dmp", 16)
                b.wait(s, "lgv", b.val("lgv"))
                b.inc(s.dma_start(out=lg_dump, in_=ltok_all[:]), "dmp", 16)
            blk.sync(_dump_a)

    g_bt = [Res(b) for _ in range(2)]
    if "B" in phases:
        stA2.close()
        stB = ExitStack()

        def SB(name, shape, dtype):
            return stB.enter_context(nc.sbuf_tensor(name, shape, dtype))

        m_em = SB("m_em", [E, 2 * T], F32)
        cnt_em = SB("cnt_em", [E, 2 * T], F32)
        smB_sb = SB("smB_sb", [128, 16], F32)
        oh_sb = SB("oh_sb", [128, 2 * E], F32)
        ltok_sb = SB("ltok_sb", [128, E], F32)
        top8_sb = SB("top8_sb", [128, E], F32)
        mi_sb = SB("mi_sb", [128, E], mybir.dt.uint32)
        ctok_sb = SB("ctok_sb", [128, 2 * E], F32)
        tmp8_sb = SB("tmp8_sb", [128, 8], F32)
        zcol_sb = SB("zcol_sb", [E, 1], F32)

        with nc.Block() as blk:
            for j in range(NT):
                def _b_v(v, j=j):
                    b.inc(v.tensor_add(ltok_sb[:], ltok_all[:, j * E:(j + 1) * E],
                                       rb_sb[:]), "bv", 1)
                    v.drain()
                    v.reduce_max(smB_sb[:, 8:9], ltok_sb[:], axis=AX.X)
                    v.drain()
                    v.tensor_scalar(out=oh_sb[:, 0:E], in0=ltok_sb[:],
                                    scalar1=smB_sb[:, 8:9], scalar2=None,
                                    op0=ALU.is_ge)
                    v.drain()
                    v.tensor_mul(tmp8_sb[:], oh_sb[:, 0:E], io8_sb[:])
                    v.drain()
                    v.reduce_sum(i01_sb[:, 2 * j:2 * j + 1], tmp8_sb[:], axis=AX.X)
                    v.scalar_tensor_tensor(out=top8_sb[:], in0=oh_sb[:, 0:E],
                                           scalar=-DROP_BIG, in1=ltok_sb[:],
                                           op0=ALU.mult, op1=ALU.add)
                    v.drain()
                    v.reduce_max(smB_sb[:, 9:10], top8_sb[:], axis=AX.X)
                    v.drain()
                    v.tensor_scalar(out=oh_sb[:, E:2 * E], in0=top8_sb[:],
                                    scalar1=smB_sb[:, 9:10], scalar2=None,
                                    op0=ALU.is_ge)
                    v.drain()
                    v.tensor_mul(tmp8_sb[:], oh_sb[:, E:2 * E], io8_sb[:])
                    v.drain()
                    v.reduce_sum(i01_sb[:, 2 * j + 1:2 * j + 2], tmp8_sb[:], axis=AX.X)
                    v.tensor_sub(smB_sb[:, 0:1], smB_sb[:, 9:10], smB_sb[:, 8:9])
                    b.dinc(v, "bv", 1)
                blk.vector(_b_v)

                def _b_s(s, j=j):
                    b.wait(s, "bv", b.val("bv"))
                    b.inc(s.activation(out=smB_sb[:, 2:3], in_=smB_sb[:, 0:1],
                                       func=AF.Exp), "bs", 1)
                blk.scalar(_b_s)

                def _b_v2(v, j=j):
                    b.wait(v, "bs", b.val("bs"))
                    v.tensor_scalar_add(smB_sb[:, 3:4], smB_sb[:, 2:3], 1.0)
                    v.drain()
                    v.reciprocal(prob_sb[:, 2 * j:2 * j + 1], smB_sb[:, 3:4])
                    v.drain()
                    v.tensor_scalar(out=prob_sb[:, 2 * j + 1:2 * j + 2],
                                    in0=prob_sb[:, 2 * j:2 * j + 1],
                                    scalar1=-1.0, scalar2=1.0,
                                    op0=ALU.mult, op1=ALU.add)
                    for kk in range(2):
                        v.tensor_scalar(out=oh_sb[:, kk * E:(kk + 1) * E],
                                        in0=io8_sb[:],
                                        scalar1=i01_sb[:, 2 * j + kk:2 * j + kk + 1],
                                        scalar2=None, op0=ALU.is_equal)
                    b.dinc(v, "bv2", 1)
                blk.vector(_b_v2)

                def _b_tp2(t, j=j):
                    b.wait(t, "bv2", b.val("bv2"))
                    for kk in range(2):
                        g_bt[kk].wait_free(t)
                        t.transpose(p_mm[kk][0:8, 0:128],
                                    oh_sb[:, kk * E:(kk + 1) * E], idf_sb[:])
                    b.dinc(t, "btp2_pe", 2)
                blk.tensor(_b_tp2)

                def _b_v3(v, j=j):
                    b.wait(v, "btp2_pe", b.val("btp2_pe"))
                    for kk in range(2):
                        v.tensor_copy(
                            out=m_em[:, kk * T + j * 128: kk * T + j * 128 + 128],
                            in_=p_mm[kk][0:8, 0:128])
                    b.dinc(v, "bv3", 1)
                    for kk in range(2):
                        g_bt[kk].consumed("bv3")
                blk.vector(_b_v3)

            def _scan(v):
                b.wait(v, "bv3", b.val("bv3"))
                v.memset(zcol_sb[:], 0.0)
                v.drain()
                v.tensor_tensor_scan(out=cnt_em[:], data0=m_em[:],
                                     data1=zcol_sb[:].to_broadcast([E, 2 * T]),
                                     initial=0.0, op0=ALU.add, op1=ALU.add)
                b.dinc(v, "scan_v", 1)
            blk.vector(_scan)

            for j in range(NT):
                def _c_tp(t, j=j):
                    if j == 0:
                        b.wait(t, "scan_v", b.val("scan_v"))
                    for kk in range(2):
                        g_bt[kk].wait_free(t)
                        b.inc(t.transpose(p_mm[kk][0:128, 0:8],
                                          cnt_em[:, kk * T + j * 128: kk * T + j * 128 + 128],
                                          idf_sb[0:8, 0:8]), "ctp_pe", 1)
                blk.tensor(_c_tp)

                def _c_v(v, j=j):
                    b.wait(v, "ctp_pe", b.val("ctp_pe"))
                    for kk in range(2):
                        b.inc(v.tensor_copy(out=ctok_sb[:, kk * E:(kk + 1) * E],
                                            in_=p_mm[kk][0:128, 0:8]), "cv", 1)
                        g_bt[kk].consumed("cv")
                    v.drain()
                    for kk in range(2):
                        ctok = ctok_sb[:, kk * E:(kk + 1) * E]
                        v.tensor_scalar(out=oh_sb[:, kk * E:(kk + 1) * E], in0=io8_sb[:],
                                        scalar1=i01_sb[:, 2 * j + kk:2 * j + kk + 1],
                                        scalar2=None, op0=ALU.is_equal)
                        v.drain()
                        v.tensor_mul(tmp8_sb[:], oh_sb[:, kk * E:(kk + 1) * E], ctok)
                        v.drain()
                        v.reduce_sum(smB_sb[:, 0:1], tmp8_sb[:], axis=AX.X)
                        v.drain()
                        v.tensor_scalar_add(smB_sb[:, 1:2], smB_sb[:, 0:1], -1.0)
                        v.drain()
                        v.scalar_tensor_tensor(out=smB_sb[:, 2:3],
                                               in0=i01_sb[:, 2 * j + kk:2 * j + kk + 1],
                                               scalar=float(CAPP), in1=smB_sb[:, 1:2],
                                               op0=ALU.mult, op1=ALU.add)
                        v.tensor_scalar(out=smB_sb[:, 3:4], in0=smB_sb[:, 1:2],
                                        scalar1=float(CAP), scalar2=None, op0=ALU.is_lt)
                        v.drain()
                        v.tensor_mul(smB_sb[:, 4:5], smB_sb[:, 2:3], smB_sb[:, 3:4])
                        v.tensor_scalar(out=smB_sb[:, 7:8], in0=smB_sb[:, 3:4],
                                        scalar1=-float(NSLOT), scalar2=float(NSLOT),
                                        op0=ALU.mult, op1=ALU.add)
                        v.drain()
                        v.tensor_add(smB_sb[:, 4:5], smB_sb[:, 4:5], smB_sb[:, 7:8])
                        v.drain()
                        v.tensor_copy(out=sidx_sb[:, 2 * j + kk:2 * j + kk + 1],
                                      in_=smB_sb[:, 4:5])
                        v.tensor_scalar_min(smB_sb[:, 5:6], smB_sb[:, 1:2],
                                            float(CAP - 1))
                        v.drain()
                        v.scalar_tensor_tensor(out=smB_sb[:, 6:7],
                                               in0=i01_sb[:, 2 * j + kk:2 * j + kk + 1],
                                               scalar=float(CAPP), in1=smB_sb[:, 5:6],
                                               op0=ALU.mult, op1=ALU.add)
                        v.drain()
                        v.tensor_copy(out=gidx_sb[:, 2 * j + kk:2 * j + kk + 1],
                                      in_=smB_sb[:, 6:7])
                    b.dinc(v, "cv", 1)
                blk.vector(_c_v)

            if dump:
                def _dump_rt(s):
                    b.wait(s, "cv", b.val("cv"))
                    b.inc(s.dma_start(out=rt_dump[0], in_=i01_sb[:]), "dmp", 16)
                    b.inc(s.dma_start(out=rt_dump[1], in_=prob_sb[:]), "dmp", 16)
                    b.inc(s.dma_start(out=rt_dump[2], in_=sidx_sb[:].bitcast(F32)),
                          "dmp", 16)
                    b.inc(s.dma_start(out=rt_dump[3], in_=gidx_sb[:].bitcast(F32)),
                          "dmp", 16)
                blk.sync(_dump_rt)
        stB.close()
    else:
        stA2.close()

    if "C" in phases:
        with nc.Block() as blk:
            def _scat(g):
                b.wait(g, "zi", b.val("zi"))
                for j in range(NT):
                    for kk in range(2):
                        b.inc(g.indirect_dma_start(
                            out=einp_d,
                            out_offset=IndirectOffsetOnAxis(
                                sidx_sb[:, 2 * j + kk:2 * j + kk + 1], 0),
                            in_=xn_all[:, j * C:(j + 1) * C],
                            in_offset=None), "cs", 16)
                g.wait_ge(b.s("cs")[0], b.val("cs"))
            blk.gpsimd(_scat)

    stPA.close()
    stA.close()

    if "D" in phases:
        stD = ExitStack()

        def SD(name, shape, dtype):
            return stD.enter_context(nc.sbuf_tensor(name, shape, dtype))

        a_fm = SD("a_fm", [128, NJ * CAPP], BF16)
        einp_fm = SD("einp_fm", [128, NCC * CAPP], BF16)
        eit_sb = SD("eit_sb", [128, 2 * C], BF16)
        w2_sb = SD("w2_sb", [128, NJ * C], BF16)
        w1_sl = SD("w1_sl", [128, 2 * 1024], BF16)
        wg_sl = SD("wg_sl", [128, 2 * 1024], BF16)
        tg_sb = SD("tg_sb", [128, 512], F32)
        m_sb2 = SD("m_sb2", [128, 512], F32)
        eo_st = SD("eo_st", [128, E * 512], F32)
        eo_sl = SD("eo_sl", [128, C], F32)
        b1_sb = SD("b1_sb", [128, NJ], F32)
        bg_sb = SD("bg_sb", [128, NJ], F32)
        b2_sb = SD("b2_sb", [128, E], F32)

        stPD = ExitStack()

        def PDp(name, shape, dtype):
            return stPD.enter_context(nc.psum_tensor(name, shape, dtype))

        p_h = PDp("p_h", [128, 512], F32)
        p_g = PDp("p_g", [128, 512], F32)
        p_o = [PDp(f"p_o{i}", [128, 512], F32) for i in range(2)]
        p_dt = [PDp(f"p_dt{i}", [128, 128], BF16) for i in range(2)]
        p_et = PDp("p_et", [128, 128], F32)

        g_h = Res(b)
        g_g = Res(b)
        g_o = [Res(b) for _ in range(2)]
        r_eit = [Res(b) for _ in range(2)]
        r_w1 = [Res(b) for _ in range(2)]
        r_wg = [Res(b) for _ in range(2)]
        r_tg = [Res(b) for _ in range(2)]
        r_m = [Res(b) for _ in range(2)]
        r_eosl = Res(b)
        g_w2 = Res(b)
        g_afm = Res(b)
        g_eost = Res(b)
        g_bias = Res(b)

        w1_arr = {}
        w2_arr = {}
        p1_q = []
        tg_q = []
        m_q = []
        p2_q = []

        with nc.Block() as blk:
            for e in range(E):
                def _bias(s, e=e):
                    if e == 0 and "C" in phases:
                        b.wait(s, "cs", b.val("cs"))
                    g_bias.wait_free(s)
                    b.inc(s.dma_start(out=b1_sb[:], in_=b1_d[e]), "bia", 16)
                    b.inc(s.dma_start(out=bg_sb[:], in_=bg_d[e]), "bia", 16)
                    b.inc(s.dma_start(out=b2_sb[:], in_=b2_d[e]), "bia", 16)
                blk.sync(_bias)
                bia_now = b.snap("bia")

                for st in range(NST):
                    def _slab_ld(s, e=e, st=st):
                        r_eit[st % 2].wait_free(s)
                        b.inc(s.dma_start(out=eit_sb[:, (st % 2) * C:(st % 2) * C + C],
                                          in_=einp_t[e * NST + st]), "slab", 16)
                    blk.sync(_slab_ld)

                    def _slab_tp(t, e=e, st=st):
                        b.wait(t, "slab", b.val("slab"))
                        base = b.val("dte_v")
                        for cc in range(NCC):
                            if base + cc - 1 > 0:
                                b.wait(t, "dte_v", base + cc - 1)
                            b.inc(t.transpose(p_dt[cc % 2][:],
                                              eit_sb[:, (st % 2) * C + cc * 128:
                                                     (st % 2) * C + cc * 128 + 128],
                                              idb_sb[:]), "dtp_pe", 1)
                    blk.tensor(_slab_tp)

                    def _slab_ev(v, e=e, st=st):
                        base = b.val("dtp_pe") - NCC
                        for cc in range(NCC):
                            b.wait(v, "dtp_pe", base + cc + 1)
                            b.inc(v.tensor_copy(
                                out=einp_fm[:, cc * CAPP + st * 128: cc * CAPP + st * 128 + 128],
                                in_=p_dt[cc % 2][:]), "dte_v", 1)
                        b.dinc(v, "dte_d", 1)
                    blk.vector(_slab_ev)
                    r_eit[st % 2].consumed("dtp_pe")
                slab_done = b.snap("dte_d")

                def _w_pre(g, e=e):
                    for jj in range(2):
                        r_w1[jj].wait_free(g)
                        b.inc(g.dma_start(out=w1_sl[:, jj * 1024:(jj + 1) * 1024],
                                          in_=w1_d[e, jj]), "w1d", 16)
                        w1_arr[(e, jj, 0)] = b.snap("w1d")
                        r_wg[jj].wait_free(g)
                        b.inc(g.dma_start(out=wg_sl[:, jj * 1024:(jj + 1) * 1024],
                                          in_=wg_d[e, jj]), "w1d", 16)
                        w1_arr[(e, jj, 1)] = b.snap("w1d")
                blk.gpsimd(_w_pre)

                for j in range(NJ):
                    def _p1_pe(t, e=e, j=j):
                        if j == 0:
                            b.wait_snap(t, slab_done)
                            g_afm.wait_free(t)
                        b.wait_snap(t, w1_arr[(e, j, 0)])
                        b.wait_snap(t, w1_arr[(e, j, 1)])
                        base_m = b.val("m_v")
                        base_tg = b.val("tg_s")
                        nbi = len(SLOT_BLOCKS)
                        for bi, (s0, w) in enumerate(SLOT_BLOCKS):
                            gi = (j * nbi + bi) % 2
                            ph = p_h if gi == 0 else p_o[0]
                            pg = p_g if gi == 0 else p_o[1]
                            if bi >= 2:
                                b.wait(t, "m_v", base_m + bi - 1)
                            else:
                                (g_h if gi == 0 else g_o[0]).wait_free(t)
                                (g_g if gi == 0 else g_o[1]).wait_free(t)
                            for cc in range(NCC):
                                b.inc(t.matmul(
                                    pg[:, 0:w],
                                    wg_sl[:, (j % 2) * 1024 + cc * 128:(j % 2) * 1024 + cc * 128 + 128],
                                    einp_fm[:, cc * CAPP + s0: cc * CAPP + s0 + w],
                                    start=(cc == 0), stop=(cc == NCC - 1)), "g_pe", 1)
                            g_now = b.snap("g_pe")
                            for cc in range(NCC):
                                b.inc(t.matmul(
                                    ph[:, 0:w],
                                    w1_sl[:, (j % 2) * 1024 + cc * 128:(j % 2) * 1024 + cc * 128 + 128],
                                    einp_fm[:, cc * CAPP + s0: cc * CAPP + s0 + w],
                                    start=(cc == 0), stop=(cc == NCC - 1)), "h_pe", 1)
                            p1_q.append((e, j, bi, s0, w, b.snap("h_pe"), g_now, gi))
                    blk.tensor(_p1_pe)

                    for bi in range(len(SLOT_BLOCKS)):
                        def _p1_sc(v, e=e, j=j, bi=bi):
                            for item in list(p1_q):
                                ee, jj, bb_, s0, w, hsn, gsn, gi = item
                                if (ee, jj, bb_) != (e, j, bi):
                                    continue
                                b.wait_snap(v, gsn)
                                if jj == 0 and bb_ == 0:
                                    b.wait_snap(v, bia_now)
                                r_tg[0].wait_free(v)
                                pg = p_g if gi == 0 else p_o[1]
                                v.tensor_scalar_add(tg_sb[:, 0:w], pg[:, 0:w],
                                                    bg_sb[:, jj:jj + 1])
                                b.dinc(v, "tg_s", 1)
                                (g_g if gi == 0 else g_o[1]).consumed("tg_s")
                                tg_q.append((ee, jj, bb_, s0, w, hsn, b.snap("tg_s"), gi))
                                p1_q.remove(item)
                        blk.vector(_p1_sc)

                        def _p1_v(v, e=e, j=j, bi=bi):
                            for item in list(tg_q):
                                ee, jj, bb_, s0, w, hsn, tsn, gi = item
                                if (ee, jj, bb_) != (e, j, bi):
                                    continue
                                b.wait_snap(v, hsn)
                                b.wait_snap(v, tsn)
                                r_m[0].wait_free(v)
                                ph = p_h if gi == 0 else p_o[0]
                                v.scalar_tensor_tensor(
                                    out=m_sb2[:, 0:w],
                                    in0=ph[:, 0:w], scalar=b1_sb[:, jj:jj + 1],
                                    in1=tg_sb[:, 0:w],
                                    op0=ALU.add, op1=ALU.mult)
                                b.dinc(v, "m_v", 1)
                                (g_h if gi == 0 else g_o[0]).consumed("m_v")
                                r_tg[0].consumed("m_v")
                                m_q.append((ee, jj, bb_, s0, w, b.snap("m_v")))
                                tg_q.remove(item)
                        blk.vector(_p1_v)

                        def _p1_si(s, e=e, j=j, bi=bi):
                            for item in list(m_q):
                                ee, jj, bb_, s0, w, msn = item
                                if (ee, jj, bb_) != (e, j, bi):
                                    continue
                                b.wait_snap(s, msn)
                                b.inc(s.activation(
                                    out=a_fm[:, jj * CAPP + s0: jj * CAPP + s0 + w],
                                    in_=m_sb2[:, 0:w],
                                    func=AF.Silu), "a_s", 1)
                                r_m[0].consumed("a_s")
                                m_q.remove(item)
                        blk.scalar(_p1_si)

                    r_w1[j % 2].consumed("h_pe")
                    r_wg[j % 2].consumed("g_pe")

                    def _w_next(g, e=e, j=j):
                        jn = j + 2
                        if jn < NJ:
                            r_w1[jn % 2].wait_free(g)
                            b.inc(g.dma_start(
                                out=w1_sl[:, (jn % 2) * 1024:(jn % 2) * 1024 + 1024],
                                in_=w1_d[e, jn]), "w1d", 16)
                            w1_arr[(e, jn, 0)] = b.snap("w1d")
                            r_wg[jn % 2].wait_free(g)
                            b.inc(g.dma_start(
                                out=wg_sl[:, (jn % 2) * 1024:(jn % 2) * 1024 + 1024],
                                in_=wg_d[e, jn]), "w1d", 16)
                            w1_arr[(e, jn, 1)] = b.snap("w1d")
                        if j == 0:
                            g_w2.wait_free(g)
                        b.inc(g.dma_start(out=w2_sb[:, j * C:(j + 1) * C],
                                          in_=w2_d[e, j * 128:(j + 1) * 128, :]),
                              "w2d", 16)
                        w2_arr[(e, j)] = b.snap("w2d")
                    blk.gpsimd(_w_next)

                if dump and e == 0:
                    def _dump_afm(s, e=e):
                        b.wait(s, "a_s", b.val("a_s"))
                        b.inc(s.dma_start(out=afm_dump, in_=a_fm[:]), "dmp", 16)
                    blk.sync(_dump_afm)
                for bi, (s0, w) in enumerate(SLOT_BLOCKS):
                    def _p2_pe(t, e=e, bi=bi, s0=s0, w=w):
                        if bi == 0:
                            b.wait(t, "a_s", b.val("a_s"))
                            b.wait_snap(t, w2_arr[(e, NJ - 1)])
                        base_oev = b.val("oev_v")
                        for oc in range(NCC):
                            if oc < 2:
                                g_o[oc % 2].wait_free(t)
                            else:
                                b.wait(t, "oev_v", base_oev + oc - 1)
                            for jj in range(NJ):
                                b.inc(t.matmul(
                                    p_o[oc % 2][:, 0:w],
                                    w2_sb[:, jj * C + oc * 128: jj * C + oc * 128 + 128],
                                    a_fm[:, jj * CAPP + s0: jj * CAPP + s0 + w],
                                    start=(jj == 0), stop=(jj == NJ - 1)), "o_pe", 1)
                            p2_q.append((e, bi, oc, w, b.snap("o_pe")))
                    blk.tensor(_p2_pe)

                    def _p2_ev(v, e=e, bi=bi):
                        while p2_q:
                            ee, bb_, oc, w_, sn = p2_q.pop(0)
                            b.wait_snap(v, sn)
                            if oc == 0:
                                g_eost.wait_free(v)
                                b.wait(v, "eow", b.val("eow"))
                            v.tensor_scalar_add(eo_st[:, oc * 512:oc * 512 + w_],
                                                p_o[oc % 2][:, 0:w_],
                                                b2_sb[:, oc:oc + 1])
                            b.dinc(v, "oev_v", 1)
                            g_o[oc % 2].consumed("oev_v")
                    blk.vector(_p2_ev)

                    if dump and e == 0 and bi == 0:
                        def _dump_eost(s, e=e):
                            b.wait(s, "oev_v", b.val("oev_v"))
                            b.inc(s.dma_start(out=eost_dump, in_=eo_st[:]), "dmp", 16)
                        blk.sync(_dump_eost)
                    nst_b = w // 128
                    for st in range(nst_b):
                        def _eo_tp(t, e=e, bi=bi, s0=s0, st=st):
                            b.wait(t, "oev_v", b.val("oev_v"))
                            base = b.val("ete_v")
                            for oc in range(NCC):
                                if base + oc > 0:
                                    b.wait(t, "ete_v", base + oc)
                                b.inc(t.transpose(
                                    p_et[:],
                                    eo_st[:, oc * 512 + st * 128: oc * 512 + st * 128 + 128],
                                    idf_sb[:]), "etp_pe", 1)
                        blk.tensor(_eo_tp)

                        def _eo_ev(v, e=e, bi=bi, s0=s0, st=st):
                            r_eosl.wait_free(v)
                            base = b.val("etp_pe") - NCC
                            for oc in range(NCC):
                                b.wait(v, "etp_pe", base + oc + 1)
                                b.inc(v.tensor_copy(out=eo_sl[:, oc * 128:(oc + 1) * 128],
                                                    in_=p_et[:]), "ete_v", 1)
                            b.dinc(v, "ete_d", 1)
                        blk.vector(_eo_ev)

                        def _eo_dma(s, e=e, bi=bi, s0=s0, st=st):
                            b.wait(s, "ete_d", b.val("ete_d"))
                            b.inc(s.dma_start(out=eo_t[e * NST + s0 // 128 + st],
                                              in_=eo_sl[:]), "eow", 16)
                        blk.sync(_eo_dma)
                        r_eosl.consumed("eow")
                    g_eost.consumed("etp_pe")

                g_afm.consumed("o_pe")
                g_w2.consumed("o_pe")
                g_bias.consumed("m_v")
                g_bias.consumed("tg_s")
                g_bias.consumed("oev_v")

        stPD.close()
        stD.close()

    if "E" in phases:
        stE = ExitStack()

        def SE(name, shape, dtype):
            return stE.enter_context(nc.sbuf_tensor(name, shape, dtype))

        g0_sb = SE("g0_sb", [128, 2 * C], F32)
        g1_sb = SE("g1_sb", [128, 2 * C], F32)
        rr_sb = SE("rr_sb", [128, 2 * C], F32)
        fo_sb = SE("fo_sb", [128, 2 * C], F32)
        r_g0 = [Res(b) for _ in range(2)]
        r_g1 = [Res(b) for _ in range(2)]
        r_rr = [Res(b) for _ in range(2)]
        r_fo = [Res(b) for _ in range(2)]

        with nc.Block() as blk:
            for j in range(NT):
                sl = j % 2

                def _e_rd(s, j=j, sl=sl):
                    if j == 0:
                        b.wait(s, "o_acc", b.val("o_acc"))
                    r_rr[sl].wait_free(s)
                    b.inc(s.dma_start(out=rr_sb[:, sl * C:(sl + 1) * C], in_=out_t[j]),
                          "rrd", 16)
                blk.sync(_e_rd)

                def _e_g(g, j=j, sl=sl):
                    if j == 0:
                        b.wait(g, "eow", b.val("eow"))
                    r_g0[sl].wait_free(g)
                    b.inc(g.indirect_dma_start(
                        out=g0_sb[:, sl * C:(sl + 1) * C], out_offset=None,
                        in_=eo_d,
                        in_offset=IndirectOffsetOnAxis(gidx_sb[:, 2 * j:2 * j + 1], 0)),
                          "gat", 16)
                    r_g1[sl].wait_free(g)
                    b.inc(g.indirect_dma_start(
                        out=g1_sb[:, sl * C:(sl + 1) * C], out_offset=None,
                        in_=eo_d,
                        in_offset=IndirectOffsetOnAxis(gidx_sb[:, 2 * j + 1:2 * j + 2], 0)),
                          "gat", 16)
                blk.gpsimd(_e_g)

                def _e_v(v, j=j, sl=sl):
                    b.wait(v, "gat", b.val("gat"))
                    b.wait(v, "rrd", b.val("rrd"))
                    r_fo[sl].wait_free(v)
                    v.tensor_scalar_mul(fo_sb[:, sl * C:(sl + 1) * C],
                                        g0_sb[:, sl * C:(sl + 1) * C],
                                        prob_sb[:, 2 * j:2 * j + 1])
                    v.drain()
                    v.scalar_tensor_tensor(out=fo_sb[:, sl * C:(sl + 1) * C],
                                           in0=g1_sb[:, sl * C:(sl + 1) * C],
                                           scalar=prob_sb[:, 2 * j + 1:2 * j + 2],
                                           in1=fo_sb[:, sl * C:(sl + 1) * C],
                                           op0=ALU.mult, op1=ALU.add)
                    v.drain()
                    v.tensor_add(fo_sb[:, sl * C:(sl + 1) * C],
                                 fo_sb[:, sl * C:(sl + 1) * C],
                                 rr_sb[:, sl * C:(sl + 1) * C])
                    b.dinc(v, "cmb_v", 1)
                    r_g0[sl].consumed("cmb_v")
                    r_g1[sl].consumed("cmb_v")
                    r_rr[sl].consumed("cmb_v")
                blk.vector(_e_v)

                def _e_wr(s, j=j, sl=sl):
                    b.wait(s, "cmb_v", b.val("cmb_v"))
                    b.inc(s.dma_start(out=out_t[j], in_=fo_sb[:, sl * C:(sl + 1) * C]),
                          "fo", 16)
                blk.sync(_e_wr)
                r_fo[sl].consumed("fo")

        stE.close()

    with nc.Block() as blk:
        def _wait_all(s):
            for name in ("x_in", "o_init", "o_acc", "zi", "cs", "slab", "w1d",
                         "w2d", "bia", "eow", "rrd", "gat", "fo", "aw_ld",
                         "c_ld", "dmp"):
                if name in b.sems:
                    b.wait(s, name, b.val(name))
        blk.sync(_wait_all)

    return nc


def _prep_inputs(inputs):
    f32 = np.float32
    x = np.ascontiguousarray(np.asarray(inputs["L0_attn_x"], dtype=f32))
    wq = np.asarray(inputs["L0_attn_Wq"], dtype=f32)
    wk = np.asarray(inputs["L0_attn_Wk"], dtype=f32)
    wv = np.asarray(inputs["L0_attn_Wv"], dtype=f32)
    wo = np.asarray(inputs["L0_attn_Wo"], dtype=f32)
    bo = np.asarray(inputs["L0_attn_bo"], dtype=f32)
    rw = np.ascontiguousarray(np.asarray(inputs["L0_ffn_router_w"], dtype=f32))
    rb = np.asarray(inputs["L0_ffn_router_b"], dtype=f32)
    w1 = np.asarray(inputs["L0_ffn_w_c_fc"], dtype=f32)
    b1 = np.asarray(inputs["L0_ffn_b_c_fc"], dtype=f32)
    wg = np.asarray(inputs["L0_ffn_w_gate"], dtype=f32)
    bg = np.asarray(inputs["L0_ffn_b_gate"], dtype=f32)
    w2 = np.ascontiguousarray(np.asarray(inputs["L0_ffn_w_c_proj"], dtype=f32))
    b2 = np.asarray(inputs["L0_ffn_b_c_proj"], dtype=f32)

    com = {
        "wq": np.ascontiguousarray(wq.T), "wk": np.ascontiguousarray(wk.T),
        "wv": np.ascontiguousarray(wv.T), "wo": np.ascontiguousarray(wo.T),
        "rw": rw,
        "bo_b": np.ascontiguousarray(np.broadcast_to(bo, (128, C))),
        "rb_b": np.ascontiguousarray(np.broadcast_to(rb, (128, E))),
        "w1r": np.ascontiguousarray(
            w1.reshape(E, NCC, 128, NJ, 128).transpose(0, 3, 2, 1, 4)
            .reshape(E, NJ, 128, 1024)),
        "wgr": np.ascontiguousarray(
            wg.reshape(E, NCC, 128, NJ, 128).transpose(0, 3, 2, 1, 4)
            .reshape(E, NJ, 128, 1024)),
        "w2": w2,
        "b1c": np.ascontiguousarray(b1.reshape(E, NJ, 128).transpose(0, 2, 1)),
        "bgc": np.ascontiguousarray(bg.reshape(E, NJ, 128).transpose(0, 2, 1)),
        "b2c": np.ascontiguousarray(b2.reshape(E, NCC, 128).transpose(0, 2, 1)),
        "ident_bf": np.eye(128, dtype=ml_dtypes.bfloat16),
        "ident_f32": np.eye(128, dtype=f32),
        "masks": _mk_masks(),
        "iota8_b": np.ascontiguousarray(
            np.broadcast_to(np.arange(E, dtype=f32), (128, E))),
    }
    return x, com


def kernel(**inputs):
    x, com = _prep_inputs(inputs)
    nc = build_nc()
    in_maps = []
    for i in range(8):
        m = dict(com)
        m["x"] = np.ascontiguousarray(x[i])
        in_maps.append(m)
    res = run_bass_kernel_spmd(nc, in_maps, core_ids=list(range(8)))
    out = np.stack([res.results[i]["out"] for i in range(8)], axis=0)
    return out.astype(np.float32)


# version 07 (final)
# speedup vs baseline: 1.4393x; 1.0221x over previous
import numpy as np
import ml_dtypes
import concourse.bass as bass
import concourse.mybir as mybir
from concourse.bass import IndirectOffsetOnAxis
from concourse.bass_utils import run_bass_kernel_spmd

F32 = mybir.dt.float32
BF16 = mybir.dt.bfloat16
I32 = mybir.dt.int32
AF = mybir.ActivationFunctionType
ALU = mybir.AluOpType
AX = mybir.AxisListType

T = 4096
NT = 32
C = 1024
NCC = 8
NH = 16
HID = 4096
NJ = 32
E = 8
CAP = 1228
CAPP = 1280
NST = 10
NSLOT = E * CAPP
SCALE = 0.125
NEG = -30000.0
DROP_BIG = float(1 << 20)
LN_EPS = 1e-5
SLOT_BLOCKS = [(0, 512), (512, 512), (1024, 256)]


def _mk_masks():
    m = np.full((3, 128, 384), NEG, np.float32)
    for p in range(128):
        lo, hi = p + 96, p + 159
        m[1, p, lo:hi + 1] = 0.0
        m[0, p, max(lo, 128):hi + 1] = 0.0
        if p <= 31:
            m[0, p, 128] = np.log(33.0 - p)
        m[2, p, lo:min(hi, 255) + 1] = 0.0
        if p >= 97:
            m[2, p, 255] = np.log(p - 95.0)
    return m


def _mk_lseg():
    L = np.zeros((128, 128), np.float32)
    for pp in range(128):
        for p in range(128):
            if pp % 8 == p % 8 and pp // 8 < p // 8:
                L[pp, p] = 1.0
    return L


class Builder:
    def __init__(self, nc):
        self.nc = nc
        self.sems = {}

    def s(self, name):
        if name not in self.sems:
            self.sems[name] = [self.nc.alloc_semaphore(name), 0]
        return self.sems[name]

    def inc(self, inst, name, k=1):
        h = self.s(name)
        inst.then_inc(h[0], k)
        h[1] += k

    def val(self, name):
        return self.s(name)[1]

    def wait(self, eng, name, v=None):
        h = self.s(name)
        eng.wait_ge(h[0], h[1] if v is None else v)

    def dinc(self, eng, name, k=1):
        eng.drain()
        self.inc(eng.nop(), name, k)

    def snap(self, name):
        return (name, self.s(name)[1])

    def wait_snap(self, eng, snap):
        if snap is not None:
            name, v = snap
            eng.wait_ge(self.s(name)[0], v)


class Res:
    def __init__(self, b):
        self.b = b
        self.snaps = []

    def wait_free(self, eng):
        for sn in self.snaps:
            self.b.wait_snap(eng, sn)
        self.snaps = []

    def consumed(self, name):
        self.snaps.append(self.b.snap(name))


def build_nc(phases="ABCDE", dump=False):
    nc = bass.Bass("TRN2", target_bir_lowering=False)
    b = Builder(nc)
    from contextlib import ExitStack

    din = dict(kind="ExternalInput")
    x_d = nc.dram_tensor("x", [T, C], F32, **din).ap()
    wq_d = nc.dram_tensor("wq", [C, C], F32, **din).ap()
    wk_d = nc.dram_tensor("wk", [C, C], F32, **din).ap()
    wv_d = nc.dram_tensor("wv", [C, C], F32, **din).ap()
    wo_d = nc.dram_tensor("wo", [C, C], F32, **din).ap()
    rw_d = nc.dram_tensor("rw", [C, E], F32, **din).ap()
    bo_d = nc.dram_tensor("bo_b", [128, C], F32, **din).ap()
    rb_d = nc.dram_tensor("rb_b", [128, E], F32, **din).ap()
    w1_d = nc.dram_tensor("w1r", [E, NJ, 128, 1024], F32, **din).ap()
    wg_d = nc.dram_tensor("wgr", [E, NJ, 128, 1024], F32, **din).ap()
    w2_d = nc.dram_tensor("w2", [E, HID, C], F32, **din).ap()
    b1_d = nc.dram_tensor("b1c", [E, 128, NJ], F32, **din).ap()
    bg_d = nc.dram_tensor("bgc", [E, 128, NJ], F32, **din).ap()
    b2_d = nc.dram_tensor("b2c", [E, 128, E], F32, **din).ap()
    idb_d = nc.dram_tensor("ident_bf", [128, 128], BF16, **din).ap()
    idf_d = nc.dram_tensor("ident_f32", [128, 128], F32, **din).ap()
    msk_d = nc.dram_tensor("masks", [3, 128, 384], F32, **din).ap()
    io8_d = nc.dram_tensor("iota8_b", [128, E], F32, **din).ap()

    skind = dict(kind="ExternalOutput") if dump else dict(kind="Internal")
    einp_d = nc.dram_tensor("einp_hbm", [NSLOT + 1, C], BF16, **skind).ap()
    eo_d = nc.dram_tensor("eo_hbm", [NSLOT, C], F32, **skind).ap()
    out_d = nc.dram_tensor("out", [T, C], F32, kind="ExternalOutput").ap()
    if dump:
        xn_dump = nc.dram_tensor("xn_dump", [T, C], BF16, kind="ExternalOutput").ap()
        lg_dump = nc.dram_tensor("lg_dump", [128, NT * E], F32, kind="ExternalOutput").ap()
        rt_dump = nc.dram_tensor("rt_dump", [4, 128, 2 * NT], F32,
                                 kind="ExternalOutput").ap()
        afm_dump = nc.dram_tensor("afm_dump", [128, NJ * CAPP], BF16,
                                  kind="ExternalOutput").ap()
        eost_dump = nc.dram_tensor("eost_dump", [128, E * 512], F32,
                                   kind="ExternalOutput").ap()

    x_t = x_d.rearrange("(n p) c -> n p c", p=128)
    out_t = out_d.rearrange("(n p) c -> n p c", p=128)
    einp_t = einp_d[0:NSLOT, :].rearrange("(n p) c -> n p c", p=128)
    eo_t = eo_d.rearrange("(n p) c -> n p c", p=128)

    idb_sb = nc.alloc_sbuf_tensor("idb_sb", [128, 128], BF16)
    idf_sb = nc.alloc_sbuf_tensor("idf_sb", [128, 128], F32)
    i01_sb = nc.alloc_sbuf_tensor("i01_sb", [128, 2 * NT], F32)
    prob_sb = nc.alloc_sbuf_tensor("prob_sb", [128, 2 * NT], F32)
    sidx_sb = nc.alloc_sbuf_tensor("sidx_sb", [128, 2 * NT], I32)
    gidx_sb = nc.alloc_sbuf_tensor("gidx_sb", [128, 2 * NT], I32)

    stA = ExitStack()

    def SA(name, shape, dtype):
        return stA.enter_context(nc.sbuf_tensor(name, shape, dtype))

    xn_all = SA("xn_all", [128, NT * C], BF16)
    zeros_sb = SA("zeros_sb", [128, C], BF16)
    cst_sb = SA("cst_sb", [128, 2], F32)
    rb_sb = SA("rb_sb", [128, E], F32)
    io8_sb = SA("io8_sb", [128, E], F32)
    ltok_all = SA("ltok_all", [128, NT * E], F32)

    stA2 = ExitStack()
    def SA2(name, shape, dtype):
        return stA2.enter_context(nc.sbuf_tensor(name, shape, dtype))

    bo_sb = SA2("bo_sb", [128, C], F32)
    msk_sb = SA2("msk_sb", [128, 3 * 384], F32)
    wq_sb = SA2("wq_sb", [128, NCC * C], BF16)
    wk_sb = SA2("wk_sb", [128, NCC * C], BF16)
    wv_sb = SA2("wv_sb", [128, NCC * C], BF16)
    wo_sb = SA2("wo_sb", [128, NCC * C], BF16)
    rw32_sb = SA2("rw32_sb", [128, NCC * E], F32)
    xs_sb = SA2("xs_sb", [128, 2 * C], F32)
    sq_sb = SA2("sq_sb", [128, 2 * C], F32)
    st_sb = SA2("st_sb", [128, 16], F32)
    xf_sb = SA2("xf_sb", [128, 2 * C], BF16)
    xf32_sb = SA2("xf32_sb", [128, C], F32)
    q_sb = SA2("q_sb", [128, 2 * C], BF16)
    k_sb = SA2("k_sb", [128, 3 * C], BF16)
    v_sb = SA2("v_sb", [128, 3 * C], BF16)
    sc_sb = SA2("sc_sb", [128, 2 * 384], F32)
    exn_sb = SA2("exn_sb", [128, 2 * 384], F32)
    exT_sb = SA2("exT_sb", [128, 2 * 384], BF16)
    sm_sb = SA2("sm_sb", [128, 16], F32)
    ao_sb = SA2("ao_sb", [128, 2 * C], BF16)
    wout_sb = SA2("wout_sb", [128, 2 * C], F32)

    stPA = ExitStack()

    def PA(name, shape, dtype):
        return stPA.enter_context(nc.psum_tensor(name, shape, dtype))

    p_tp = [PA(f"p_tp{i}", [128, 128], F32) for i in range(3)]
    p_mm = [PA(f"p_mm{i}", [128, 512], F32) for i in range(2)]
    p_sc = [PA(f"p_sc{i}", [128, 384], F32) for i in range(2)]
    p_av = PA("p_av", [128, 128], F32)

    r_xs = [Res(b) for _ in range(2)]
    r_sq = [Res(b) for _ in range(2)]
    r_xf = [Res(b) for _ in range(2)]
    r_q = [Res(b) for _ in range(2)]
    r_ao = [Res(b) for _ in range(2)]
    r_wout = [Res(b) for _ in range(2)]
    r_sc = [Res(b) for _ in range(2)]
    r_exn = [Res(b) for _ in range(2)]
    r_exT = [Res(b) for _ in range(2)]
    r_sm = [Res(b) for _ in range(2)]
    g_tp = [Res(b) for _ in range(3)]
    g_mm = [Res(b) for _ in range(2)]
    g_sc = [Res(b) for _ in range(2)]
    g_av = Res(b)

    exn_done = {}
    exp_done = {}
    exT_done = {}
    qk_done = {}
    tp_done = {}
    av_done = {}
    evac_q = []
    wo_evac_q = []

    with nc.Block() as blk:
        def _pre_v(v):
            v.memset(cst_sb[:, 0:1], LN_EPS)
            b.inc(v.memset(cst_sb[:, 1:2], -0.5), "cst_v", 1)
        blk.vector(_pre_v)

        def _pre_sy(s):
            for dst, src in ((bo_sb, bo_d), (rb_sb, rb_d), (io8_sb, io8_d),
                             (idb_sb, idb_d), (idf_sb, idf_d)):
                b.inc(s.dma_start(out=dst[:], in_=src), "c_ld", 16)
            b.inc(s.dma_start(out=msk_sb[:], in_=msk_d.transpose([1, 0, 2])),
                  "c_ld", 16)
        blk.sync(_pre_sy)

        def _pre_gp(g):
            g.memset(zeros_sb[:], 0.0)
            b.inc(g.tensor_copy(out=zeros_sb[:, 0:1], in_=zeros_sb[:, 0:1]), "z_done", 1)
            for dst, src in ((wq_sb, wq_d), (wk_sb, wk_d), (wv_sb, wv_d), (wo_sb, wo_d)):
                for cc in range(NCC):
                    b.inc(g.dma_start(out=dst[:, cc * C:(cc + 1) * C],
                                      in_=src[cc * 128:(cc + 1) * 128, :]), "aw_ld", 16)
        blk.gpsimd(_pre_gp)

        def _pre_rw(s):
            for cc in range(NCC):
                b.inc(s.dma_start(out=rw32_sb[:, cc * E:(cc + 1) * E],
                                  in_=rw_d[cc * 128:(cc + 1) * 128, :]), "c_ld", 16)
        blk.sync(_pre_rw)

        def _pre_zi(s):
            b.wait(s, "z_done")
            for i in range(NSLOT // 128):
                b.inc(s.dma_start(out=einp_t[i], in_=zeros_sb[:]), "zi", 16)
        blk.scalar(_pre_zi)

        def kvt(jq, dk):
            tt = jq - 1 + dk
            return None if (tt < 0 or tt > NT - 1) else tt % 3

        def attn_pe_body(t, jq, h):
            if 0 <= h - 3 <= NH - 1:
                h2 = h - 3
                hp2 = (h2 % 2) * 64
                b.wait_snap(t, exT_done[(jq, h2)])
                if h2 % 2 == 0:
                    g_av.wait_free(t)
                for dk in range(3):
                    s3 = kvt(jq, dk)
                    vt = (zeros_sb[:, h2 * 64:h2 * 64 + 64] if s3 is None
                          else v_sb[:, s3 * C + h2 * 64: s3 * C + h2 * 64 + 64])
                    b.inc(t.matmul(p_av[hp2:hp2 + 64, :], vt,
                                   exT_sb[:, (h2 % 2) * 384 + dk * 128:
                                          (h2 % 2) * 384 + dk * 128 + 128],
                                   start=(dk == 0), stop=(dk == 2),
                                   tile_position=(0, hp2)), "av_pe", 1)
                av_done[(jq, h2)] = b.snap("av_pe")
                r_exT[h2 % 2].consumed("av_pe")
            if 0 <= h - 2 <= NH - 1:
                h1 = h - 2
                b.wait_snap(t, exn_done[(jq, h1)])
                for dk in range(3):
                    g_tp[dk].wait_free(t)
                    b.inc(t.transpose(p_tp[dk][:],
                                      exn_sb[:, (h1 % 2) * 384 + dk * 128:
                                             (h1 % 2) * 384 + dk * 128 + 128],
                                      idf_sb[:]), "atp_pe", 1)
                tp_done[(jq, h1)] = b.snap("atp_pe")
                r_exn[h1 % 2].consumed("atp_pe")
            if h <= NH - 1:
                hp, hc = (h % 2) * 64, (h // 2) * 128
                qs = (jq % 2) * C
                g_sc[h % 2].wait_free(t)
                for dk in range(3):
                    s3 = kvt(jq, dk)
                    kt = (zeros_sb[hp:hp + 64, hc:hc + 128] if s3 is None
                          else k_sb[hp:hp + 64, s3 * C + hc: s3 * C + hc + 128])
                    b.inc(t.matmul(p_sc[h % 2][:, dk * 128:(dk + 1) * 128],
                                   q_sb[hp:hp + 64, qs + hc: qs + hc + 128], kt,
                                   start=(dk == 0), stop=(dk == 2)), "qk_pe", 1)
                qk_done[(jq, h)] = b.snap("qk_pe")

        def attn_evac_body(sc, jq, h):
            if 0 <= h - 3 <= NH - 1 and (h - 3) % 2 == 1:
                h2 = h - 3
                cc = h2 // 2
                b.wait_snap(sc, av_done[(jq, h2)])
                if cc == 0:
                    r_ao[jq % 2].wait_free(sc)
                sc.copy(
                    out=ao_sb[:, (jq % 2) * C + cc * 128:(jq % 2) * C + cc * 128 + 128],
                    in_=p_av[:])
                b.dinc(sc, "ave_v", 1)
                g_av.consumed("ave_v")
            if 0 <= h - 2 <= NH - 1:
                h1 = h - 2
                b.wait_snap(sc, tp_done[(jq, h1)])
                r_exT[h1 % 2].wait_free(sc)
                for dk in range(3):
                    b.inc(sc.copy(
                        out=exT_sb[:, (h1 % 2) * 384 + dk * 128:
                                   (h1 % 2) * 384 + dk * 128 + 128],
                        in_=p_tp[dk][:]), "exT_v", 1)
                    g_tp[dk].consumed("exT_v")
                b.dinc(sc, "exT_d", 1)
                exT_done[(jq, h1)] = b.snap("exT_d")

        def attn_vec_body(v, jq, h, msk):
            if h <= NH - 1:
                hs = (h % 2) * 384
                b.wait_snap(v, qk_done[(jq, h)])
                r_sc[h % 2].wait_free(v)
                b.inc(v.scalar_tensor_tensor(out=sc_sb[:, hs:hs + 384],
                                             in0=p_sc[h % 2][:], scalar=SCALE,
                                             in1=msk, op0=ALU.mult, op1=ALU.add),
                      "msk_v", 1)
                g_sc[h % 2].consumed("msk_v")
                smx = sm_sb[:, (h % 2) * 8:(h % 2) * 8 + 8]
                r_sm[h % 2].wait_free(v)
                b.inc(v.reduce_max(smx[:, 1:2], sc_sb[:, hs:hs + 384], axis=AX.X,
                                   negate=True), "mx_v", 1)

        def attn_act_body(s, jq, h):
            if h <= NH - 1:
                hs = (h % 2) * 384
                smx = sm_sb[:, (h % 2) * 8:(h % 2) * 8 + 8]
                b.wait(s, "mx_v", b.val("mx_v"))
                r_exn[h % 2].wait_free(s)
                b.inc(s.activation(out=exn_sb[:, hs:hs + 384],
                                   in_=sc_sb[:, hs:hs + 384],
                                   func=AF.Exp, bias=smx[:, 1:2]), "exp_s", 1)
                exp_done[(jq, h)] = b.snap("exp_s")
                r_sc[h % 2].consumed("exp_s")
                r_sm[h % 2].consumed("exp_s")

        def attn_vec2_body(v, jq, h):
            if 0 <= h <= NH - 1:
                hs = (h % 2) * 384
                smx = sm_sb[:, (h % 2) * 8:(h % 2) * 8 + 8]
                b.wait(v, "exp_s", exp_done[(jq, h)][1])
                v.reduce_sum(smx[:, 2:3], exn_sb[:, hs:hs + 384], axis=AX.X)
                v.drain()
                v.reciprocal(smx[:, 3:4], smx[:, 2:3])
                v.drain()
                v.tensor_scalar_mul(exn_sb[:, hs:hs + 384], exn_sb[:, hs:hs + 384],
                                    smx[:, 3:4])
                b.dinc(v, "exn_v", 1)
                exn_done[(jq, h)] = b.snap("exn_v")

        for j in range(NT + 1):
            jq = j - 1
            sl = j % 2
            if j < NT:
                def _ld(s, j=j, sl=sl):
                    r_xs[sl].wait_free(s)
                    b.inc(s.dma_start(out=xs_sb[:, sl * C:(sl + 1) * C], in_=x_t[j]),
                          "x_in", 16)
                blk.sync(_ld)

                def _ln_v(v, j=j, sl=sl):
                    xv = xs_sb[:, sl * C:(sl + 1) * C]
                    st = st_sb[:, sl * 8:sl * 8 + 8]
                    b.wait(v, "x_in", 16 * (j + 1))
                    if j == 0:
                        b.wait(v, "c_ld")
                    v.reduce_sum(st[:, 0:1], xv, axis=AX.X)
                    v.drain()
                    v.tensor_scalar_mul(st[:, 2:3], st[:, 0:1], 1.0 / C)
                    b.dinc(v, "oi_v", 1)
                blk.vector(_ln_v)

                def _ln_s(v, j=j, sl=sl):
                    xv = xs_sb[:, sl * C:(sl + 1) * C]
                    st = st_sb[:, sl * 8:sl * 8 + 8]
                    b.wait(v, "x_in", 16 * (j + 1))
                    r_sq[sl].wait_free(v)
                    b.inc(v.scalar_tensor_tensor(out=sq_sb[:, sl * C:(sl + 1) * C],
                                                 in0=xv, scalar=1.0, in1=xv,
                                                 op0=ALU.mult, op1=ALU.mult,
                                                 accum_out=st[:, 1:2]), "ln_s1", 1)
                blk.vector(_ln_s)

                def _ln_v2(v, j=j, sl=sl):
                    st = st_sb[:, sl * 8:sl * 8 + 8]
                    b.wait(v, "ln_s1", b.val("ln_s1"))
                    v.tensor_mul(st[:, 3:4], st[:, 2:3], st[:, 2:3])
                    v.drain()
                    v.scalar_tensor_tensor(out=st[:, 4:5], in0=st[:, 1:2],
                                           scalar=1.0 / C, in1=st[:, 3:4],
                                           op0=ALU.mult, op1=ALU.subtract)
                    b.dinc(v, "ln_v2", 1)
                blk.vector(_ln_v2)

                def _ln_s2(s, j=j, sl=sl):
                    st = st_sb[:, sl * 8:sl * 8 + 8]
                    b.wait(s, "ln_v2", b.val("ln_v2"))
                    if j == 0:
                        b.wait(s, "cst_v")
                    s.activation(out=st[:, 5:6], in_=st[:, 4:5], func=AF.Ln,
                                 bias=cst_sb[:, 0:1])
                    s.drain()
                    b.inc(s.activation(out=st[:, 6:7], in_=st[:, 5:6], func=AF.Exp,
                                       scale=cst_sb[:, 1:2]), "ln_s2", 1)
                blk.scalar(_ln_s2)

                def _ln_v3(v, j=j, sl=sl):
                    st = st_sb[:, sl * 8:sl * 8 + 8]
                    b.wait(v, "ln_s2", b.val("ln_s2"))
                    v.scalar_tensor_tensor(out=st[:, 7:8], in0=st[:, 2:3],
                                           scalar=-1.0, in1=st[:, 6:7],
                                           op0=ALU.mult, op1=ALU.mult)
                    b.dinc(v, "ln_v3", 1)
                blk.vector(_ln_v3)

                def _xn(v, j=j, sl=sl):
                    xv = xs_sb[:, sl * C:(sl + 1) * C]
                    st = st_sb[:, sl * 8:sl * 8 + 8]
                    b.wait(v, "ln_s2", b.val("ln_s2"))
                    v.scalar_tensor_tensor(out=sq_sb[:, sl * C:(sl + 1) * C],
                                           in0=xv, scalar=st[:, 6:7],
                                           in1=st[:, 7:8].to_broadcast([128, C]),
                                           op0=ALU.mult, op1=ALU.add)
                    v.tensor_copy(out=xn_all[:, j * C:(j + 1) * C],
                                  in_=sq_sb[:, sl * C:(sl + 1) * C])
                    b.dinc(v, "xn_s", 1)
                blk.vector(_xn)
                r_xs[sl].consumed("xn_s")
                r_xs[sl].consumed("oi_v")

                def _oinit(s, j=j, sl=sl):
                    b.wait(s, "oi_v", b.val("oi_v"))
                    b.inc(s.dma_start(out=out_t[j], in_=xs_sb[:, sl * C:(sl + 1) * C]),
                          "o_init", 16)
                blk.sync(_oinit)
                r_xs[sl].consumed("o_init")

                def _tp(t, j=j, sl=sl):
                    b.wait(t, "xn_s", b.val("xn_s"))
                    r_xf[sl].wait_free(t)
                    base = b.val("tpe_v")
                    for cc in range(NCC):
                        if cc < 3:
                            g_tp[cc].wait_free(t)
                        else:
                            b.wait(t, "tpe_v", base + 2 * (cc - 2))
                        b.inc(t.transpose(p_tp[cc % 3][:],
                                          sq_sb[:, sl * C + cc * 128: sl * C + cc * 128 + 128],
                                          idf_sb[:]), "tp_pe", 1)
                blk.tensor(_tp)
                r_sq[sl].consumed("tp_pe")

                def _tpe(v, j=j, sl=sl):
                    base = b.val("tp_pe") - NCC
                    for cc in range(NCC):
                        b.wait(v, "tp_pe", base + cc + 1)
                        b.inc(v.tensor_copy(
                            out=xf_sb[:, sl * C + cc * 128: sl * C + cc * 128 + 128],
                            in_=p_tp[cc % 3][:]), "tpe_v", 1)
                        b.inc(v.tensor_copy(
                            out=xf32_sb[:, cc * 128: cc * 128 + 128],
                            in_=p_tp[cc % 3][:]), "tpe_v", 1)
                        g_tp[cc % 3].consumed("tpe_v")
                    b.dinc(v, "tpe_d", 1)
                blk.vector(_tpe)

                def _qkv(t, j=j, sl=sl):
                    if j == 0:
                        b.wait(t, "aw_ld")
                    b.wait(t, "tpe_d", b.val("tpe_d"))
                    b.wait(t, "woe_v", b.val("woe_v"))
                    xf = xf_sb[:, sl * C:(sl + 1) * C]
                    base = b.val("qkv_v")
                    uses = [("q", ec) for ec in range(NCC)] + \
                           [("k", ec) for ec in range(NCC)] + \
                           [("v", 0), ("v", 1), ("lg", 0)]
                    prev_use = {}
                    for u, (kind, idx) in enumerate(uses):
                        bank = idx % 2 if kind != "lg" else 0
                        if bank in prev_use:
                            b.wait(t, "qkv_v", base + prev_use[bank] + 1)
                        else:
                            g_mm[bank].wait_free(t)
                        prev_use[bank] = u
                        if kind == "q" or kind == "k":
                            wsb = wq_sb if kind == "q" else wk_sb
                            for cc in range(NCC):
                                b.inc(t.matmul(p_mm[bank][:, 0:128],
                                               wsb[:, cc * C + idx * 128: cc * C + idx * 128 + 128],
                                               xf[:, cc * 128:(cc + 1) * 128],
                                               start=(cc == 0), stop=(cc == NCC - 1)),
                                      "mm_pe", 1)
                        elif kind == "v":
                            for cc in range(NCC):
                                b.inc(t.matmul(p_mm[bank][:, 0:512],
                                               xf[:, cc * 128:(cc + 1) * 128],
                                               wv_sb[:, cc * C + idx * 512: cc * C + idx * 512 + 512],
                                               start=(cc == 0), stop=(cc == NCC - 1)),
                                      "mm_pe", 1)
                        else:
                            for cc in range(NCC):
                                b.inc(t.matmul(p_mm[bank][0:128, 0:E],
                                               xf32_sb[:, cc * 128:(cc + 1) * 128],
                                               rw32_sb[:, cc * E:(cc + 1) * E],
                                               start=(cc == 0), stop=(cc == NCC - 1)),
                                      "mm_pe", 1)
                        evac_q.append((j, kind, idx, bank, b.snap("mm_pe")))
                blk.tensor(_qkv)
                r_xf[sl].consumed("mm_pe")

                def _qkve(v, j=j, sl=sl):
                    while evac_q:
                        jj, kind, idx, bank, sn = evac_q.pop(0)
                        b.wait_snap(v, sn)
                        if kind == "q":
                            if idx == 0:
                                r_q[jj % 2].wait_free(v)
                            b.inc(v.tensor_copy(
                                out=q_sb[:, (jj % 2) * C + idx * 128:(jj % 2) * C + idx * 128 + 128],
                                in_=p_mm[bank][:, 0:128]), "qkv_v", 1)
                        elif kind == "k":
                            b.inc(v.tensor_copy(
                                out=k_sb[:, (jj % 3) * C + idx * 128:(jj % 3) * C + idx * 128 + 128],
                                in_=p_mm[bank][:, 0:128]), "qkv_v", 1)
                        elif kind == "v":
                            b.inc(v.tensor_copy(
                                out=v_sb[:, (jj % 3) * C + idx * 512:(jj % 3) * C + idx * 512 + 512],
                                in_=p_mm[bank][:, 0:512]), "qkv_v", 1)
                        else:
                            v.tensor_copy(out=ltok_all[:, jj * E:(jj + 1) * E],
                                          in_=p_mm[bank][0:128, 0:E])
                            b.dinc(v, "lgv", 1)
                        g_mm[bank].consumed("qkv_v" if kind != "lg" else "lgv")
                    b.dinc(v, "qkv_d", 1)
                blk.vector(_qkve)


            if 0 <= jq <= NT - 1:
                mtile = 0 if jq == 0 else (2 if jq == NT - 1 else 1)
                msk = msk_sb[:, mtile * 384:(mtile + 1) * 384]
                for h in range(NH + 4):
                    def _a_pe(t, jq=jq, h=h):
                        if h == 0:
                            b.wait(t, "qkv_d", b.val("qkv_d"))
                            if jq == 0:
                                b.wait(t, "z_done")
                        attn_pe_body(t, jq, h)
                    blk.tensor(_a_pe)

                    def _a_ve(v, jq=jq, h=h, msk=msk):
                        attn_vec_body(v, jq, h, msk)
                    blk.vector(_a_ve)

                    def _a_sc(s, jq=jq, h=h):
                        attn_evac_body(s, jq, h)
                        attn_act_body(s, jq, h)
                    blk.scalar(_a_sc)

                    def _a_ve2(v, jq=jq, h=h):
                        attn_vec2_body(v, jq, h - 1)
                    blk.vector(_a_ve2)
                r_q[jq % 2].consumed("qk_pe")

                def _wo(t, jq=jq):
                    b.wait(t, "ave_v", b.val("ave_v"))
                    for half in range(2):
                        g_mm[half].wait_free(t)
                        for fc in range(NCC):
                            b.inc(t.matmul(p_mm[half][:, 0:512],
                                           ao_sb[:, (jq % 2) * C + fc * 128:(jq % 2) * C + fc * 128 + 128],
                                           wo_sb[:, fc * C + half * 512: fc * C + half * 512 + 512],
                                           start=(fc == 0), stop=(fc == NCC - 1)),
                                  "wo_pe", 1)
                        wo_evac_q.append((jq, half, b.snap("wo_pe")))
                blk.tensor(_wo)
                r_ao[jq % 2].consumed("wo_pe")

                def _woe(v, jq=jq):
                    while wo_evac_q:
                        jj, half, sn = wo_evac_q.pop(0)
                        b.wait_snap(v, sn)
                        if half == 0:
                            r_wout[jj % 2].wait_free(v)
                        b.inc(v.tensor_tensor(
                            out=wout_sb[:, (jj % 2) * C + half * 512:(jj % 2) * C + half * 512 + 512],
                            in0=p_mm[half][:, 0:512],
                            in1=bo_sb[:, half * 512: half * 512 + 512],
                            op=ALU.add), "woe_v", 1)
                        g_mm[half].consumed("woe_v")
                    b.dinc(v, "woe_d", 1)
                blk.vector(_woe)

                def _oacc(g, jq=jq):
                    b.wait(g, "woe_d", b.val("woe_d"))
                    b.wait(g, "o_init", 16 * (jq + 1))
                    b.inc(g.dma_start(out=out_t[jq],
                                      in_=wout_sb[:, (jq % 2) * C:(jq % 2) * C + C],
                                      accum_op=ALU.add), "o_acc", 16)
                blk.gpsimd(_oacc)
                r_wout[jq % 2].consumed("o_acc")

        if dump:
            def _dump_a(s):
                b.wait(s, "xn_s", b.val("xn_s"))
                for j in range(NT):
                    b.inc(s.dma_start(
                        out=xn_dump.rearrange("(n p) c -> n p c", p=128)[j],
                        in_=xn_all[:, j * C:(j + 1) * C]), "dmp", 16)
                b.wait(s, "lgv", b.val("lgv"))
                b.inc(s.dma_start(out=lg_dump, in_=ltok_all[:]), "dmp", 16)
            blk.sync(_dump_a)

    g_bt = [Res(b) for _ in range(2)]
    if "B" in phases:
        stA2.close()
        stB = ExitStack()

        def SB(name, shape, dtype):
            return stB.enter_context(nc.sbuf_tensor(name, shape, dtype))

        m_em = SB("m_em", [E, 2 * T], F32)
        cnt_em = SB("cnt_em", [E, 2 * T], F32)
        smB_sb = SB("smB_sb", [128, 16], F32)
        oh_sb = SB("oh_sb", [128, 2 * E], F32)
        ltok_sb = SB("ltok_sb", [128, E], F32)
        top8_sb = SB("top8_sb", [128, E], F32)
        mi_sb = SB("mi_sb", [128, E], mybir.dt.uint32)
        ctok_sb = SB("ctok_sb", [128, 2 * E], F32)
        tmp8_sb = SB("tmp8_sb", [128, 8], F32)
        zcol_sb = SB("zcol_sb", [E, 1], F32)

        with nc.Block() as blk:
            for j in range(NT):
                def _b_v(v, j=j):
                    b.inc(v.tensor_add(ltok_sb[:], ltok_all[:, j * E:(j + 1) * E],
                                       rb_sb[:]), "bv", 1)
                    v.drain()
                    v.reduce_max(smB_sb[:, 8:9], ltok_sb[:], axis=AX.X)
                    v.drain()
                    v.tensor_scalar(out=oh_sb[:, 0:E], in0=ltok_sb[:],
                                    scalar1=smB_sb[:, 8:9], scalar2=None,
                                    op0=ALU.is_ge)
                    v.drain()
                    v.tensor_mul(tmp8_sb[:], oh_sb[:, 0:E], io8_sb[:])
                    v.drain()
                    v.reduce_sum(i01_sb[:, 2 * j:2 * j + 1], tmp8_sb[:], axis=AX.X)
                    v.scalar_tensor_tensor(out=top8_sb[:], in0=oh_sb[:, 0:E],
                                           scalar=-DROP_BIG, in1=ltok_sb[:],
                                           op0=ALU.mult, op1=ALU.add)
                    v.drain()
                    v.reduce_max(smB_sb[:, 9:10], top8_sb[:], axis=AX.X)
                    v.drain()
                    v.tensor_scalar(out=oh_sb[:, E:2 * E], in0=top8_sb[:],
                                    scalar1=smB_sb[:, 9:10], scalar2=None,
                                    op0=ALU.is_ge)
                    v.drain()
                    v.tensor_mul(tmp8_sb[:], oh_sb[:, E:2 * E], io8_sb[:])
                    v.drain()
                    v.reduce_sum(i01_sb[:, 2 * j + 1:2 * j + 2], tmp8_sb[:], axis=AX.X)
                    v.tensor_sub(smB_sb[:, 0:1], smB_sb[:, 9:10], smB_sb[:, 8:9])
                    b.dinc(v, "bv", 1)
                blk.vector(_b_v)

                def _b_s(s, j=j):
                    b.wait(s, "bv", b.val("bv"))
                    b.inc(s.activation(out=smB_sb[:, 2:3], in_=smB_sb[:, 0:1],
                                       func=AF.Exp), "bs", 1)
                blk.scalar(_b_s)

                def _b_v2(v, j=j):
                    b.wait(v, "bs", b.val("bs"))
                    v.tensor_scalar_add(smB_sb[:, 3:4], smB_sb[:, 2:3], 1.0)
                    v.drain()
                    v.reciprocal(prob_sb[:, 2 * j:2 * j + 1], smB_sb[:, 3:4])
                    v.drain()
                    v.tensor_scalar(out=prob_sb[:, 2 * j + 1:2 * j + 2],
                                    in0=prob_sb[:, 2 * j:2 * j + 1],
                                    scalar1=-1.0, scalar2=1.0,
                                    op0=ALU.mult, op1=ALU.add)
                    for kk in range(2):
                        v.tensor_scalar(out=oh_sb[:, kk * E:(kk + 1) * E],
                                        in0=io8_sb[:],
                                        scalar1=i01_sb[:, 2 * j + kk:2 * j + kk + 1],
                                        scalar2=None, op0=ALU.is_equal)
                    b.dinc(v, "bv2", 1)
                blk.vector(_b_v2)

                def _b_tp2(t, j=j):
                    b.wait(t, "bv2", b.val("bv2"))
                    for kk in range(2):
                        g_bt[kk].wait_free(t)
                        t.transpose(p_mm[kk][0:8, 0:128],
                                    oh_sb[:, kk * E:(kk + 1) * E], idf_sb[:])
                    b.dinc(t, "btp2_pe", 2)
                blk.tensor(_b_tp2)

                def _b_v3(v, j=j):
                    b.wait(v, "btp2_pe", b.val("btp2_pe"))
                    for kk in range(2):
                        v.tensor_copy(
                            out=m_em[:, kk * T + j * 128: kk * T + j * 128 + 128],
                            in_=p_mm[kk][0:8, 0:128])
                    b.dinc(v, "bv3", 1)
                    for kk in range(2):
                        g_bt[kk].consumed("bv3")
                blk.vector(_b_v3)

            def _scan(v):
                b.wait(v, "bv3", b.val("bv3"))
                v.memset(zcol_sb[:], 0.0)
                v.drain()
                v.tensor_tensor_scan(out=cnt_em[:], data0=m_em[:],
                                     data1=zcol_sb[:].to_broadcast([E, 2 * T]),
                                     initial=0.0, op0=ALU.add, op1=ALU.add)
                b.dinc(v, "scan_v", 1)
            blk.vector(_scan)

            for j in range(NT):
                def _c_tp(t, j=j):
                    if j == 0:
                        b.wait(t, "scan_v", b.val("scan_v"))
                    for kk in range(2):
                        g_bt[kk].wait_free(t)
                        b.inc(t.transpose(p_mm[kk][0:128, 0:8],
                                          cnt_em[:, kk * T + j * 128: kk * T + j * 128 + 128],
                                          idf_sb[0:8, 0:8]), "ctp_pe", 1)
                blk.tensor(_c_tp)

                def _c_v(v, j=j):
                    b.wait(v, "ctp_pe", b.val("ctp_pe"))
                    for kk in range(2):
                        b.inc(v.tensor_copy(out=ctok_sb[:, kk * E:(kk + 1) * E],
                                            in_=p_mm[kk][0:128, 0:8]), "cv", 1)
                        g_bt[kk].consumed("cv")
                    v.drain()
                    for kk in range(2):
                        ctok = ctok_sb[:, kk * E:(kk + 1) * E]
                        v.tensor_scalar(out=oh_sb[:, kk * E:(kk + 1) * E], in0=io8_sb[:],
                                        scalar1=i01_sb[:, 2 * j + kk:2 * j + kk + 1],
                                        scalar2=None, op0=ALU.is_equal)
                        v.drain()
                        v.tensor_mul(tmp8_sb[:], oh_sb[:, kk * E:(kk + 1) * E], ctok)
                        v.drain()
                        v.reduce_sum(smB_sb[:, 0:1], tmp8_sb[:], axis=AX.X)
                        v.drain()
                        v.tensor_scalar_add(smB_sb[:, 1:2], smB_sb[:, 0:1], -1.0)
                        v.drain()
                        v.scalar_tensor_tensor(out=smB_sb[:, 2:3],
                                               in0=i01_sb[:, 2 * j + kk:2 * j + kk + 1],
                                               scalar=float(CAPP), in1=smB_sb[:, 1:2],
                                               op0=ALU.mult, op1=ALU.add)
                        v.tensor_scalar(out=smB_sb[:, 3:4], in0=smB_sb[:, 1:2],
                                        scalar1=float(CAP), scalar2=None, op0=ALU.is_lt)
                        v.drain()
                        v.tensor_mul(smB_sb[:, 4:5], smB_sb[:, 2:3], smB_sb[:, 3:4])
                        v.tensor_scalar(out=smB_sb[:, 7:8], in0=smB_sb[:, 3:4],
                                        scalar1=-float(NSLOT), scalar2=float(NSLOT),
                                        op0=ALU.mult, op1=ALU.add)
                        v.drain()
                        v.tensor_add(smB_sb[:, 4:5], smB_sb[:, 4:5], smB_sb[:, 7:8])
                        v.drain()
                        v.tensor_copy(out=sidx_sb[:, 2 * j + kk:2 * j + kk + 1],
                                      in_=smB_sb[:, 4:5])
                        v.tensor_scalar_min(smB_sb[:, 5:6], smB_sb[:, 1:2],
                                            float(CAP - 1))
                        v.drain()
                        v.scalar_tensor_tensor(out=smB_sb[:, 6:7],
                                               in0=i01_sb[:, 2 * j + kk:2 * j + kk + 1],
                                               scalar=float(CAPP), in1=smB_sb[:, 5:6],
                                               op0=ALU.mult, op1=ALU.add)
                        v.drain()
                        v.tensor_copy(out=gidx_sb[:, 2 * j + kk:2 * j + kk + 1],
                                      in_=smB_sb[:, 6:7])
                    b.dinc(v, "cv", 1)
                blk.vector(_c_v)

            if dump:
                def _dump_rt(s):
                    b.wait(s, "cv", b.val("cv"))
                    b.inc(s.dma_start(out=rt_dump[0], in_=i01_sb[:]), "dmp", 16)
                    b.inc(s.dma_start(out=rt_dump[1], in_=prob_sb[:]), "dmp", 16)
                    b.inc(s.dma_start(out=rt_dump[2], in_=sidx_sb[:].bitcast(F32)),
                          "dmp", 16)
                    b.inc(s.dma_start(out=rt_dump[3], in_=gidx_sb[:].bitcast(F32)),
                          "dmp", 16)
                blk.sync(_dump_rt)
        stB.close()
    else:
        stA2.close()

    if "C" in phases:
        with nc.Block() as blk:
            def _scat(g):
                b.wait(g, "zi", b.val("zi"))
                for j in range(NT):
                    for kk in range(2):
                        b.inc(g.indirect_dma_start(
                            out=einp_d,
                            out_offset=IndirectOffsetOnAxis(
                                sidx_sb[:, 2 * j + kk:2 * j + kk + 1], 0),
                            in_=xn_all[:, j * C:(j + 1) * C],
                            in_offset=None), "cs", 16)
                g.wait_ge(b.s("cs")[0], b.val("cs"))
            blk.gpsimd(_scat)

    stPA.close()
    stA.close()

    if "D" in phases:
        stD = ExitStack()

        def SD(name, shape, dtype):
            return stD.enter_context(nc.sbuf_tensor(name, shape, dtype))

        a_fm = SD("a_fm", [128, NJ * CAPP], BF16)
        einp_fm = SD("einp_fm", [128, NCC * CAPP], BF16)
        eit_sb = SD("eit_sb", [128, 2 * C], BF16)
        w2_sb = SD("w2_sb", [128, NJ * C], BF16)
        w1_sl = SD("w1_sl", [128, 2 * 1024], BF16)
        wg_sl = SD("wg_sl", [128, 2 * 1024], BF16)
        tg_sb = SD("tg_sb", [128, 512], F32)
        m_sb2 = SD("m_sb2", [128, 512], F32)
        eo_st = SD("eo_st", [128, E * 512], F32)
        eo_sl = SD("eo_sl", [128, C], F32)
        b1_sb = SD("b1_sb", [128, NJ], F32)
        bg_sb = SD("bg_sb", [128, NJ], F32)
        b2_sb = SD("b2_sb", [128, E], F32)

        stPD = ExitStack()

        def PDp(name, shape, dtype):
            return stPD.enter_context(nc.psum_tensor(name, shape, dtype))

        p_h = PDp("p_h", [128, 512], F32)
        p_g = PDp("p_g", [128, 512], F32)
        p_o = [PDp(f"p_o{i}", [128, 512], F32) for i in range(2)]
        p_dt = [PDp(f"p_dt{i}", [128, 128], BF16) for i in range(2)]
        p_et = PDp("p_et", [128, 128], F32)

        g_h = Res(b)
        g_g = Res(b)
        g_o = [Res(b) for _ in range(2)]
        r_eit = [Res(b) for _ in range(2)]
        r_w1 = [Res(b) for _ in range(2)]
        r_wg = [Res(b) for _ in range(2)]
        r_tg = [Res(b) for _ in range(2)]
        r_m = [Res(b) for _ in range(2)]
        r_eosl = Res(b)
        g_w2 = Res(b)
        g_afm = Res(b)
        g_eost = Res(b)
        g_bias = Res(b)

        w1_arr = {}
        w2_arr = {}
        p1_q = []
        tg_q = []
        m_q = []
        p2_q = []

        with nc.Block() as blk:
            for e in range(E):
                def _bias(s, e=e):
                    if e == 0 and "C" in phases:
                        b.wait(s, "cs", b.val("cs"))
                    g_bias.wait_free(s)
                    b.inc(s.dma_start(out=b1_sb[:], in_=b1_d[e]), "bia", 16)
                    b.inc(s.dma_start(out=bg_sb[:], in_=bg_d[e]), "bia", 16)
                    b.inc(s.dma_start(out=b2_sb[:], in_=b2_d[e]), "bia", 16)
                blk.sync(_bias)
                bia_now = b.snap("bia")

                for st in range(NST):
                    def _slab_ld(s, e=e, st=st):
                        r_eit[st % 2].wait_free(s)
                        b.inc(s.dma_start(out=eit_sb[:, (st % 2) * C:(st % 2) * C + C],
                                          in_=einp_t[e * NST + st]), "slab", 16)
                    blk.sync(_slab_ld)

                    def _slab_tp(t, e=e, st=st):
                        b.wait(t, "slab", b.val("slab"))
                        base = b.val("dte_v")
                        for cc in range(NCC):
                            if base + cc - 1 > 0:
                                b.wait(t, "dte_v", base + cc - 1)
                            b.inc(t.transpose(p_dt[cc % 2][:],
                                              eit_sb[:, (st % 2) * C + cc * 128:
                                                     (st % 2) * C + cc * 128 + 128],
                                              idb_sb[:]), "dtp_pe", 1)
                    blk.tensor(_slab_tp)

                    def _slab_ev(v, e=e, st=st):
                        base = b.val("dtp_pe") - NCC
                        for cc in range(NCC):
                            b.wait(v, "dtp_pe", base + cc + 1)
                            b.inc(v.tensor_copy(
                                out=einp_fm[:, cc * CAPP + st * 128: cc * CAPP + st * 128 + 128],
                                in_=p_dt[cc % 2][:]), "dte_v", 1)
                        b.dinc(v, "dte_d", 1)
                    blk.vector(_slab_ev)
                    r_eit[st % 2].consumed("dtp_pe")
                slab_done = b.snap("dte_d")

                def _w_pre(g, e=e):
                    for jj in range(2):
                        r_w1[jj].wait_free(g)
                        b.inc(g.dma_start(out=w1_sl[:, jj * 1024:(jj + 1) * 1024],
                                          in_=w1_d[e, jj]), "w1d", 16)
                        w1_arr[(e, jj, 0)] = b.snap("w1d")
                        r_wg[jj].wait_free(g)
                        b.inc(g.dma_start(out=wg_sl[:, jj * 1024:(jj + 1) * 1024],
                                          in_=wg_d[e, jj]), "w1d", 16)
                        w1_arr[(e, jj, 1)] = b.snap("w1d")
                blk.gpsimd(_w_pre)

                for j in range(NJ):
                    def _p1_pe(t, e=e, j=j):
                        if j == 0:
                            b.wait_snap(t, slab_done)
                            g_afm.wait_free(t)
                        b.wait_snap(t, w1_arr[(e, j, 0)])
                        b.wait_snap(t, w1_arr[(e, j, 1)])
                        base_m = b.val("m_v")
                        base_tg = b.val("tg_s")
                        nbi = len(SLOT_BLOCKS)
                        for bi, (s0, w) in enumerate(SLOT_BLOCKS):
                            gi = (j * nbi + bi) % 2
                            ph = p_h if gi == 0 else p_o[0]
                            pg = p_g if gi == 0 else p_o[1]
                            if bi >= 2:
                                b.wait(t, "m_v", base_m + bi - 1)
                            else:
                                (g_h if gi == 0 else g_o[0]).wait_free(t)
                                (g_g if gi == 0 else g_o[1]).wait_free(t)
                            for cc in range(NCC):
                                b.inc(t.matmul(
                                    pg[:, 0:w],
                                    wg_sl[:, (j % 2) * 1024 + cc * 128:(j % 2) * 1024 + cc * 128 + 128],
                                    einp_fm[:, cc * CAPP + s0: cc * CAPP + s0 + w],
                                    start=(cc == 0), stop=(cc == NCC - 1)), "g_pe", 1)
                            g_now = b.snap("g_pe")
                            for cc in range(NCC):
                                b.inc(t.matmul(
                                    ph[:, 0:w],
                                    w1_sl[:, (j % 2) * 1024 + cc * 128:(j % 2) * 1024 + cc * 128 + 128],
                                    einp_fm[:, cc * CAPP + s0: cc * CAPP + s0 + w],
                                    start=(cc == 0), stop=(cc == NCC - 1)), "h_pe", 1)
                            p1_q.append((e, j, bi, s0, w, b.snap("h_pe"), g_now, gi))
                    blk.tensor(_p1_pe)

                    for bi in range(len(SLOT_BLOCKS)):
                        def _p1_sc(v, e=e, j=j, bi=bi):
                            for item in list(p1_q):
                                ee, jj, bb_, s0, w, hsn, gsn, gi = item
                                if (ee, jj, bb_) != (e, j, bi):
                                    continue
                                b.wait_snap(v, gsn)
                                if jj == 0 and bb_ == 0:
                                    b.wait_snap(v, bia_now)
                                r_tg[0].wait_free(v)
                                pg = p_g if gi == 0 else p_o[1]
                                v.tensor_scalar_add(tg_sb[:, 0:w], pg[:, 0:w],
                                                    bg_sb[:, jj:jj + 1])
                                b.dinc(v, "tg_s", 1)
                                (g_g if gi == 0 else g_o[1]).consumed("tg_s")
                                tg_q.append((ee, jj, bb_, s0, w, hsn, b.snap("tg_s"), gi))
                                p1_q.remove(item)
                        blk.vector(_p1_sc)

                        def _p1_v(v, e=e, j=j, bi=bi):
                            for item in list(tg_q):
                                ee, jj, bb_, s0, w, hsn, tsn, gi = item
                                if (ee, jj, bb_) != (e, j, bi):
                                    continue
                                b.wait_snap(v, hsn)
                                b.wait_snap(v, tsn)
                                r_m[0].wait_free(v)
                                ph = p_h if gi == 0 else p_o[0]
                                v.scalar_tensor_tensor(
                                    out=m_sb2[:, 0:w],
                                    in0=ph[:, 0:w], scalar=b1_sb[:, jj:jj + 1],
                                    in1=tg_sb[:, 0:w],
                                    op0=ALU.add, op1=ALU.mult)
                                b.dinc(v, "m_v", 1)
                                (g_h if gi == 0 else g_o[0]).consumed("m_v")
                                r_tg[0].consumed("m_v")
                                m_q.append((ee, jj, bb_, s0, w, b.snap("m_v")))
                                tg_q.remove(item)
                        blk.vector(_p1_v)

                        def _p1_si(s, e=e, j=j, bi=bi):
                            for item in list(m_q):
                                ee, jj, bb_, s0, w, msn = item
                                if (ee, jj, bb_) != (e, j, bi):
                                    continue
                                b.wait_snap(s, msn)
                                b.inc(s.activation(
                                    out=a_fm[:, jj * CAPP + s0: jj * CAPP + s0 + w],
                                    in_=m_sb2[:, 0:w],
                                    func=AF.Silu), "a_s", 1)
                                r_m[0].consumed("a_s")
                                m_q.remove(item)
                        blk.scalar(_p1_si)

                    r_w1[j % 2].consumed("h_pe")
                    r_wg[j % 2].consumed("g_pe")

                    def _w_next(g, e=e, j=j):
                        jn = j + 2
                        if jn < NJ:
                            r_w1[jn % 2].wait_free(g)
                            b.inc(g.dma_start(
                                out=w1_sl[:, (jn % 2) * 1024:(jn % 2) * 1024 + 1024],
                                in_=w1_d[e, jn]), "w1d", 16)
                            w1_arr[(e, jn, 0)] = b.snap("w1d")
                            r_wg[jn % 2].wait_free(g)
                            b.inc(g.dma_start(
                                out=wg_sl[:, (jn % 2) * 1024:(jn % 2) * 1024 + 1024],
                                in_=wg_d[e, jn]), "w1d", 16)
                            w1_arr[(e, jn, 1)] = b.snap("w1d")
                        if j == 0:
                            g_w2.wait_free(g)
                        b.inc(g.dma_start(out=w2_sb[:, j * C:(j + 1) * C],
                                          in_=w2_d[e, j * 128:(j + 1) * 128, :]),
                              "w2d", 16)
                        w2_arr[(e, j)] = b.snap("w2d")
                    blk.gpsimd(_w_next)

                if dump and e == 0:
                    def _dump_afm(s, e=e):
                        b.wait(s, "a_s", b.val("a_s"))
                        b.inc(s.dma_start(out=afm_dump, in_=a_fm[:]), "dmp", 16)
                    blk.sync(_dump_afm)
                for bi, (s0, w) in enumerate(SLOT_BLOCKS):
                    def _p2_pe(t, e=e, bi=bi, s0=s0, w=w):
                        if bi == 0:
                            b.wait(t, "a_s", b.val("a_s"))
                            b.wait_snap(t, w2_arr[(e, NJ - 1)])
                        base_oev = b.val("oev_v")
                        for oc in range(NCC):
                            if oc < 2:
                                g_o[oc % 2].wait_free(t)
                            else:
                                b.wait(t, "oev_v", base_oev + oc - 1)
                            for jj in range(NJ):
                                b.inc(t.matmul(
                                    p_o[oc % 2][:, 0:w],
                                    w2_sb[:, jj * C + oc * 128: jj * C + oc * 128 + 128],
                                    a_fm[:, jj * CAPP + s0: jj * CAPP + s0 + w],
                                    start=(jj == 0), stop=(jj == NJ - 1)), "o_pe", 1)
                            p2_q.append((e, bi, oc, w, b.snap("o_pe")))
                    blk.tensor(_p2_pe)

                    def _p2_ev(v, e=e, bi=bi):
                        while p2_q:
                            ee, bb_, oc, w_, sn = p2_q.pop(0)
                            b.wait_snap(v, sn)
                            if oc == 0:
                                g_eost.wait_free(v)
                                b.wait(v, "eow", b.val("eow"))
                            v.tensor_scalar_add(eo_st[:, oc * 512:oc * 512 + w_],
                                                p_o[oc % 2][:, 0:w_],
                                                b2_sb[:, oc:oc + 1])
                            b.dinc(v, "oev_v", 1)
                            g_o[oc % 2].consumed("oev_v")
                    blk.vector(_p2_ev)

                    if dump and e == 0 and bi == 0:
                        def _dump_eost(s, e=e):
                            b.wait(s, "oev_v", b.val("oev_v"))
                            b.inc(s.dma_start(out=eost_dump, in_=eo_st[:]), "dmp", 16)
                        blk.sync(_dump_eost)
                    nst_b = w // 128
                    for st in range(nst_b):
                        def _eo_tp(t, e=e, bi=bi, s0=s0, st=st):
                            b.wait(t, "oev_v", b.val("oev_v"))
                            base = b.val("ete_v")
                            for oc in range(NCC):
                                if base + oc > 0:
                                    b.wait(t, "ete_v", base + oc)
                                b.inc(t.transpose(
                                    p_et[:],
                                    eo_st[:, oc * 512 + st * 128: oc * 512 + st * 128 + 128],
                                    idf_sb[:]), "etp_pe", 1)
                        blk.tensor(_eo_tp)

                        def _eo_ev(v, e=e, bi=bi, s0=s0, st=st):
                            r_eosl.wait_free(v)
                            base = b.val("etp_pe") - NCC
                            for oc in range(NCC):
                                b.wait(v, "etp_pe", base + oc + 1)
                                b.inc(v.tensor_copy(out=eo_sl[:, oc * 128:(oc + 1) * 128],
                                                    in_=p_et[:]), "ete_v", 1)
                            b.dinc(v, "ete_d", 1)
                        blk.vector(_eo_ev)

                        def _eo_dma(s, e=e, bi=bi, s0=s0, st=st):
                            b.wait(s, "ete_d", b.val("ete_d"))
                            b.inc(s.dma_start(out=eo_t[e * NST + s0 // 128 + st],
                                              in_=eo_sl[:]), "eow", 16)
                        blk.sync(_eo_dma)
                        r_eosl.consumed("eow")
                    g_eost.consumed("etp_pe")

                g_afm.consumed("o_pe")
                g_w2.consumed("o_pe")
                g_bias.consumed("m_v")
                g_bias.consumed("tg_s")
                g_bias.consumed("oev_v")

        stPD.close()
        stD.close()

    if "E" in phases:
        stE = ExitStack()

        def SE(name, shape, dtype):
            return stE.enter_context(nc.sbuf_tensor(name, shape, dtype))

        g0_sb = SE("g0_sb", [128, 2 * C], F32)
        g1_sb = SE("g1_sb", [128, 2 * C], F32)
        rr_sb = SE("rr_sb", [128, 2 * C], F32)
        fo_sb = SE("fo_sb", [128, 2 * C], F32)
        r_g0 = [Res(b) for _ in range(2)]
        r_g1 = [Res(b) for _ in range(2)]
        r_rr = [Res(b) for _ in range(2)]
        r_fo = [Res(b) for _ in range(2)]

        with nc.Block() as blk:
            for j in range(NT):
                sl = j % 2

                def _e_rd(s, j=j, sl=sl):
                    if j == 0:
                        b.wait(s, "o_acc", b.val("o_acc"))
                    r_rr[sl].wait_free(s)
                    b.inc(s.dma_start(out=rr_sb[:, sl * C:(sl + 1) * C], in_=out_t[j]),
                          "rrd", 16)
                blk.sync(_e_rd)

                def _e_g(g, j=j, sl=sl):
                    if j == 0:
                        b.wait(g, "eow", b.val("eow"))
                    r_g0[sl].wait_free(g)
                    b.inc(g.indirect_dma_start(
                        out=g0_sb[:, sl * C:(sl + 1) * C], out_offset=None,
                        in_=eo_d,
                        in_offset=IndirectOffsetOnAxis(gidx_sb[:, 2 * j:2 * j + 1], 0)),
                          "gat", 16)
                    r_g1[sl].wait_free(g)
                    b.inc(g.indirect_dma_start(
                        out=g1_sb[:, sl * C:(sl + 1) * C], out_offset=None,
                        in_=eo_d,
                        in_offset=IndirectOffsetOnAxis(gidx_sb[:, 2 * j + 1:2 * j + 2], 0)),
                          "gat", 16)
                blk.gpsimd(_e_g)

                def _e_v(v, j=j, sl=sl):
                    b.wait(v, "gat", b.val("gat"))
                    b.wait(v, "rrd", b.val("rrd"))
                    r_fo[sl].wait_free(v)
                    v.tensor_scalar_mul(fo_sb[:, sl * C:(sl + 1) * C],
                                        g0_sb[:, sl * C:(sl + 1) * C],
                                        prob_sb[:, 2 * j:2 * j + 1])
                    v.drain()
                    v.scalar_tensor_tensor(out=fo_sb[:, sl * C:(sl + 1) * C],
                                           in0=g1_sb[:, sl * C:(sl + 1) * C],
                                           scalar=prob_sb[:, 2 * j + 1:2 * j + 2],
                                           in1=fo_sb[:, sl * C:(sl + 1) * C],
                                           op0=ALU.mult, op1=ALU.add)
                    v.drain()
                    v.tensor_add(fo_sb[:, sl * C:(sl + 1) * C],
                                 fo_sb[:, sl * C:(sl + 1) * C],
                                 rr_sb[:, sl * C:(sl + 1) * C])
                    b.dinc(v, "cmb_v", 1)
                    r_g0[sl].consumed("cmb_v")
                    r_g1[sl].consumed("cmb_v")
                    r_rr[sl].consumed("cmb_v")
                blk.vector(_e_v)

                def _e_wr(s, j=j, sl=sl):
                    b.wait(s, "cmb_v", b.val("cmb_v"))
                    b.inc(s.dma_start(out=out_t[j], in_=fo_sb[:, sl * C:(sl + 1) * C]),
                          "fo", 16)
                blk.sync(_e_wr)
                r_fo[sl].consumed("fo")

        stE.close()

    with nc.Block() as blk:
        def _wait_all(s):
            for name in ("x_in", "o_init", "o_acc", "zi", "cs", "slab", "w1d",
                         "w2d", "bia", "eow", "rrd", "gat", "fo", "aw_ld",
                         "c_ld", "dmp"):
                if name in b.sems:
                    b.wait(s, name, b.val(name))
        blk.sync(_wait_all)

    return nc


def _prep_inputs(inputs):
    f32 = np.float32
    x = np.ascontiguousarray(np.asarray(inputs["L0_attn_x"], dtype=f32))
    wq = np.asarray(inputs["L0_attn_Wq"], dtype=f32)
    wk = np.asarray(inputs["L0_attn_Wk"], dtype=f32)
    wv = np.asarray(inputs["L0_attn_Wv"], dtype=f32)
    wo = np.asarray(inputs["L0_attn_Wo"], dtype=f32)
    bo = np.asarray(inputs["L0_attn_bo"], dtype=f32)
    rw = np.ascontiguousarray(np.asarray(inputs["L0_ffn_router_w"], dtype=f32))
    rb = np.asarray(inputs["L0_ffn_router_b"], dtype=f32)
    w1 = np.asarray(inputs["L0_ffn_w_c_fc"], dtype=f32)
    b1 = np.asarray(inputs["L0_ffn_b_c_fc"], dtype=f32)
    wg = np.asarray(inputs["L0_ffn_w_gate"], dtype=f32)
    bg = np.asarray(inputs["L0_ffn_b_gate"], dtype=f32)
    w2 = np.ascontiguousarray(np.asarray(inputs["L0_ffn_w_c_proj"], dtype=f32))
    b2 = np.asarray(inputs["L0_ffn_b_c_proj"], dtype=f32)

    com = {
        "wq": np.ascontiguousarray(wq.T), "wk": np.ascontiguousarray(wk.T),
        "wv": np.ascontiguousarray(wv.T), "wo": np.ascontiguousarray(wo.T),
        "rw": rw,
        "bo_b": np.ascontiguousarray(np.broadcast_to(bo, (128, C))),
        "rb_b": np.ascontiguousarray(np.broadcast_to(rb, (128, E))),
        "w1r": np.ascontiguousarray(
            w1.reshape(E, NCC, 128, NJ, 128).transpose(0, 3, 2, 1, 4)
            .reshape(E, NJ, 128, 1024)),
        "wgr": np.ascontiguousarray(
            wg.reshape(E, NCC, 128, NJ, 128).transpose(0, 3, 2, 1, 4)
            .reshape(E, NJ, 128, 1024)),
        "w2": w2,
        "b1c": np.ascontiguousarray(b1.reshape(E, NJ, 128).transpose(0, 2, 1)),
        "bgc": np.ascontiguousarray(bg.reshape(E, NJ, 128).transpose(0, 2, 1)),
        "b2c": np.ascontiguousarray(b2.reshape(E, NCC, 128).transpose(0, 2, 1)),
        "ident_bf": np.eye(128, dtype=ml_dtypes.bfloat16),
        "ident_f32": np.eye(128, dtype=f32),
        "masks": _mk_masks(),
        "iota8_b": np.ascontiguousarray(
            np.broadcast_to(np.arange(E, dtype=f32), (128, E))),
    }
    return x, com


def kernel(**inputs):
    x, com = _prep_inputs(inputs)
    nc = build_nc()
    in_maps = []
    for i in range(8):
        m = dict(com)
        m["x"] = np.ascontiguousarray(x[i])
        in_maps.append(m)
    res = run_bass_kernel_spmd(nc, in_maps, core_ids=list(range(8)))
    out = np.stack([res.results[i]["out"] for i in range(8)], axis=0)
    return out.astype(np.float32)
